# version 01
# baseline (speedup 1.0000x reference)
import math
from contextlib import ExitStack

import numpy as np
import ml_dtypes
import concourse.bass as bass
import concourse.mybir as mybir
from concourse.bass_utils import run_bass_kernel_spmd

F32 = mybir.dt.float32
BF16 = mybir.dt.bfloat16
AF = mybir.ActivationFunctionType
ALU = mybir.AluOpType
AX = mybir.AxisListType

B, S, D = 2, 8192, 1024
NCORE = 8
CH = 1024
T = 2 * CH
NEG = -1.0e9
C_SHIFT = 10.0
EPS_LN = 1e-5
EPS_RMS = 1e-5

BF = ml_dtypes.bfloat16


def core_chunks(i):
    b, j = i // 4, i % 4
    return b, j, 7 - j



def fm(x):
    return np.ascontiguousarray(np.asarray(x).T)


def col_tiles(v):
    v = np.asarray(v, np.float32)
    G = v.shape[0] // 128
    return np.ascontiguousarray(v.reshape(G, 128).T)


def causal_mask_tiles():
    i = np.arange(128)[:, None]
    j = np.arange(512)[None, :]
    out = np.zeros((4, 128, 512), np.float32)
    for di, d in enumerate((0, 128, 256, 384)):
        out[di] = (j >= i + d).astype(np.float32)
    return out


def window_old_mask():
    i = np.arange(128)[:, None]
    j = np.arange(128)[None, :]
    return (j < i).astype(np.float32)


def rope_tables(positions):
    HH = 32
    inv_freq = 1.0 / (10000.0 ** (np.arange(0, HH, 2, dtype=np.float64) / HH))
    t = np.asarray(positions, np.float64)
    ang = t[None, :] * inv_freq[:, None]
    cos = np.cos(ang)
    sin = np.sin(ang)
    cos32 = np.concatenate([cos, cos], 0)
    sin32 = np.concatenate([-sin, sin], 0)
    return cos32.astype(np.float32), sin32.astype(np.float32)


def perm32():
    P = np.zeros((32, 32), np.float32)
    for d in range(16):
        P[d, d + 16] = 1.0
        P[d + 16, d] = 1.0
    return P



class Buf:
    __slots__ = ("name", "w", "r")

    def __init__(self, name):
        self.name = name
        self.w = []
        self.r = []


class Prog:
    ENGS = ("pe", "act", "dve", "pool", "sp")
    DMAQ = {"sp": "sp_d", "act": "act_d", "pool": "gp_d"}

    def __init__(self, nc, ctx):
        self.nc = nc
        self.ctx = ctx
        self.eng = {"pe": nc.tensor, "act": nc.scalar, "dve": nc.vector,
                    "pool": nc.gpsimd, "sp": nc.sync}
        self.sems = {}
        self.cnt = {}
        for name in ("pe", "act", "dve", "pool", "sp", "sp_d", "act_d", "gp_d"):
            self.sems[name] = ctx.enter_context(nc.semaphore("S_" + name))
            self.cnt[name] = 0
        self.waited = {}
        self._tiles = {}

    def sb(self, name, shape, dtype):
        t = self.ctx.enter_context(self.nc.sbuf_tensor(name, list(shape), dtype))
        b = Buf(name)
        self._tiles[name] = (t, b)
        return t, b

    def ps(self, name, shape, dtype=F32):
        t = self.ctx.enter_context(self.nc.psum_tensor(name, list(shape), dtype))
        b = Buf(name)
        self._tiles[name] = (t, b)
        return t, b

    def _wait(self, q, tokens):
        for (sname, cnt) in tokens:
            key = (q, sname)
            if self.waited.get(key, 0) < cnt:
                self.eng[q].wait_ge(self.sems[sname], cnt)
                self.waited[key] = cnt

    def _tok(self, sname, inst, step):
        inst.then_inc(self.sems[sname], step)
        self.cnt[sname] += step
        return (sname, self.cnt[sname])

    def op(self, q, outs, ins, emit):
        for b in ins:
            self._wait(q, b.w)
        for b in outs:
            self._wait(q, b.w)
            self._wait(q, b.r)
        inst = emit(self.eng[q])
        tok = self._tok(q, inst, 1)
        for b in ins:
            b.r.append(tok)
        for b in outs:
            b.w = [tok]
            b.r = []
        return tok

    def dma(self, q, outs, ins, emit):
        for b in ins:
            self._wait(q, b.w)
        for b in outs:
            self._wait(q, b.w)
            self._wait(q, b.r)
        inst = emit(self.eng[q])
        tok = self._tok(self.DMAQ[q], inst, 16)
        for b in ins:
            b.r.append(tok)
        for b in outs:
            b.w = [tok]
            b.r = []
        return tok

    def pe_group(self, outs, ins, emit):
        for b in ins:
            self._wait("pe", b.w)
        for b in outs:
            self._wait("pe", b.w)
            self._wait("pe", b.r)
        last = emit(self.nc.tensor)
        tok = self._tok("pe", last, 1)
        for b in ins:
            b.r.append(tok)
        for b in outs:
            b.w = [tok]
            b.r = []
        return tok

    def barrier_token(self):
        return [(n, c) for n, c in self.cnt.items() if c > 0]

    def finish(self, out_dma_tokens):
        for (sname, cnt) in out_dma_tokens:
            self._wait("sp", [(sname, cnt)])



def mm_kchain(p, psum_ap, lhsT_aps, rhs_aps, in_bufs, out_buf, tile_position=None):
    n = len(lhsT_aps)
    assert n == len(rhs_aps) and n >= 1

    def emit(te):
        last = None
        for i in range(n):
            last = te.matmul(psum_ap, lhsT_aps[i], rhs_aps[i],
                             start=(i == 0), stop=(i == n - 1),
                             tile_position=tile_position)
        return last
    return p.pe_group([out_buf], in_bufs, emit)



def build_l0a_qkv():
    nc = bass.Bass("TRN2", target_bir_lowering=False)
    xf = nc.dram_tensor("xf", [D, T], F32, kind="ExternalInput").ap()
    wqkv = nc.dram_tensor("wqkv", [D, 3 * D], F32, kind="ExternalInput").ap()
    bqkv = nc.dram_tensor("bqkv", [128, 16], F32, kind="ExternalInput").ap()
    bv = nc.dram_tensor("bv", [1, D], F32, kind="ExternalInput").ap()
    q_out = nc.dram_tensor("q_out", [D, T], BF16, kind="ExternalOutput").ap()
    k_out = nc.dram_tensor("k_out", [D, T], BF16, kind="ExternalOutput").ap()
    v_out = nc.dram_tensor("v_out", [T, D], BF16, kind="ExternalOutput").ap()

    with ExitStack() as ctx:
        p = Prog(nc, ctx)
        xb, b_xb = p.sb("xb", [128, 8 * T], BF16)
        wsb, b_wsb = p.sb("wsb", [128, 8 * 3 * D], BF16)
        bq, b_bq = p.sb("bq", [128, 16], F32)
        bvr, b_bvr = p.sb("bvr", [1, D], F32)
        bvb, b_bvb = p.sb("bvb", [1, D], BF16)
        ones_r, b_ones = p.sb("ones_r", [1, 128], BF16)
        ev, b_ev = p.sb("ev", [128, 4 * 512], BF16)
        ev_bufs = [Buf(f"ev{i}") for i in range(4)]
        psA, b_psA = p.ps("psA", [128, 512])
        psB, b_psB = p.ps("psB", [128, 512])
        ps_bufs = [b_psA, b_psB]
        ps_t = [psA, psB]

        p.dma("pool", [b_xb], [], lambda e: e.dma_start(
            xb[:].rearrange("p (c t) -> p c t", c=8),
            xf.rearrange("(c p) t -> p c t", p=128)))
        p.dma("pool", [b_wsb], [], lambda e: e.dma_start(
            wsb[:].rearrange("p (c n) -> p c n", c=8),
            wqkv.rearrange("(c p) n -> p c n", p=128)))
        p.dma("sp", [b_bq], [], lambda e: e.dma_start(bq[:, :], bqkv))
        p.dma("sp", [b_bvr], [], lambda e: e.dma_start(bvr[:, :], bv))
        p.op("dve", [b_bvb], [b_bvr], lambda e: e.tensor_copy(bvb[:, :], bvr[:, :]))
        p.op("dve", [b_ones], [], lambda e: e.memset(ones_r[:, :], 1.0))

        out_toks = []
        for sec, (base_col, bias_base, dst) in enumerate(
                ((0, 0, q_out), (D, 8, k_out))):
            for m in range(8):
                for ntile in range(4):
                    pb = ps_bufs[(m * 4 + ntile) % 2]
                    pt = ps_t[(m * 4 + ntile) % 2]
                    mm_kchain(
                        p, pt[:, :],
                        [wsb[:, k * 3 * D + base_col + m * 128:
                             k * 3 * D + base_col + m * 128 + 128] for k in range(8)],
                        [xb[:, k * T + ntile * 512: k * T + ntile * 512 + 512]
                         for k in range(8)],
                        [b_wsb, b_xb], pb)
                    eb = ev_bufs[(m * 4 + ntile) % 4]
                    esl = ev[:, ((m * 4 + ntile) % 4) * 512:((m * 4 + ntile) % 4) * 512 + 512]
                    p.op("dve", [eb], [pb, b_bq], lambda e, _pt=pt, _esl=esl, _m=m, _bb=bias_base: e.tensor_scalar(
                        _esl, _pt[:, :], bq[:, _bb + _m:_bb + _m + 1], None, ALU.add))
                    out_toks.append(p.dma("sp", [], [eb], lambda e, _esl=esl, _dst=dst, _m=m, _n=ntile: e.dma_start(
                        _dst[_m * 128:(_m + 1) * 128, _n * 512:(_n + 1) * 512], _esl)))

        for tt in range(16):
            for nv in range(2):
                idx = tt * 2 + nv
                pb = ps_bufs[idx % 2]
                pt = ps_t[idx % 2]

                def emit(te, _tt=tt, _nv=nv, _pt=pt):
                    for k in range(8):
                        te.matmul(_pt[:, :],
                                  xb[:, k * T + _tt * 128: k * T + _tt * 128 + 128],
                                  wsb[:, k * 3 * D + 2 * D + _nv * 512:
                                      k * 3 * D + 2 * D + _nv * 512 + 512],
                                  start=(k == 0), stop=False)
                    return te.matmul(_pt[:, :], ones_r[:, :],
                                     bvb[:, _nv * 512:_nv * 512 + 512],
                                     start=False, stop=True)
                p.pe_group([pb], [b_xb, b_wsb, b_ones, b_bvb], emit)
                eb = ev_bufs[idx % 4]
                esl = ev[:, (idx % 4) * 512:(idx % 4) * 512 + 512]
                p.op("act", [eb], [pb], lambda e, _pt=pt, _esl=esl: e.activation(
                    _esl, _pt[:, :], AF.Copy, scale=1.0))
                out_toks.append(p.dma("sp", [], [eb], lambda e, _esl=esl, _tt=tt, _nv=nv: e.dma_start(
                    v_out[_tt * 128:(_tt + 1) * 128, _nv * 512:(_nv + 1) * 512], _esl)))

        p.finish(out_toks)
    return nc


def host_l0a_qkv_inputs(inputs, core):
    b, lo, hi = core_chunks(core)
    x = np.asarray(inputs["L0_attn_x"][b])
    xc = np.concatenate([x[lo * CH:(lo + 1) * CH], x[hi * CH:(hi + 1) * CH]], 0)
    bqkv = np.asarray(inputs["L0_attn_b_qkv"], np.float32)
    return {
        "xf": fm(xc),
        "wqkv": np.ascontiguousarray(np.asarray(inputs["L0_attn_W_qkv"], np.float32)),
        "bqkv": np.concatenate([col_tiles(bqkv[0:D]), col_tiles(bqkv[D:2 * D])], 1),
        "bv": np.ascontiguousarray(bqkv[2 * D:3 * D].reshape(1, D)),
    }



ATTN_EXTENTS = [1, 2, 3, 4, 1, 2, 3, 4] + [4] * 12
N_SLOTS = len(ATTN_EXTENTS)


def attn_slot_map(j):
    spanA, spanB = (0, 3) if j < 2 else (1, 2)
    slots = []
    for e in range(1, 5):
        slots.append((spanA, spanA * 4 + e - 1))
    for e in range(1, 5):
        slots.append((spanB, spanB * 4 + e - 1))
    fulls = [(spanA, qt) for qt in range(spanA * 4 + 4, 16)]
    fulls += [(spanB, qt) for qt in range(spanB * 4 + 4, 16)]
    assert len(fulls) == 12
    slots += fulls
    return slots


def build_attn_partial():
    nc = bass.Bass("TRN2", target_bir_lowering=False)
    dr = lambda n, s, d, k: nc.dram_tensor(n, s, d, kind=k).ap()
    qp = dr("qp", [512, N_SLOTS * 512], BF16, "ExternalInput")
    kk_in = dr("kk_in", [512, N_SLOTS * 2048], BF16, "ExternalInput")
    v_in = dr("v_in", [N_SLOTS * 2048, 512], BF16, "ExternalInput")
    masks4 = dr("masks4", [128, 4 * 512], F32, "ExternalInput")
    o_part = dr("o_part", [512, N_SLOTS * 512], BF16, "ExternalOutput")
    sums_part = dr("sums_part", [8, N_SLOTS * 512], F32, "ExternalOutput")

    out_toks = []
    with ExitStack() as ctx:
        p = Prog(nc, ctx)
        q_sb, b_q = p.sb("q_sb", [128, 4 * 512], BF16)
        kk_st, b_k0 = p.sb("kk_st", [128, 2 * 2048], BF16)
        v_st, b_v0 = p.sb("v_st", [128, 2 * 2048], BF16)
        b_kst = [b_k0, Buf("kst1")]
        b_vst = [b_v0, Buf("vst1")]
        e_sb, b_e0 = p.sb("e_sb", [128, 4 * 512], BF16)
        b_e = [b_e0, Buf("e1"), Buf("e2"), Buf("e3")]
        masks_sb, b_masks = p.sb("masks_sb", [128, 4 * 512], BF16)
        ones_col, b_ones = p.sb("ones_col", [128, 1], BF16)
        oev, b_oev = p.sb("oev", [128, 4 * 512], BF16)
        suev, b_suev = p.sb("suev", [128, 2 * 512], F32)
        suev_t = suev
        ps, psb = {}, {}
        for n in ("pv0", "pv1", "pv2", "pv3", "s0", "s1", "u0", "u1"):
            ps[n], psb[n] = p.ps("ps_" + n, [128, 512])

        csh, b_csh = p.sb("csh", [128, 1], F32)
        p.dma("pool", [b_masks], [], lambda e: e.dma_start(masks_sb[:, :], masks4))
        p.op("dve", [b_ones], [], lambda e: e.memset(ones_col[:, :], 1.0))
        p.op("dve", [b_csh], [], lambda e: e.memset(csh[:, :], -C_SHIFT))

        kbctr = 0
        for u in range(N_SLOTS):
            ext = ATTN_EXTENTS[u]
            ragged = u < 8
            p.dma("sp", [b_q], [], lambda e, _u=u: e.dma_start(
                q_sb[:].rearrange("p (c t) -> p c t", c=4),
                qp[:, _u * 512:(_u + 1) * 512].rearrange("(c p) t -> p c t", p=128)))
            for kb in range(ext):
                par = kbctr % 2
                kbctr += 1
                fst = kb == 0
                lst = kb == ext - 1
                diag = ragged and lst
                p.dma("sp", [b_kst[par]], [], lambda e, _u=u, _k=kb, _par=par: e.dma_start(
                    kk_st[:, _par * 2048:(_par + 1) * 2048].rearrange("p (c k) -> p c k", c=4),
                    kk_in[:, _u * 2048 + _k * 512:_u * 2048 + (_k + 1) * 512].rearrange(
                        "(c p) k -> p c k", p=128)))
                p.dma("sp", [b_vst[par]], [], lambda e, _u=u, _k=kb, _par=par: e.dma_start(
                    v_st[:, _par * 2048:(_par + 1) * 2048].rearrange("p (c d) -> p c d", c=4),
                    v_in[_u * 2048 + _k * 512:_u * 2048 + (_k + 1) * 512, :].rearrange(
                        "(c p) d -> p c d", p=128)))
                for pi in range(4):
                    slot = (pi % 2) * 2
                    for kc in range(4):
                        kk_off = par * 2048 + pi * 512 + kc * 128
                        for hh in range(2):
                            sp_, spb_ = (ps["s0"], psb["s0"]) if hh == 0 else (ps["s1"], psb["s1"])
                            p.pe_group([spb_], [b_kst[par], b_q],
                                       lambda te, _s=sp_, _o=kk_off, _h=hh, _pi=pi: te.matmul(
                                           _s[:, :], kk_st[_h * 64:(_h + 1) * 64, _o:_o + 128],
                                           q_sb[_h * 64:(_h + 1) * 64, _pi * 512:(_pi + 1) * 512],
                                           start=True, stop=True, tile_position=(_h * 64, 0)))
                            dst = e_sb[:, (slot + hh) * 512:(slot + hh + 1) * 512]
                            p.op("act", [b_e[slot + hh]], [spb_, b_csh],
                                 lambda e, _s=sp_, _d=dst: e.activation(
                                     _d, _s[:, :], AF.Exp, scale=0.125, bias=csh[:, 0:1]))
                            if diag:
                                p.op("dve", [b_e[slot + hh]], [b_e[slot + hh], b_masks],
                                     lambda e, _d=dst, _m=kc: e.tensor_tensor(
                                         _d, _d, masks_sb[:, _m * 512:(_m + 1) * 512], ALU.mult))
                        pvt, pvb = ps[f"pv{pi}"], psb[f"pv{pi}"]
                        vc_off = par * 2048 + kc * 512
                        ut, ub = (ps["u0"], psb["u0"]) if pi < 2 else (ps["u1"], psb["u1"])
                        for hh in range(2):
                            dv = (pi * 2 + hh) * 64
                            p.pe_group([pvb], [b_vst[par], b_e[slot + hh]],
                                       lambda te, _dv=dv, _e=slot + hh, _f=fst and kc == 0,
                                       _l=lst and kc == 3, _hh=hh, _vo=vc_off, _pv=pvt: te.matmul(
                                           _pv[_hh * 64:(_hh + 1) * 64, :],
                                           v_st[:, _vo + _dv:_vo + _dv + 64],
                                           e_sb[:, _e * 512:(_e + 1) * 512],
                                           start=_f, stop=_l, tile_position=(0, _hh * 64)))
                            strip = ((pi % 2) * 2 + hh) * 32
                            p.pe_group([ub], [b_e[slot + hh], b_ones],
                                       lambda te, _st=strip, _e=slot + hh, _f=fst and kc == 0,
                                       _l=lst and kc == 3, _u=ut: te.matmul(
                                           _u[_st:_st + 1, :], ones_col[:, :],
                                           e_sb[:, _e * 512:(_e + 1) * 512],
                                           start=_f, stop=_l, tile_position=(0, _st)))
            for pi in range(4):
                p.op("act", [b_oev], [psb[f"pv{pi}"]],
                     lambda e, _p=ps[f"pv{pi}"], _pi=pi: e.activation(
                         oev[:, _pi * 512:(_pi + 1) * 512], _p[:, :], AF.Copy, scale=1.0))
            out_toks.append(p.dma("sp", [], [b_oev], lambda e, _u=u: e.dma_start(
                o_part[:, _u * 512:(_u + 1) * 512].rearrange("(c p) t -> p c t", p=128),
                oev[:].rearrange("p (c t) -> p c t", c=4))))
            for ui in range(2):
                p.op("dve", [b_suev], [psb["u0" if ui == 0 else "u1"]],
                     lambda e, _u=ps["u0" if ui == 0 else "u1"], _ui=ui: e.tensor_copy(
                         suev[:, _ui * 512:(_ui + 1) * 512], _u[:, :]))
            for ui in range(2):
                out_toks.append(p.dma("sp", [], [b_suev], lambda e, _u=u, _ui=ui: e.dma_start(
                    sums_part[_ui * 4:(_ui + 1) * 4, _u * 512:(_u + 1) * 512],
                    bass.AP(suev_t, _ui * 512, [[32 * 1024, 4], [1, 512]]))))
        p.finish(out_toks)
    return nc



def emit_postln_chunk(p, *, t_ap, t_buf, col, scr_bf, scr_bf_buf,
                      rows, rows_buf, consts, consts_buf, ones_f32, ones_f32_buf,
                      ones_bf, ones_bf_buf, psS, psSb, psQ, psQb, psA, psAb,
                      psB, psBb, out_f32_ap, stride=None):
    TW = stride if stride is not None else T
    sl = [Buf("lnsl%d" % i) for i in range(4)]
    for g in range(8):
        a, b2 = (g % 2) * 2, (g % 2) * 2 + 1
        p.op("dve", [sl[a]], [t_buf], lambda e, _g=g, _a=a: e.tensor_copy(
            scr_bf[:, _a * 512:(_a + 1) * 512], t_ap[:, _g * TW + col:_g * TW + col + 512]))
        p.op("dve", [sl[b2]], [sl[a]], lambda e, _a=a, _b=b2: e.tensor_tensor(
            scr_bf[:, _b * 512:(_b + 1) * 512],
            scr_bf[:, _a * 512:(_a + 1) * 512],
            scr_bf[:, _a * 512:(_a + 1) * 512], ALU.mult))
        p.pe_group([psSb], [sl[a], ones_bf_buf], lambda te, _g=g, _a=a: te.matmul(
            psS[0:1, :], ones_bf[:, :], scr_bf[:, _a * 512:(_a + 1) * 512],
            start=(_g == 0), stop=(_g == 7)))
        p.pe_group([psQb], [sl[b2], ones_bf_buf], lambda te, _g=g, _b=b2: te.matmul(
            psQ[0:1, :], ones_bf[:, :], scr_bf[:, _b * 512:(_b + 1) * 512],
            start=(_g == 0), stop=(_g == 7)))
    p.op("dve", [rows_buf], [psSb, consts_buf], lambda e: e.tensor_scalar(
        rows[:, 0:512], psS[0:1, :], consts[0:1, 16:17], None, ALU.mult))
    p.op("dve", [rows_buf], [psQb, consts_buf], lambda e: e.tensor_scalar(
        rows[:, 512:1024], psQ[0:1, :], consts[0:1, 16:17], None, ALU.mult))
    p.op("dve", [rows_buf], [rows_buf], lambda e: e.tensor_tensor(
        rows[:, 1024:1536], rows[:, 0:512], rows[:, 0:512], ALU.mult))
    p.op("dve", [rows_buf], [rows_buf], lambda e: e.tensor_tensor(
        rows[:, 1536:2048], rows[:, 512:1024], rows[:, 1024:1536], ALU.subtract))
    p.op("act", [rows_buf], [rows_buf, consts_buf], lambda e: e.activation(
        rows[:, 1024:1536], rows[:, 1536:2048], AF.Ln, bias=consts[0:1, 17:18]))
    p.op("act", [rows_buf], [rows_buf], lambda e: e.activation(
        rows[:, 512:1024], rows[:, 1024:1536], AF.Exp, scale=-0.5))
    p.op("dve", [rows_buf], [rows_buf], lambda e: e.tensor_tensor(
        rows[:, 1024:1536], rows[:, 0:512], rows[:, 512:1024], ALU.mult))
    p.pe_group([psAb], [rows_buf, ones_f32_buf], lambda te: te.matmul(
        psA[:, :], ones_f32[:, :], rows[:, 512:1024], start=True, stop=True))
    p.pe_group([psBb], [rows_buf, ones_f32_buf], lambda te: te.matmul(
        psB[:, :], ones_f32[:, :], rows[:, 1024:1536], start=True, stop=True))
    for g in range(8):
        dst, dbuf = out_f32_ap(g)
        p.op("dve", [dbuf], [t_buf, psAb], lambda e, _g=g, _d=dst: e.tensor_tensor(
            _d, t_ap[:, _g * TW + col:_g * TW + col + 512], psA[:, :], ALU.mult))
        p.op("dve", [dbuf], [dbuf, psBb], lambda e, _g=g, _d=dst: e.tensor_tensor(
            _d, _d, psB[:, :], ALU.subtract))


def emit_rms_chunk(p, *, src_ap, src_buf, col_src, scr_bf, scr_bf_buf,
                   rows, rows_buf, consts, consts_buf, ones_f32, ones_f32_buf,
                   ones_bf, ones_bf_buf, psQ, psQb, psA, psAb,
                   w_col, w_col_buf, xn_ap, xn_buf, col_dst, scr_f32, scr_f32_buf,
                   src_stride=512, dst_stride=None, w_off=0):
    dst_st = dst_stride if dst_stride is not None else T
    sl = [Buf("rmsl%d" % i) for i in range(4)]
    for g in range(8):
        a, b2 = (g % 2) * 2, (g % 2) * 2 + 1
        p.op("dve", [sl[a]], [src_buf], lambda e, _g=g, _a=a: e.tensor_copy(
            scr_bf[:, _a * 512:(_a + 1) * 512],
            src_ap[:, _g * src_stride + col_src:_g * src_stride + col_src + 512]))
        p.op("dve", [sl[b2]], [sl[a]], lambda e, _a=a, _b=b2: e.tensor_tensor(
            scr_bf[:, _b * 512:(_b + 1) * 512],
            scr_bf[:, _a * 512:(_a + 1) * 512],
            scr_bf[:, _a * 512:(_a + 1) * 512], ALU.mult))
        p.pe_group([psQb], [sl[b2], ones_bf_buf], lambda te, _g=g, _b=b2: te.matmul(
            psQ[0:1, :], ones_bf[:, :], scr_bf[:, _b * 512:(_b + 1) * 512],
            start=(_g == 0), stop=(_g == 7)))
    p.op("dve", [rows_buf], [psQb, consts_buf], lambda e: e.tensor_scalar(
        rows[:, 0:512], psQ[0:1, :], consts[0:1, 16:17], None, ALU.mult))
    p.op("act", [rows_buf], [rows_buf], lambda e: e.activation(
        rows[:, 512:1024], rows[:, 0:512], AF.Ln))
    p.op("act", [rows_buf], [rows_buf], lambda e: e.activation(
        rows[:, 0:512], rows[:, 512:1024], AF.Exp, scale=0.5))
    p.op("dve", [rows_buf], [rows_buf, consts_buf], lambda e: e.tensor_scalar(
        rows[:, 512:1024], rows[:, 0:512], consts[0:1, 17:18], None, ALU.add))
    p.op("dve", [rows_buf], [rows_buf], lambda e: e.reciprocal(
        rows[:, 0:512], rows[:, 512:1024]))
    p.pe_group([psAb], [rows_buf, ones_f32_buf], lambda te: te.matmul(
        psA[:, :], ones_f32[:, :], rows[:, 0:512], start=True, stop=True))
    for g in range(8):
        p.op("dve", [scr_f32_buf], [src_buf, psAb], lambda e, _g=g: e.tensor_tensor(
            scr_f32[:, _g * 512:(_g + 1) * 512],
            src_ap[:, _g * src_stride + col_src:_g * src_stride + col_src + 512],
            psA[:, :], ALU.mult))
        p.op("dve", [xn_buf], [scr_f32_buf, w_col_buf], lambda e, _g=g: e.tensor_scalar(
            xn_ap[:, _g * dst_st + col_dst:_g * dst_st + col_dst + 512],
            scr_f32[:, _g * 512:(_g + 1) * 512], w_col[:, w_off + _g:w_off + _g + 1],
            None, ALU.mult))



MLA_OFF = dict(wkvd=0, wqd=2048, wqq=4096, wku=6144, wrk=7168, wvu=11264)


def build_l2b():
    nc = bass.Bass("TRN2", target_bir_lowering=False)
    dr = lambda n, s, d, k: nc.dram_tensor(n, s, d, kind=k).ap()
    o_parts = dr("o_parts", [4 * 8 * 512, 512], BF16, "ExternalInput")
    sums_in = dr("sums_in", [4 * 16 * 4, 512], F32, "ExternalInput")
    xf = dr("xf", [D, T], F32, "ExternalInput")
    wproj = dr("wproj", [D, D], F32, "ExternalInput")
    sel8 = dr("sel8", [16, 8 * 128], F32, "ExternalInput")
    bcols_in = dr("bcols_in", [128, 16], F32, "ExternalInput")
    wkvd = dr("wkvd", [D, 256], F32, "ExternalInput")
    wqd = dr("wqd", [D, 256], F32, "ExternalInput")
    wqq = dr("wqq", [256, D], F32, "ExternalInput")
    wku = dr("wku", [256, 512], F32, "ExternalInput")
    wrk = dr("wrk", [D, 512], F32, "ExternalInput")
    wvu = dr("wvu", [256, D], F32, "ExternalInput")
    ropeCq = dr("ropeCq", [128, T], F32, "ExternalInput")
    ropeSq = dr("ropeSq", [128, T], F32, "ExternalInput")
    ropeCk = dr("ropeCk", [128, T], F32, "ExternalInput")
    ropeSk = dr("ropeSk", [128, T], F32, "ExternalInput")
    permq = dr("permq", [128, 128], F32, "ExternalInput")
    permk = dr("permk", [128, 128], F32, "ExternalInput")
    r1_out = dr("r1_out", [D, T], F32, "ExternalOutput")
    qq_out = dr("qq_out", [D, T], BF16, "ExternalOutput")
    k_out2 = dr("k_out2", [512, T], BF16, "ExternalOutput")
    kr_out = dr("kr_out", [512, T], BF16, "ExternalOutput")
    vm_out = dr("vm_out", [T, D], BF16, "ExternalOutput")

    out_toks = []
    with ExitStack() as ctx:
        p = Prog(nc, ctx)
        oc_sb, b_oc = p.sb("oc_sb", [128, 16 * 512], BF16)
        o_full, b_of = p.sb("o_full", [128, 8 * 512], BF16)
        onorm, b_on = p.sb("onorm", [128, 8 * 512], BF16)
        oscr, b_oscr = p.sb("oscr", [128, 2 * 512], BF16)
        s_sb, b_s = p.sb("s_sb", [16, 4 * 512], F32)
        srow, b_srow = p.sb("srow", [16, 4 * 512], F32)
        sbf, b_sbf = p.sb("sbf", [16, 512], BF16)
        sel_sb, b_sel = p.sb("sel_sb", [16, 8 * 128], BF16)
        rt_sb, b_rt = p.sb("rt_sb", [128, 512], BF16)
        wbig, b_wbig = p.sb("wbig", [128, 13312], BF16)
        bcols, b_bcols = p.sb("bcols", [128, 18], F32)
        x_st, b_xst = p.sb("x_st", [128, 8 * 512], F32)
        t_ar, b_t = p.sb("t_ar", [128, 8 * 512], F32)
        scr_bf, b_scrbf = p.sb("scr_bf", [128, 4 * 512], BF16)
        rows, b_rows = p.sb("rows", [1, 4 * 512], F32)
        r1_ar, b_r1 = p.sb("r1_ar", [128, 8 * 512], F32)
        xn_sb, b_xn = p.sb("xn_sb", [128, 8 * T], BF16)
        lat_sb, b_lat = p.sb("lat_sb", [128, 2 * T], BF16)
        qlat_sb, b_qlat = p.sb("qlat_sb", [128, 2 * T], BF16)
        rope_sb, b_rope = p.sb("rope_sb", [128, 2 * 512], F32)
        perm_sb, b_perm = p.sb("perm_sb", [128, 2 * 128], BF16)
        rsc, b_rsc = p.sb("rsc", [128, 3 * 512], F32)
        rbf, b_rbf = p.sb("rbf", [128, 512], BF16)
        ev, b_ev0 = p.sb("ev", [128, 4 * 512], BF16)
        b_ev = [b_ev0] + [Buf(f"ev{i}") for i in range(1, 4)]
        ones_col, b_ones = p.sb("ones_col", [128, 1], BF16)
        ones_f32, b_ones32 = p.sb("ones_f32", [1, 128], F32)

        ps, psb = {}, {}
        for n in ("pv0", "pv1", "pv2", "pv3", "s0", "s1", "u0", "u1"):
            ps[n], psb[n] = p.ps("ps_" + n, [128, 512])

        p.dma("pool", [b_sel], [], lambda e: e.dma_start(sel_sb[:, :], sel8))
        p.dma("pool", [b_perm], [], lambda e: e.dma_start(perm_sb[:, 0:128], permq))
        p.dma("pool", [b_perm], [b_perm], lambda e: e.dma_start(perm_sb[:, 128:256], permk))
        p.dma("sp", [b_bcols], [], lambda e: e.dma_start(bcols[:, 0:16], bcols_in))
        p.op("dve", [b_bcols], [b_bcols], lambda e: e.memset(bcols[:, 16:17], 1.0 / 1024.0))
        p.op("dve", [b_bcols], [b_bcols], lambda e: e.memset(bcols[:, 17:18], 1e-5))
        p.op("dve", [b_ones], [], lambda e: e.memset(ones_col[:, :], 1.0))
        p.op("dve", [b_ones32], [], lambda e: e.memset(ones_f32[:, :], 1.0))
        p.dma("pool", [b_wbig], [], lambda e: e.dma_start(
            wbig[:, 0:8192].rearrange("p (k m) -> p k m", k=8),
            wproj.rearrange("(k p) m -> p k m", p=128)))

        for qt in range(4):
            p.dma("sp", [b_xst], [], lambda e, _q=qt: e.dma_start(
                x_st[:].rearrange("p (g t) -> p g t", g=8),
                xf[:, _q * 512:(_q + 1) * 512].rearrange("(g p) t -> p g t", p=128)))
            p.dma("sp", [b_s], [], lambda e, _q=qt: e.dma_start(
                s_sb[:].rearrange("h (s t) -> h s t", s=4),
                sums_in[_q * 64:(_q + 1) * 64, :].rearrange("(h s) t -> h s t", s=4)))
            p.op("dve", [b_srow], [b_s], lambda e: e.tensor_tensor(
                srow[:, 0:512], s_sb[:, 0:512], s_sb[:, 512:1024], ALU.add))
            p.op("dve", [b_srow], [b_s, b_srow], lambda e: e.tensor_tensor(
                srow[:, 512:1024], s_sb[:, 1024:1536], s_sb[:, 1536:2048], ALU.add))
            p.op("dve", [b_srow], [b_srow], lambda e: e.tensor_tensor(
                srow[:, 1024:1536], srow[:, 0:512], srow[:, 512:1024], ALU.add))
            p.op("dve", [b_srow], [b_srow], lambda e: e.reciprocal(
                srow[:, 1536:2048], srow[:, 1024:1536]))
            p.op("dve", [b_sbf], [b_srow], lambda e: e.tensor_copy(
                sbf[:, :], srow[:, 1536:2048]))
            for hh in range(2):
                p.dma("sp", [b_oc], [], lambda e, _q=qt, _h=hh: e.dma_start(
                    oc_sb[:].rearrange("p (s c t) -> p s c t", s=4, c=4),
                    o_parts[_q * 4096 + _h * 2048:_q * 4096 + (_h + 1) * 2048, :].rearrange(
                        "(s c p) t -> p s c t", p=128, c=4)))
                for pi in range(4):
                    pg = hh * 4 + pi
                    p.op("dve", [b_oscr], [b_oc], lambda e, _pi=pi: e.tensor_tensor(
                        oscr[:, 0:512], oc_sb[:, (0 * 4 + _pi) * 512:(0 * 4 + _pi + 1) * 512],
                        oc_sb[:, (1 * 4 + _pi) * 512:(1 * 4 + _pi + 1) * 512], ALU.add))
                    p.op("dve", [b_oscr], [b_oc, b_oscr], lambda e, _pi=pi: e.tensor_tensor(
                        oscr[:, 512:1024], oc_sb[:, (2 * 4 + _pi) * 512:(2 * 4 + _pi + 1) * 512],
                        oc_sb[:, (3 * 4 + _pi) * 512:(3 * 4 + _pi + 1) * 512], ALU.add))
                    p.op("dve", [b_of], [b_oscr], lambda e, _pg=pg: e.tensor_tensor(
                        o_full[:, _pg * 512:(_pg + 1) * 512],
                        oscr[:, 0:512], oscr[:, 512:1024], ALU.add))
            for pg in range(8):
                p.pe_group([psb["s0"]], [b_sbf, b_sel], lambda te, _pg=pg: te.matmul(
                    ps["s0"][:, :], sel_sb[:, _pg * 128:(_pg + 1) * 128],
                    sbf[:, :], start=True, stop=True))
                p.op("act", [b_rt], [psb["s0"]], lambda e: e.activation(
                    rt_sb[:, :], ps["s0"][:, :], AF.Copy, scale=1.0))
                p.op("dve", [b_on], [b_of, b_rt], lambda e, _pg=pg: e.tensor_tensor(
                    onorm[:, _pg * 512:(_pg + 1) * 512],
                    o_full[:, _pg * 512:(_pg + 1) * 512], rt_sb[:, :], ALU.mult))
            for fo in range(8):
                pt, pb_ = (ps["s1"], psb["s1"]) if fo % 2 == 0 else (ps["pv0"], psb["pv0"])

                def emitc(te, _fo=fo, _pt=pt):
                    last = None
                    for k in range(8):
                        last = te.matmul(_pt[:, :],
                                         wbig[:, k * 1024 + _fo * 128:k * 1024 + _fo * 128 + 128],
                                         onorm[:, k * 512:(k + 1) * 512],
                                         start=(k == 0), stop=(k == 7))
                    return last
                p.pe_group([pb_], [b_wbig, b_on], emitc)
                p.op("dve", [b_t], [pb_, b_bcols, b_xst], lambda e, _fo=fo, _pt=pt: e.scalar_tensor_tensor(
                    t_ar[:, _fo * 512:(_fo + 1) * 512], _pt[:, :], bcols[:, _fo:_fo + 1],
                    x_st[:, _fo * 512:(_fo + 1) * 512], ALU.add, ALU.add))
            emit_postln_chunk(
                p, t_ap=t_ar, t_buf=b_t, col=0, scr_bf=scr_bf, scr_bf_buf=b_scrbf,
                rows=rows, rows_buf=b_rows, consts=bcols, consts_buf=b_bcols,
                ones_f32=ones_f32, ones_f32_buf=b_ones32, ones_bf=ones_col,
                ones_bf_buf=b_ones, psS=ps["u0"], psSb=psb["u0"], psQ=ps["u1"],
                psQb=psb["u1"], psA=ps["pv2"], psAb=psb["pv2"], psB=ps["pv3"],
                psBb=psb["pv3"],
                out_f32_ap=lambda g: (r1_ar[:, g * 512:(g + 1) * 512], b_r1),
                stride=512)
            out_toks.append(p.dma("sp", [], [b_r1], lambda e, _q=qt: e.dma_start(
                r1_out[:, _q * 512:(_q + 1) * 512].rearrange("(g p) t -> p g t", p=128),
                r1_ar[:].rearrange("p (g t) -> p g t", g=8))))
            emit_rms_chunk(
                p, src_ap=r1_ar, src_buf=b_r1, col_src=0, scr_bf=scr_bf,
                scr_bf_buf=b_scrbf, rows=rows, rows_buf=b_rows, consts=bcols,
                consts_buf=b_bcols, ones_f32=ones_f32, ones_f32_buf=b_ones32,
                ones_bf=ones_col, ones_bf_buf=b_ones, psQ=ps["u1"], psQb=psb["u1"],
                psA=ps["pv2"], psAb=psb["pv2"], w_col=bcols, w_col_buf=b_bcols,
                xn_ap=xn_sb, xn_buf=b_xn, col_dst=qt * 512, scr_f32=t_ar,
                scr_f32_buf=b_t, src_stride=512, dst_stride=T, w_off=8)

        p.dma("pool", [b_wbig], [b_wbig], lambda e: e.dma_start(
            wbig[:, 0:2048].rearrange("p (k m) -> p k m", k=8),
            wkvd.rearrange("(k p) m -> p k m", p=128)))
        p.dma("pool", [b_wbig], [b_wbig], lambda e: e.dma_start(
            wbig[:, 2048:4096].rearrange("p (k m) -> p k m", k=8),
            wqd.rearrange("(k p) m -> p k m", p=128)))
        p.dma("pool", [b_wbig], [b_wbig], lambda e: e.dma_start(
            wbig[:, 4096:6144].rearrange("p (k m) -> p k m", k=2),
            wqq.rearrange("(k p) m -> p k m", p=128)))
        p.dma("pool", [b_wbig], [b_wbig], lambda e: e.dma_start(
            wbig[:, 6144:7168].rearrange("p (k m) -> p k m", k=2),
            wku.rearrange("(k p) m -> p k m", p=128)))
        p.dma("pool", [b_wbig], [b_wbig], lambda e: e.dma_start(
            wbig[:, 7168:11264].rearrange("p (k m) -> p k m", k=8),
            wrk.rearrange("(k p) m -> p k m", p=128)))
        p.dma("pool", [b_wbig], [b_wbig], lambda e: e.dma_start(
            wbig[:, 11264:13312].rearrange("p (k m) -> p k m", k=2),
            wvu.rearrange("(k p) m -> p k m", p=128)))

        def proj(dst_sb, dst_buf, woff, nk, kstride, src_sb, src_buf, m, nt, pt, pb_):
            def emitc(te):
                last = None
                for k in range(nk):
                    last = te.matmul(pt[:, :],
                                     wbig[:, woff + k * kstride + m * 128:
                                          woff + k * kstride + m * 128 + 128],
                                     src_sb[:, k * T + nt * 512:k * T + nt * 512 + 512],
                                     start=(k == 0), stop=(k == nk - 1))
                return last
            p.pe_group([pb_], [b_wbig, src_buf], emitc)
            p.op("act", [dst_buf], [pb_], lambda e: e.activation(
                dst_sb[:, m * T + nt * 512:m * T + nt * 512 + 512], pt[:, :], AF.Copy, scale=1.0))

        pscyc = [("s0", psb["s0"]), ("s1", psb["s1"])]
        ci = 0
        for m in range(2):
            for nt in range(4):
                pn, pb_ = pscyc[ci % 2]; ci += 1
                proj(lat_sb, b_lat, MLA_OFF["wkvd"], 8, 256, xn_sb, b_xn, m, nt, ps[pn], pb_)
        for m in range(2):
            for nt in range(4):
                pn, pb_ = pscyc[ci % 2]; ci += 1
                proj(qlat_sb, b_qlat, MLA_OFF["wqd"], 8, 256, xn_sb, b_xn, m, nt, ps[pn], pb_)
        for m in range(4):
            for nt in range(4):
                pn, pb_ = pscyc[ci % 2]; ci += 1
                evi = (m * 4 + nt) % 4

                def emitk(te, _m=m, _nt=nt, _pt=ps[pn]):
                    last = None
                    for k in range(2):
                        last = te.matmul(_pt[:, :],
                                         wbig[:, MLA_OFF["wku"] + k * 512 + _m * 128:
                                              MLA_OFF["wku"] + k * 512 + _m * 128 + 128],
                                         lat_sb[:, k * T + _nt * 512:k * T + _nt * 512 + 512],
                                         start=(k == 0), stop=(k == 1))
                    return last
                p.pe_group([pb_], [b_wbig, b_lat], emitk)
                p.op("act", [b_ev[evi]], [pb_], lambda e, _e=evi, _pt=ps[pn]: e.activation(
                    ev[:, _e * 512:(_e + 1) * 512], _pt[:, :], AF.Copy, scale=1.0))
                out_toks.append(p.dma("sp", [], [b_ev[evi]], lambda e, _m=m, _nt=nt, _e=evi: e.dma_start(
                    k_out2[_m * 128:(_m + 1) * 128, _nt * 512:(_nt + 1) * 512],
                    ev[:, _e * 512:(_e + 1) * 512])))
        for tt in range(16):
            for nv in range(2):
                pn, pb_ = pscyc[ci % 2]; ci += 1
                evi = (tt * 2 + nv) % 4

                def emitv(te, _tt=tt, _nv=nv, _pt=ps[pn]):
                    last = None
                    for k in range(2):
                        last = te.matmul(_pt[:, :],
                                         lat_sb[:, k * T + _tt * 128:k * T + _tt * 128 + 128],
                                         wbig[:, MLA_OFF["wvu"] + k * 1024 + _nv * 512:
                                              MLA_OFF["wvu"] + k * 1024 + _nv * 512 + 512],
                                         start=(k == 0), stop=(k == 1))
                    return last
                p.pe_group([pb_], [b_wbig, b_lat], emitv)
                p.op("act", [b_ev[evi]], [pb_], lambda e, _e=evi, _pt=ps[pn]: e.activation(
                    ev[:, _e * 512:(_e + 1) * 512], _pt[:, :], AF.Copy, scale=1.0))
                out_toks.append(p.dma("sp", [], [b_ev[evi]], lambda e, _tt=tt, _nv=nv, _e=evi: e.dma_start(
                    vm_out[_tt * 128:(_tt + 1) * 128, _nv * 512:(_nv + 1) * 512],
                    ev[:, _e * 512:(_e + 1) * 512])))

        def rope_block(src_kind, m_count, woff, nk, kstride, src_sb, src_buf,
                       ctab_off, perm_off, dst_dram):
            nonlocal ci
            for nt in range(4):
                p.dma("sp", [b_rope], [], lambda e, _nt=nt, _c=ctab_off: e.dma_start(
                    rope_sb[:, 0:512], (ropeCq if _c == 0 else ropeCk)[:, _nt * 512:(_nt + 1) * 512]))
                p.dma("sp", [b_rope], [b_rope], lambda e, _nt=nt, _c=ctab_off: e.dma_start(
                    rope_sb[:, 512:1024], (ropeSq if _c == 0 else ropeSk)[:, _nt * 512:(_nt + 1) * 512]))
                for m in range(m_count):
                    pn, pb_ = pscyc[ci % 2]; ci += 1
                    pt = ps[pn]

                    def emitr(te, _m=m, _nt=nt, _pt=pt):
                        last = None
                        for k in range(nk):
                            last = te.matmul(_pt[:, :],
                                             wbig[:, woff + k * kstride + _m * 128:
                                                  woff + k * kstride + _m * 128 + 128],
                                             src_sb[:, k * T + _nt * 512:k * T + _nt * 512 + 512],
                                             start=(k == 0), stop=(k == nk - 1))
                        return last
                    p.pe_group([pb_], [b_wbig, src_buf], emitr)
                    p.op("act", [b_rsc], [pb_], lambda e, _pt=pt: e.activation(
                        rsc[:, 0:512], _pt[:, :], AF.Copy, scale=1.0))
                    p.op("dve", [b_rbf], [b_rsc], lambda e: e.tensor_copy(
                        rbf[:, :], rsc[:, 0:512]))
                    pvn = "pv" + str(ci % 2)
                    p.pe_group([psb[pvn]], [b_perm, b_rbf], lambda te, _pv=ps[pvn], _po=perm_off: te.matmul(
                        _pv[:, :], perm_sb[:, _po:_po + 128], rbf[:, :], start=True, stop=True))
                    p.op("dve", [b_rsc], [b_rsc, b_rope], lambda e: e.tensor_tensor(
                        rsc[:, 512:1024], rsc[:, 0:512], rope_sb[:, 0:512], ALU.mult))
                    p.op("dve", [b_rsc], [b_rsc, psb[pvn], b_rope], lambda e, _pv=ps[pvn]: e.tensor_tensor(
                        rsc[:, 1024:1536], _pv[:, :], rope_sb[:, 512:1024], ALU.mult))
                    evi = ci % 4
                    p.op("dve", [b_ev[evi]], [b_rsc], lambda e, _e=evi: e.tensor_tensor(
                        ev[:, _e * 512:(_e + 1) * 512], rsc[:, 512:1024],
                        rsc[:, 1024:1536], ALU.add))
                    out_toks.append(p.dma("sp", [], [b_ev[evi]], lambda e, _m=m, _nt=nt, _e=evi: e.dma_start(
                        dst_dram[_m * 128:(_m + 1) * 128, _nt * 512:(_nt + 1) * 512],
                        ev[:, _e * 512:(_e + 1) * 512])))

        rope_block("kr", 4, MLA_OFF["wrk"], 8, 512, xn_sb, b_xn, 1, 128, kr_out)
        rope_block("qq", 8, MLA_OFF["wqq"], 2, 1024, qlat_sb, b_qlat, 0, 0, qq_out)
        p.finish(out_toks)
    return nc



def host_attn_partial_inputs(core, q_full, k_full, v_full):
    j = core % 4
    hh = j & 1
    slots = attn_slot_map(j)
    qp = np.zeros((512, N_SLOTS * 512), BF)
    kk = np.zeros((512, N_SLOTS * 2048), BF)
    vv = np.zeros((N_SLOTS * 2048, 512), BF)
    for u, (span, qt) in enumerate(slots):
        qp[:, u * 512:(u + 1) * 512] = q_full[hh * 512:(hh + 1) * 512, qt * 512:(qt + 1) * 512]
        kk[:, u * 2048:(u + 1) * 2048] = k_full[hh * 512:(hh + 1) * 512,
                                                span * 2048:(span + 1) * 2048]
        vv[u * 2048:(u + 1) * 2048, :] = v_full[span * 2048:(span + 1) * 2048,
                                                hh * 512:(hh + 1) * 512]
    return {"qp": qp, "kk_in": kk, "v_in": vv,
            "masks4": causal_mask_tiles().transpose(1, 0, 2).reshape(128, 4 * 512)}


def host_route_partials(core, res_2a):
    b, lo, hi = core_chunks(core)
    qtiles = [2 * lo, 2 * lo + 1, 2 * hi, 2 * hi + 1]
    o_parts = np.zeros((4 * 8 * 512, 512), BF)
    sums_in = np.zeros((256, 512), np.float32)
    for qi, qt in enumerate(qtiles):
        smax = qt // 4
        for hh in range(2):
            for s in range(min(smax + 1, 4)):
                jj = (0 if s in (0, 3) else 2) + hh
                cc = b * 4 + jj
                u = attn_slot_map(jj).index((s, qt))
                o_parts[(qi * 8 + hh * 4 + s) * 512:(qi * 8 + hh * 4 + s + 1) * 512, :] = \
                    res_2a[cc]["o_part"][:, u * 512:(u + 1) * 512]
                for hl in range(8):
                    h = hh * 8 + hl
                    sums_in[qi * 64 + h * 4 + s, :] = \
                        res_2a[cc]["sums_part"][hl, u * 512:(u + 1) * 512]
    return o_parts, sums_in


def host_sel8():
    sel = np.zeros((16, 8 * 128), np.float32)
    for pg in range(8):
        sel[2 * pg, pg * 128:pg * 128 + 64] = 1.0
        sel[2 * pg + 1, pg * 128 + 64:pg * 128 + 128] = 1.0
    return sel


def host_rope_mats(core):
    b, lo, hi = core_chunks(core)
    pos = np.concatenate([np.arange(lo * CH, (lo + 1) * CH),
                          np.arange(hi * CH, (hi + 1) * CH)])
    cos32, sin32 = rope_tables(pos)
    Cq = np.zeros((128, T), np.float32)
    Sq = np.zeros((128, T), np.float32)
    Cq[0:32] = 1.0
    Cq[64:96] = 1.0
    Cq[32:64] = cos32
    Cq[96:128] = cos32
    Sq[32:64] = sin32
    Sq[96:128] = sin32
    Ck = np.tile(cos32, (4, 1))
    Sk = np.tile(sin32, (4, 1))
    P = perm32()
    permq = np.zeros((128, 128), np.float32)
    permq[32:64, 32:64] = P
    permq[96:128, 96:128] = P
    permk = np.zeros((128, 128), np.float32)
    for i in range(4):
        permk[32 * i:32 * (i + 1), 32 * i:32 * (i + 1)] = P
    return Cq, Sq, Ck, Sk, permq, permk


def host_l2b_inputs(inputs, core, res_2a):
    b, lo, hi = core_chunks(core)
    x = np.asarray(inputs["L0_attn_x"][b])
    xc = np.concatenate([x[lo * CH:(lo + 1) * CH], x[hi * CH:(hi + 1) * CH]], 0)
    o_parts, sums_in = host_route_partials(core, res_2a)
    wqu = np.asarray(inputs["L0_ffn_W_qu"], np.float32)
    wrq = np.asarray(inputs["L0_ffn_W_ropeq"], np.float32)
    wqq = np.zeros((256, D), np.float32)
    for h in range(16):
        wqq[:, h * 64:h * 64 + 32] = wqu[:, h * 32:(h + 1) * 32]
        wqq[:, h * 64 + 32:h * 64 + 64] = wrq[:, h * 32:(h + 1) * 32]
    Cq, Sq, Ck, Sk, permq, permk = host_rope_mats(core)
    return {
        "o_parts": o_parts, "sums_in": sums_in, "xf": fm(xc),
        "wproj": np.ascontiguousarray(np.asarray(inputs["L0_attn_W_proj"], np.float32)),
        "sel8": host_sel8(),
        "bcols_in": np.concatenate([col_tiles(np.asarray(inputs["L0_attn_b_proj"])),
                                    col_tiles(np.asarray(inputs["L0_ffn_w_ln1"]))], 1),
        "wkvd": np.ascontiguousarray(np.asarray(inputs["L0_ffn_W_kvd"], np.float32)),
        "wqd": np.ascontiguousarray(np.asarray(inputs["L0_ffn_W_qd"], np.float32)),
        "wqq": wqq,
        "wku": np.ascontiguousarray(np.asarray(inputs["L0_ffn_W_ku"], np.float32)),
        "wrk": np.ascontiguousarray(np.asarray(inputs["L0_ffn_W_ropek"], np.float32)),
        "wvu": np.ascontiguousarray(np.asarray(inputs["L0_ffn_W_vu"], np.float32)),
        "ropeCq": Cq, "ropeSq": Sq, "ropeCk": Ck, "ropeSk": Sk,
        "permq": permq, "permk": permk,
    }


def assemble_full_fm(res, key, cores):
    Dp = res[cores[0]][key].shape[0]
    out = np.zeros((Dp, S), res[cores[0]][key].dtype)
    for c in cores:
        _, lo, hi = core_chunks(c)
        out[:, lo * CH:(lo + 1) * CH] = res[c][key][:, 0:CH]
        out[:, hi * CH:(hi + 1) * CH] = res[c][key][:, CH:2 * CH]
    return out


def assemble_full_tm(res, key, cores):
    Dp = res[cores[0]][key].shape[1]
    out = np.zeros((S, Dp), res[cores[0]][key].dtype)
    for c in cores:
        _, lo, hi = core_chunks(c)
        out[lo * CH:(lo + 1) * CH, :] = res[c][key][0:CH, :]
        out[hi * CH:(hi + 1) * CH, :] = res[c][key][CH:2 * CH, :]
    return out



def build_l3b():
    nc = bass.Bass("TRN2", target_bir_lowering=False)
    dr = lambda n, s, d, k: nc.dram_tensor(n, s, d, kind=k).ap()
    o_parts = dr("o_parts", [4 * 8 * 512, 512], BF16, "ExternalInput")
    sums_in = dr("sums_in", [4 * 16 * 4, 512], F32, "ExternalInput")
    r1 = dr("r1", [D, T], F32, "ExternalInput")
    w_o = dr("w_o", [D, D], F32, "ExternalInput")
    sel8 = dr("sel8", [16, 8 * 128], F32, "ExternalInput")
    bcols_in = dr("bcols_in", [128, 8], F32, "ExternalInput")
    wg_s = dr("wg_s", [D, D], F32, "ExternalInput")
    wu_s = dr("wu_s", [D, D], F32, "ExternalInput")
    wd_s = dr("wd_s", [D, D], F32, "ExternalInput")
    wg_r = dr("wg_r", [7, D, D], F32, "ExternalInput")
    wu_r = dr("wu_r", [7, D, D], F32, "ExternalInput")
    wd_r = dr("wd_r", [7, D, D], F32, "ExternalInput")
    w_router = dr("w_router", [D, 7], F32, "ExternalInput")
    rb_in = dr("rb_in", [1, 7], F32, "ExternalInput")
    idf32 = dr("idf32", [128, 128], F32, "ExternalInput")
    esel_in = dr("esel_in", [7, 7 * 128], F32, "ExternalInput")
    wqk_s_in = dr("wqk_s_in", [D, 512], F32, "ExternalInput")
    wv_s_in = dr("wv_s_in", [D, 512], F32, "ExternalInput")
    wqk_l_in = dr("wqk_l_in", [D, 2048], F32, "ExternalInput")
    wv_l_in = dr("wv_l_in", [D, 512], F32, "ExternalInput")
    r2_out = dr("r2_out", [D, T], F32, "ExternalOutput")
    qs_out = dr("qs_out", [256, T], BF16, "ExternalOutput")
    ks_out = dr("ks_out", [256, T], BF16, "ExternalOutput")
    vs_out = dr("vs_out", [T, 512], BF16, "ExternalOutput")
    ql_out = dr("ql_out", [D, T], BF16, "ExternalOutput")
    kl_out = dr("kl_out", [D, T], BF16, "ExternalOutput")
    vl_out = dr("vl_out", [T, 512], BF16, "ExternalOutput")
    x2r_dram = nc.dram_tensor("x2r_dram", [D, T], F32, kind="Internal").ap()
    y_dram = nc.dram_tensor("y_dram", [D, T], F32, kind="Internal").ap()

    out_toks = []
    with ExitStack() as ctx:
        p = Prog(nc, ctx)
        SH, b_sh0 = p.sb("SH", [128, 24576], BF16)
        xn2, b_xn2 = p.sb("xn2", [128, 8 * T], BF16)
        scr_bf, b_scrbf = p.sb("scr_bf", [128, 4 * 512], BF16)
        rows, b_rows = p.sb("rows", [1, 4 * 512], F32)
        bcols, b_bcols = p.sb("bcols", [128, 18], F32)
        t_ar, b_t = p.sb("t_ar", [128, 8 * 512], F32)
        x2_ar, b_x2 = p.sb("x2_ar", [128, 8 * 512], F32)
        r1_st, b_r1st = p.sb("r1_st", [128, 8 * 512], F32)
        s_sb, b_s = p.sb("s_sb", [16, 4 * 512], F32)
        srow, b_srow = p.sb("srow", [16, 4 * 512], F32)
        sbf, b_sbf = p.sb("sbf", [16, 512], BF16)
        sel_sb, b_sel = p.sb("sel_sb", [16, 8 * 128], BF16)
        rt_sb, b_rt = p.sb("rt_sb", [128, 512], BF16)
        ones_col, b_ones = p.sb("ones_col", [128, 1], BF16)
        ones_f32, b_ones32 = p.sb("ones_f32", [1, 128], F32)
        ones_r, b_onesr = p.sb("ones_r", [1, 128], BF16)
        wrt_sb, b_wrt = p.sb("wrt_sb", [128, 64], BF16)
        rbb, b_rbb = p.sb("rbb", [1, 8], BF16)
        id_sb, b_id = p.sb("id_sb", [128, 128], F32)
        lgs, b_lgs = p.sb("lgs", [128, 32], F32)
        esel_sb, b_esel = p.sb("esel_sb", [7, 7 * 128], F32)
        wts_fm, b_wts = p.sb("wts_fm", [7, 2048], F32)
        gu_sb, b_gu = p.sb("gu_sb", [128, 3 * 512], BF16)
        wb_sb, b_wb = p.sb("wb_sb", [128, 512], BF16)
        hmid, b_hm = p.sb("hmid", [128, 8 * 512], BF16)
        yev, b_yev0 = p.sb("yev", [128, 2 * 512], F32)
        b_yev = [b_yev0, Buf("yev1")]
        ev, b_ev0 = p.sb("ev", [128, 4 * 512], BF16)
        b_ev = [b_ev0] + [Buf(f"ev{i}") for i in range(1, 4)]
        r2st, b_r2st = p.sb("r2st", [128, 8 * 512], BF16)

        ps, psb = {}, {}
        for n in ("pv0", "pv1", "pv2", "pv3", "s0", "s1", "u0", "u1"):
            ps[n], psb[n] = p.ps("ps_" + n, [128, 512])

        p.dma("pool", [b_sel], [], lambda e: e.dma_start(sel_sb[:, :], sel8))
        p.dma("sp", [b_bcols], [], lambda e: e.dma_start(bcols[:, 0:8], bcols_in))
        p.op("dve", [b_bcols], [b_bcols], lambda e: e.memset(bcols[:, 16:17], 1.0 / 1024.0))
        p.op("dve", [b_bcols], [b_bcols], lambda e: e.memset(bcols[:, 17:18], 1e-5))
        p.op("dve", [b_ones], [], lambda e: e.memset(ones_col[:, :], 1.0))
        p.op("dve", [b_ones32], [], lambda e: e.memset(ones_f32[:, :], 1.0))
        p.op("dve", [b_onesr], [], lambda e: e.memset(ones_r[:, :], 1.0))
        p.dma("pool", [b_rbb], [], lambda e: e.dma_start(rbb[:, 0:7], rb_in))
        p.dma("sp", [b_id], [], lambda e: e.dma_start(id_sb[:, :], idf32))
        p.dma("sp", [b_esel], [], lambda e: e.dma_start(esel_sb[:, :], esel_in))
        p.dma("pool", [b_wrt], [], lambda e: e.dma_start(
            wrt_sb[:].rearrange("p (k m) -> p k m", k=8)[:, :, 0:7],
            w_router.rearrange("(k p) m -> p k m", p=128)))
        b_wbig = Buf("wo")
        wbig = SH[:, 13824:22016]
        p.dma("pool", [b_wbig], [], lambda e: e.dma_start(
            wbig.rearrange("p (k m) -> p k m", k=8),
            w_o.rearrange("(k p) m -> p k m", p=128)))

        b_oc, b_of, b_on = Buf("oc"), Buf("ofull"), Buf("onorm")
        oc_sb = SH[:, 0:8192]
        o_scr = SH[:, 8192:9728]
        onorm = SH[:, 9728:13824]

        for qt in range(4):
            p.dma("sp", [b_r1st], [], lambda e, _q=qt: e.dma_start(
                r1_st[:].rearrange("p (g t) -> p g t", g=8),
                r1[:, _q * 512:(_q + 1) * 512].rearrange("(g p) t -> p g t", p=128)))
            p.dma("sp", [b_s], [], lambda e, _q=qt: e.dma_start(
                s_sb[:].rearrange("h (s t) -> h s t", s=4),
                sums_in[_q * 64:(_q + 1) * 64, :].rearrange("(h s) t -> h s t", s=4)))
            p.op("dve", [b_srow], [b_s], lambda e: e.tensor_tensor(
                srow[:, 0:512], s_sb[:, 0:512], s_sb[:, 512:1024], ALU.add))
            p.op("dve", [b_srow], [b_s, b_srow], lambda e: e.tensor_tensor(
                srow[:, 512:1024], s_sb[:, 1024:1536], s_sb[:, 1536:2048], ALU.add))
            p.op("dve", [b_srow], [b_srow], lambda e: e.tensor_tensor(
                srow[:, 1024:1536], srow[:, 0:512], srow[:, 512:1024], ALU.add))
            p.op("dve", [b_srow], [b_srow], lambda e: e.reciprocal(
                srow[:, 1536:2048], srow[:, 1024:1536]))
            p.op("dve", [b_sbf], [b_srow], lambda e: e.tensor_copy(
                sbf[:, :], srow[:, 1536:2048]))
            for hh in range(2):
                p.dma("sp", [b_oc], [], lambda e, _q=qt, _h=hh: e.dma_start(
                    oc_sb.rearrange("p (s c t) -> p s c t", s=4, c=4),
                    o_parts[_q * 4096 + _h * 2048:_q * 4096 + (_h + 1) * 2048, :].rearrange(
                        "(s c p) t -> p s c t", p=128, c=4)))
                for pi in range(4):
                    pg = hh * 4 + pi
                    p.op("dve", [b_of], [b_oc], lambda e, _pi=pi: e.tensor_tensor(
                        o_scr[:, 0:512],
                        oc_sb[:, (0 * 4 + _pi) * 512:(0 * 4 + _pi + 1) * 512],
                        oc_sb[:, (1 * 4 + _pi) * 512:(1 * 4 + _pi + 1) * 512], ALU.add))
                    p.op("dve", [b_of], [b_oc, b_of], lambda e, _pi=pi: e.tensor_tensor(
                        o_scr[:, 512:1024],
                        oc_sb[:, (2 * 4 + _pi) * 512:(2 * 4 + _pi + 1) * 512],
                        oc_sb[:, (3 * 4 + _pi) * 512:(3 * 4 + _pi + 1) * 512], ALU.add))
                    p.op("dve", [b_of], [b_of], lambda e: e.tensor_tensor(
                        o_scr[:, 1024:1536], o_scr[:, 0:512], o_scr[:, 512:1024], ALU.add))
                    p.pe_group([psb["s0"]], [b_sbf, b_sel], lambda te, _pg=pg: te.matmul(
                        ps["s0"][:, :], sel_sb[:, _pg * 128:(_pg + 1) * 128],
                        sbf[:, :], start=True, stop=True))
                    p.op("act", [b_rt], [psb["s0"]], lambda e: e.activation(
                        rt_sb[:, :], ps["s0"][:, :], AF.Copy, scale=1.0))
                    p.op("dve", [b_on], [b_of, b_rt], lambda e, _pg=pg: e.tensor_tensor(
                        onorm[:, _pg * 512:(_pg + 1) * 512],
                        o_scr[:, 1024:1536], rt_sb[:, :], ALU.mult))
            for fo in range(8):
                pt, pb_ = (ps["s1"], psb["s1"]) if fo % 2 == 0 else (ps["pv0"], psb["pv0"])

                def emitc(te, _fo=fo, _pt=pt):
                    last = None
                    for k in range(8):
                        last = te.matmul(_pt[:, :],
                                         SH[:, 13824 + k * 1024 + _fo * 128:
                                            13824 + k * 1024 + _fo * 128 + 128],
                                         onorm[:, k * 512:(k + 1) * 512],
                                         start=(k == 0), stop=(k == 7))
                    return last
                p.pe_group([pb_], [b_wbig, b_on], emitc)
                p.op("dve", [b_x2], [pb_, b_r1st], lambda e, _fo=fo, _pt=pt: e.tensor_tensor(
                    x2_ar[:, _fo * 512:(_fo + 1) * 512], _pt[:, :],
                    r1_st[:, _fo * 512:(_fo + 1) * 512], ALU.add))
                p.op("dve", [b_t], [b_x2, b_r1st], lambda e, _fo=fo: e.tensor_tensor(
                    t_ar[:, _fo * 512:(_fo + 1) * 512],
                    x2_ar[:, _fo * 512:(_fo + 1) * 512],
                    r1_st[:, _fo * 512:(_fo + 1) * 512], ALU.add))
            out_toks.append(p.dma("sp", [], [b_t], lambda e, _q=qt: e.dma_start(
                x2r_dram[:, _q * 512:(_q + 1) * 512].rearrange("(g p) t -> p g t", p=128),
                t_ar[:].rearrange("p (g t) -> p g t", g=8))))
            emit_rms_chunk(
                p, src_ap=x2_ar, src_buf=b_x2, col_src=0, scr_bf=scr_bf,
                scr_bf_buf=b_scrbf, rows=rows, rows_buf=b_rows, consts=bcols,
                consts_buf=b_bcols, ones_f32=ones_f32, ones_f32_buf=b_ones32,
                ones_bf=ones_col, ones_bf_buf=b_ones, psQ=ps["u1"], psQb=psb["u1"],
                psA=ps["pv2"], psAb=psb["pv2"], w_col=bcols, w_col_buf=b_bcols,
                xn_ap=xn2, xn_buf=b_xn2, col_dst=qt * 512, scr_f32=t_ar,
                scr_f32_buf=b_t, src_stride=512, dst_stride=T, w_off=0)

        for tt in range(16):
            def emitr(te, _tt=tt):
                for k in range(8):
                    te.matmul(ps["s1"][0:128, 0:7],
                              xn2[:, k * T + _tt * 128:k * T + _tt * 128 + 128],
                              wrt_sb[:, k * 8:k * 8 + 7],
                              start=(k == 0), stop=False)
                return te.matmul(ps["s1"][0:128, 0:7], ones_r[:, :], rbb[:, 0:7],
                                 start=False, stop=True)
            p.pe_group([psb["s1"]], [b_xn2, b_wrt, b_onesr, b_rbb], emitr)
            p.op("act", [b_lgs], [psb["s1"]], lambda e: e.activation(
                lgs[:, 0:7], ps["s1"][0:128, 0:7], AF.Exp, scale=-1.0))
            p.op("dve", [b_lgs], [b_lgs], lambda e: e.tensor_scalar(
                lgs[:, 8:15], lgs[:, 0:7], 1.0, None, ALU.add))
            p.op("dve", [b_lgs], [b_lgs], lambda e: e.reciprocal(
                lgs[:, 16:23], lgs[:, 8:15]))
            p.op("dve", [b_lgs], [b_lgs], lambda e: e.reduce_max(
                lgs[:, 24:25], lgs[:, 16:23], axis=AX.X))
            p.op("dve", [b_lgs], [b_lgs], lambda e: e.tensor_scalar(
                lgs[:, 0:7], lgs[:, 16:23], lgs[:, 24:25], None, ALU.is_ge))
            p.op("dve", [b_lgs], [b_lgs], lambda e: e.scalar_tensor_tensor(
                lgs[:, 8:15], lgs[:, 0:7], -1e9, lgs[:, 16:23], ALU.mult, ALU.add))
            p.op("dve", [b_lgs], [b_lgs], lambda e: e.reduce_max(
                lgs[:, 25:26], lgs[:, 8:15], axis=AX.X))
            p.op("dve", [b_lgs], [b_lgs], lambda e: e.tensor_tensor(
                lgs[:, 26:27], lgs[:, 24:25], lgs[:, 25:26], ALU.add))
            p.op("dve", [b_lgs], [b_lgs], lambda e: e.reciprocal(
                lgs[:, 27:28], lgs[:, 26:27]))
            p.op("dve", [b_lgs], [b_lgs], lambda e: e.tensor_scalar(
                lgs[:, 0:7], lgs[:, 16:23], lgs[:, 25:26], None, ALU.is_ge))
            p.op("dve", [b_lgs], [b_lgs], lambda e: e.tensor_tensor(
                lgs[:, 8:15], lgs[:, 16:23], lgs[:, 0:7], ALU.mult))
            p.op("dve", [b_lgs], [b_lgs], lambda e: e.tensor_scalar(
                lgs[:, 0:7], lgs[:, 8:15], lgs[:, 27:28], None, ALU.mult))
            p.pe_group([psb["u0"]], [b_lgs, b_id], lambda te: te.transpose(
                ps["u0"][0:7, 0:128], lgs[:, 0:7], id_sb[:, :]))
            p.op("dve", [b_wts], [psb["u0"]], lambda e, _tt=tt: e.tensor_copy(
                wts_fm[:, _tt * 128:(_tt + 1) * 128], ps["u0"][0:7, 0:128]))

        b_wg, b_wu, b_wd = Buf("wg"), Buf("wu"), Buf("wd")
        wg_a = SH[:, 0:8192]
        wu_a = SH[:, 8192:16384]
        wd_a = SH[:, 16384:24576]
        yregs = [Buf(f"yreg{i}") for i in range(32)]
        for e in range(8):
            for (dst, bufs, src) in ((wg_a, b_wg, wg_s if e == 0 else wg_r),
                                     (wu_a, b_wu, wu_s if e == 0 else wu_r),
                                     (wd_a, b_wd, wd_s if e == 0 else wd_r)):
                extra = [b_oc, b_of, b_on, b_sh0, b_wbig] if e == 0 else []
                if e == 0:
                    p.dma("pool", [bufs] + extra, [], lambda ee, _d=dst, _s=src: ee.dma_start(
                        _d.rearrange("p (k m) -> p k m", k=8),
                        _s.rearrange("(k p) m -> p k m", p=128)))
                else:
                    p.dma("pool", [bufs], [], lambda ee, _d=dst, _s=src, _e=e: ee.dma_start(
                        _d.rearrange("p (k m) -> p k m", k=8),
                        _s[_e - 1:_e].rearrange("one (k p) m -> p (one k) m", p=128)))
            for nt in range(4):
                if e > 0:
                    p.pe_group([psb["u1"]], [b_wts, b_esel], lambda te, _e=e, _nt=nt: te.matmul(
                        ps["u1"][:, :], esel_sb[:, (_e - 1) * 128:_e * 128],
                        wts_fm[0:7, _nt * 512:(_nt + 1) * 512], start=True, stop=True))
                    p.op("act", [b_wb], [psb["u1"]], lambda e_: e_.activation(
                        wb_sb[:, :], ps["u1"][:, :], AF.Copy, scale=1.0))
                for i in range(8):
                    pg_, pgb = (ps["pv0"], psb["pv0"]) if i % 2 == 0 else (ps["pv1"], psb["pv1"])
                    pu_, pub = (ps["s0"], psb["s0"]) if i % 2 == 0 else (ps["s1"], psb["s1"])

                    def emitg(te, _i=i, _nt=nt, _pt=pg_):
                        last = None
                        for k in range(8):
                            last = te.matmul(_pt[:, :],
                                             wg_a[:, k * 1024 + _i * 128:k * 1024 + _i * 128 + 128],
                                             xn2[:, k * T + _nt * 512:k * T + _nt * 512 + 512],
                                             start=(k == 0), stop=(k == 7))
                        return last
                    p.pe_group([pgb], [b_wg, b_xn2], emitg)

                    def emitu(te, _i=i, _nt=nt, _pt=pu_):
                        last = None
                        for k in range(8):
                            last = te.matmul(_pt[:, :],
                                             wu_a[:, k * 1024 + _i * 128:k * 1024 + _i * 128 + 128],
                                             xn2[:, k * T + _nt * 512:k * T + _nt * 512 + 512],
                                             start=(k == 0), stop=(k == 7))
                        return last
                    p.pe_group([pub], [b_wu, b_xn2], emitu)
                    p.op("act", [b_gu], [pgb], lambda e_, _pt=pg_: e_.activation(
                        gu_sb[:, 0:512], _pt[:, :], AF.Silu, scale=1.0))
                    p.op("act", [b_gu], [pub, b_gu], lambda e_, _pt=pu_: e_.activation(
                        gu_sb[:, 512:1024], _pt[:, :], AF.Copy, scale=1.0))
                    if e == 0:
                        p.op("dve", [b_hm], [b_gu], lambda e_, _i=i: e_.tensor_tensor(
                            hmid[:, _i * 512:(_i + 1) * 512],
                            gu_sb[:, 0:512], gu_sb[:, 512:1024], ALU.mult))
                    else:
                        p.op("dve", [b_gu], [b_gu], lambda e_: e_.tensor_tensor(
                            gu_sb[:, 1024:1536], gu_sb[:, 0:512], gu_sb[:, 512:1024], ALU.mult))
                        p.op("dve", [b_hm], [b_gu, b_wb], lambda e_, _i=i: e_.tensor_tensor(
                            hmid[:, _i * 512:(_i + 1) * 512],
                            gu_sb[:, 1024:1536], wb_sb[:, :], ALU.mult))
                for fo in range(8):
                    pd_, pdb = (ps["pv2"], psb["pv2"]) if fo % 2 == 0 else (ps["pv3"], psb["pv3"])

                    def emitd(te, _fo=fo, _pt=pd_):
                        last = None
                        for i in range(8):
                            last = te.matmul(_pt[:, :],
                                             wd_a[:, i * 1024 + _fo * 128:i * 1024 + _fo * 128 + 128],
                                             hmid[:, i * 512:(i + 1) * 512],
                                             start=(i == 0), stop=(i == 7))
                        return last
                    p.pe_group([pdb], [b_wd, b_hm], emitd)
                    yv = b_yev[fo % 2]
                    p.op("dve", [yv], [pdb], lambda e_, _fo=fo, _pt=pd_: e_.tensor_copy(
                        yev[:, (_fo % 2) * 512:(_fo % 2 + 1) * 512], _pt[:, :]))
                    yr = yregs[fo * 4 + nt]
                    p.dma("pool", [yr], [yv], lambda e_, _fo=fo, _nt=nt, _e=e: e_.dma_start(
                        y_dram[_fo * 128:(_fo + 1) * 128, _nt * 512:(_nt + 1) * 512],
                        yev[:, (_fo % 2) * 512:(_fo % 2 + 1) * 512],
                        accum_op=(ALU.bypass if _e == 0 else ALU.add)))

        r2regs = [Buf(f"r2reg{i}") for i in range(4)]
        for qt in range(4):
            p.dma("sp", [b_x2], list(yregs[qt::4]), lambda e, _q=qt: e.dma_start(
                x2_ar[:].rearrange("p (g t) -> p g t", g=8),
                y_dram[:, _q * 512:(_q + 1) * 512].rearrange("(g p) t -> p g t", p=128)))
            p.dma("sp", [b_r1st], [], lambda e, _q=qt: e.dma_start(
                r1_st[:].rearrange("p (g t) -> p g t", g=8),
                x2r_dram[:, _q * 512:(_q + 1) * 512].rearrange("(g p) t -> p g t", p=128)))
            for g in range(8):
                p.op("dve", [b_t], [b_x2, b_r1st], lambda e, _g=g: e.tensor_tensor(
                    t_ar[:, _g * 512:(_g + 1) * 512],
                    x2_ar[:, _g * 512:(_g + 1) * 512],
                    r1_st[:, _g * 512:(_g + 1) * 512], ALU.add))
            emit_postln_chunk(
                p, t_ap=t_ar, t_buf=b_t, col=0, scr_bf=scr_bf, scr_bf_buf=b_scrbf,
                rows=rows, rows_buf=b_rows, consts=bcols, consts_buf=b_bcols,
                ones_f32=ones_f32, ones_f32_buf=b_ones32, ones_bf=ones_col,
                ones_bf_buf=b_ones, psS=ps["u0"], psSb=psb["u0"], psQ=ps["u1"],
                psQb=psb["u1"], psA=ps["pv2"], psAb=psb["pv2"], psB=ps["pv3"],
                psBb=psb["pv3"],
                out_f32_ap=lambda g: (x2_ar[:, g * 512:(g + 1) * 512], b_x2),
                stride=512)
            out_toks.append(p.dma("sp", [r2regs[qt]], [b_x2], lambda e, _q=qt: e.dma_start(
                r2_out[:, _q * 512:(_q + 1) * 512].rearrange("(g p) t -> p g t", p=128),
                x2_ar[:].rearrange("p (g t) -> p g t", g=8))))

        b_w3 = Buf("w3a")
        p.dma("pool", [b_w3, b_wg, b_wu, b_wd], [], lambda e: e.dma_start(
            SH[:, 0:4096].rearrange("p (k m) -> p k m", k=8),
            wqk_s_in.rearrange("(k p) m -> p k m", p=128)))
        p.dma("pool", [b_w3], [b_w3], lambda e: e.dma_start(
            SH[:, 4096:8192].rearrange("p (k m) -> p k m", k=8),
            wv_s_in.rearrange("(k p) m -> p k m", p=128)))
        p.dma("pool", [b_w3], [b_w3], lambda e: e.dma_start(
            SH[:, 8192:12288].rearrange("p (k m) -> p k m", k=8),
            wv_l_in.rearrange("(k p) m -> p k m", p=128)))
        ci = 0
        pscyc = [(ps["s0"], psb["s0"]), (ps["s1"], psb["s1"])]
        for nt in range(4):
            p.dma("pool", [b_r2st], [r2regs[nt]], lambda e, _nt=nt: e.dma_start(
                r2st[:].rearrange("p (k t) -> p k t", k=8),
                r2_out[:, _nt * 512:(_nt + 1) * 512].rearrange("(k p) t -> p k t", p=128)))
            for sec, dst in ((0, qs_out), (256, ks_out)):
                for m in range(2):
                    pt, pb_ = pscyc[ci % 2]; ci += 1

                    def emits(te, _m=m, _nt=nt, _sec=sec, _pt=pt):
                        last = None
                        for k in range(8):
                            last = te.matmul(_pt[:, :],
                                             SH[:, k * 512 + _sec + _m * 128:
                                                k * 512 + _sec + _m * 128 + 128],
                                             r2st[:, k * 512:(k + 1) * 512],
                                             start=(k == 0), stop=(k == 7))
                        return last
                    p.pe_group([pb_], [b_w3, b_r2st], emits)
                    evi = ci % 4
                    p.op("act", [b_ev[evi]], [pb_], lambda e, _e=evi, _pt=pt: e.activation(
                        ev[:, _e * 512:(_e + 1) * 512], _pt[:, :], AF.Copy, scale=1.0))
                    out_toks.append(p.dma("sp", [], [b_ev[evi]],
                                          lambda e, _m=m, _nt=nt, _e=evi, _d=dst: e.dma_start(
                                              _d[_m * 128:(_m + 1) * 128, _nt * 512:(_nt + 1) * 512],
                                              ev[:, _e * 512:(_e + 1) * 512])))
            for off, dst in ((4096, vs_out), (8192, vl_out)):
                for tl in range(4):
                    pt, pb_ = pscyc[ci % 2]; ci += 1

                    def emitv(te, _tl=tl, _nt=nt, _off=off, _pt=pt):
                        last = None
                        for k in range(8):
                            last = te.matmul(_pt[:, :],
                                             r2st[:, k * 512 + _tl * 128:k * 512 + _tl * 128 + 128],
                                             SH[:, _off + k * 512:_off + (k + 1) * 512],
                                             start=(k == 0), stop=(k == 7))
                        return last
                    p.pe_group([pb_], [b_w3, b_r2st], emitv)
                    evi = ci % 4
                    p.op("act", [b_ev[evi]], [pb_], lambda e, _e=evi, _pt=pt: e.activation(
                        ev[:, _e * 512:(_e + 1) * 512], _pt[:, :], AF.Copy, scale=1.0))
                    out_toks.append(p.dma("sp", [], [b_ev[evi]],
                                          lambda e, _tl=tl, _nt=nt, _e=evi, _d=dst: e.dma_start(
                                              _d[_nt * 512 + _tl * 128:_nt * 512 + (_tl + 1) * 128, :],
                                              ev[:, _e * 512:(_e + 1) * 512])))
        b_w3b = Buf("w3b")
        p.dma("pool", [b_w3b, b_w3], [], lambda e: e.dma_start(
            SH[:, 0:16384].rearrange("p (k m) -> p k m", k=8),
            wqk_l_in.rearrange("(k p) m -> p k m", p=128)))
        for nt in range(4):
            p.dma("pool", [b_r2st], [r2regs[nt]], lambda e, _nt=nt: e.dma_start(
                r2st[:].rearrange("p (k t) -> p k t", k=8),
                r2_out[:, _nt * 512:(_nt + 1) * 512].rearrange("(k p) t -> p k t", p=128)))
            for sec, dst in ((0, ql_out), (1024, kl_out)):
                for m in range(8):
                    pt, pb_ = pscyc[ci % 2]; ci += 1

                    def emitl(te, _m=m, _nt=nt, _sec=sec, _pt=pt):
                        last = None
                        for k in range(8):
                            last = te.matmul(_pt[:, :],
                                             SH[:, k * 2048 + _sec + _m * 128:
                                                k * 2048 + _sec + _m * 128 + 128],
                                             r2st[:, k * 512:(k + 1) * 512],
                                             start=(k == 0), stop=(k == 7))
                        return last
                    p.pe_group([pb_], [b_w3b, b_r2st], emitl)
                    evi = ci % 4
                    p.op("act", [b_ev[evi]], [pb_], lambda e, _e=evi, _pt=pt: e.activation(
                        ev[:, _e * 512:(_e + 1) * 512], _pt[:, :], AF.Copy, scale=1.0))
                    out_toks.append(p.dma("sp", [], [b_ev[evi]],
                                          lambda e, _m=m, _nt=nt, _e=evi, _d=dst: e.dma_start(
                                              _d[_m * 128:(_m + 1) * 128, _nt * 512:(_nt + 1) * 512],
                                              ev[:, _e * 512:(_e + 1) * 512])))
        p.finish(out_toks)
    return nc


def host_l3b_inputs(inputs, core, res_3a, res_2b):
    o_parts, sums_in = host_route_partials(core, res_3a)
    return {
        "o_parts": o_parts, "sums_in": sums_in,
        "r1": res_2b[core]["r1_out"],
        "w_o": np.ascontiguousarray(np.asarray(inputs["L0_ffn_W_o"], np.float32)),
        "sel8": host_sel8(),
        "bcols_in": col_tiles(np.asarray(inputs["L0_ffn_w_ln2"])),
        "wg_s": np.ascontiguousarray(np.asarray(inputs["L0_ffn_Wg_s"], np.float32)),
        "wu_s": np.ascontiguousarray(np.asarray(inputs["L0_ffn_Wu_s"], np.float32)),
        "wd_s": np.ascontiguousarray(np.asarray(inputs["L0_ffn_Wd_s"], np.float32)),
        "wg_r": np.ascontiguousarray(np.asarray(inputs["L0_ffn_Wg_r"], np.float32)),
        "wu_r": np.ascontiguousarray(np.asarray(inputs["L0_ffn_Wu_r"], np.float32)),
        "wd_r": np.ascontiguousarray(np.asarray(inputs["L0_ffn_Wd_r"], np.float32)),
        "w_router": np.ascontiguousarray(np.asarray(inputs["L0_ffn_W_router"], np.float32)),
        "rb_in": np.asarray(inputs["L0_ffn_routing_bias"], np.float32).reshape(1, 7),
        "idf32": np.eye(128, dtype=np.float32),
        "esel_in": np.concatenate([np.tile((np.arange(7) == e).astype(np.float32)[:, None],
                                           (1, 128)) for e in range(7)], 1),
        "wqk_s_in": np.ascontiguousarray(np.asarray(inputs["L1_attn_Wqk_short"], np.float32)),
        "wv_s_in": np.ascontiguousarray(np.asarray(inputs["L1_attn_Wv_short"], np.float32)),
        "wqk_l_in": np.ascontiguousarray(np.asarray(inputs["L1_attn_Wqk_long"], np.float32)),
        "wv_l_in": np.ascontiguousarray(np.asarray(inputs["L1_attn_Wv_long"], np.float32)),
    }



NKB_S, NKB_L = 6, 12


def build_l4():
    nc = bass.Bass("TRN2", target_bir_lowering=False)
    dr = lambda n, s, d, k: nc.dram_tensor(n, s, d, kind=k).ap()
    qs_in = dr("qs_in", [256, T], BF16, "ExternalInput")
    ql_in = dr("ql_in", [D, T], BF16, "ExternalInput")
    ks_pad = dr("ks_pad", [256, 2 * 2048], BF16, "ExternalInput")
    vs_pad = dr("vs_pad", [2 * 2048, 512], BF16, "ExternalInput")
    kl_pad = dr("kl_pad", [D, 2 * 2048], BF16, "ExternalInput")
    vl_pad = dr("vl_pad", [2 * 2048, 512], BF16, "ExternalInput")
    masksL4 = dr("masksL4", [4 * 18 * 128, 512], F32, "ExternalInput")
    r2 = dr("r2", [D, T], F32, "ExternalInput")
    wproj = dr("wproj", [D, D], F32, "ExternalInput")
    sel2 = dr("sel2", [128, 2 * 128], F32, "ExternalInput")
    wexp = dr("wexp", [D, 4096], F32, "ExternalInput")
    mupT = dr("mupT", [7 * 32 * 128, 128], F32, "ExternalInput")
    mdnT = dr("mdnT", [7 * 32 * 128, 128], F32, "ExternalInput")
    w7up = dr("w7up", [128, 5 * 32 * 2], F32, "ExternalInput")
    w7dn = dr("w7dn", [128, 5 * 32 * 2], F32, "ExternalInput")
    bias_in = dr("bias_in", [128, 80], F32, "ExternalInput")
    wcon = dr("wcon", [4096, D], F32, "ExternalInput")
    idbf = dr("idbf", [128, 128], F32, "ExternalInput")
    out_f = dr("out_f", [D, T], F32, "ExternalOutput")
    r3_dram = nc.dram_tensor("r3_dram", [D, T], F32, kind="Internal").ap()

    out_toks = []
    with ExitStack() as ctx:
        p = Prog(nc, ctx)
        WS, b_ws = p.sb("WS", [128, 16384], BF16)
        hA_t, b_hA = p.sb("hA", [128, 32 * 512], BF16)
        hB_t, b_hB = p.sb("hB", [128, 32 * 512], BF16)
        ET, b_ET = p.sb("ET", [128, 32 * 128], BF16)
        EDT, b_EDT = p.sb("EDT", [128, 32 * 128], BF16)
        x_st, b_xst = p.sb("x_st", [128, 8 * 512], F32)
        t_ar, b_t = p.sb("t_ar", [128, 8 * 512], F32)
        u_ar, b_u = p.sb("u_ar", [128, 8 * 512], F32)
        r3bf, b_r3bf = p.sb("r3bf", [128, 8 * 512], BF16)
        scr_bf, b_scrbf = p.sb("scr_bf", [128, 4 * 512], BF16)
        rows, b_rows = p.sb("rows", [1, 4 * 512], F32)
        b_e = [Buf("e0"), Buf("e1"), Buf("e2"), Buf("e3")]
        b_o = Buf("ocarve")
        o_raw, b_oraw = p.sb("o_raw", [128, 1024], BF16)
        r_sb, b_r = p.sb("r_sb", [128, 1024], F32)
        rb_sb, b_rb = p.sb("rb_sb", [128, 1024], BF16)
        sel_sb, b_sel = p.sb("sel_sb", [128, 256], BF16)
        csh, b_csh = p.sb("csh", [128, 1], F32)
        ones_col, b_ones = p.sb("ones_col", [128, 1], BF16)
        ones_f32, b_ones32 = p.sb("ones_f32", [1, 128], F32)
        bcols, b_bcols = p.sb("bcols", [128, 100], F32)
        w7, b_w7 = p.sb("w7", [128, 2 * 320], F32)
        mtmp, b_mtmp = p.sb("mtmp", [128, 128], BF16)
        id_sb, b_id = p.sb("id_sb", [128, 128], BF16)
        ev, b_ev0 = p.sb("ev", [128, 4 * 512], BF16)
        b_ev = [b_ev0] + [Buf(f"ev{i}") for i in range(1, 4)]

        b_q, b_msk = Buf("qcarve"), Buf("mskcarve")
        b_kst = [Buf("kst0"), Buf("kst1")]
        b_vst = [Buf("vst0"), Buf("vst1")]
        q_sb = hA_t[:, 0:5120]
        msk = hA_t[:, 5120:14336]
        kst = hB_t[:, 0:4096]
        vst = hB_t[:, 4096:8192]
        o_sb = hB_t[:, 8192:12288]
        e_sb = hB_t[:, 12288:14336]

        ps, psb = {}, {}
        for n in ("pv0", "pv1", "pv2", "pv3", "s0", "s1", "u0", "u1"):
            ps[n], psb[n] = p.ps("ps_" + n, [128, 512])

        p.dma("pool", [b_sel], [], lambda e: e.dma_start(sel_sb[:, :], sel2))
        p.dma("pool", [b_id], [], lambda e: e.dma_start(id_sb[:, :], idbf))
        p.op("dve", [b_csh], [], lambda e: e.memset(csh[:, :], -C_SHIFT))
        p.op("dve", [b_ones], [], lambda e: e.memset(ones_col[:, :], 1.0))
        p.op("dve", [b_ones32], [], lambda e: e.memset(ones_f32[:, :], 1.0))
        p.dma("sp", [b_bcols], [], lambda e: e.dma_start(bcols[:, 20:100], bias_in))
        p.op("dve", [b_bcols], [b_bcols], lambda e: e.memset(bcols[:, 16:17], 1.0 / 1024.0))
        p.op("dve", [b_bcols], [b_bcols], lambda e: e.memset(bcols[:, 17:18], 1e-5))
        p.dma("pool", [b_w7], [], lambda e: e.dma_start(w7[:, 0:320], w7up))
        p.dma("pool", [b_w7], [b_w7], lambda e: e.dma_start(w7[:, 320:640], w7dn))
        b_wp = Buf("wp")
        p.dma("pool", [b_wp], [], lambda e: e.dma_start(
            WS[:, 0:8192].rearrange("p (k m) -> p k m", k=8),
            wproj.rearrange("(k p) m -> p k m", p=128)))

        r3regs = [Buf(f"r3reg{i}") for i in range(4)]
        for qt in range(4):
            p.dma("sp", [b_q], [], lambda e, _qt=qt: e.dma_start(
                q_sb[:, 0:1024].rearrange("p (c t) -> p c t", c=2),
                qs_in[:, _qt * 512:(_qt + 1) * 512].rearrange("(c p) t -> p c t", p=128)))
            p.dma("sp", [b_q], [b_q], lambda e, _qt=qt: e.dma_start(
                q_sb[:, 1024:5120].rearrange("p (c t) -> p c t", c=8),
                ql_in[:, _qt * 512:(_qt + 1) * 512].rearrange("(c p) t -> p c t", p=128)))
            p.dma("pool", [b_msk], [], lambda e, _qt=qt: e.dma_start(
                msk.rearrange("p (kb t) -> p kb t", kb=18),
                masksL4[_qt * 18 * 128:(_qt + 1) * 18 * 128, :].rearrange(
                    "(kb p) t -> p kb t", p=128)))
            for pass_i, (nkb, win, kdram, vdram, scl, opair0, ktiles) in enumerate((
                    (NKB_S, 256, ks_pad, vs_pad, 1.0 / math.sqrt(32.0), 0, 2),
                    (NKB_L, 1024, kl_pad, vl_pad, 1.0 / math.sqrt(128.0), 4, 8))):
                mb0 = 0 if pass_i == 0 else NKB_S
                for un in ("u0", "u1"):
                    p.op("dve", [psb[un]], [], lambda e, _u=ps[un]: e.memset(_u[:, :], 1.0))
                for kb in range(nkb):
                    par = kb % 2
                    fst, lst = kb == 0, kb == nkb - 1
                    kc0 = (qt // 2) * 2048 + 1024 + (qt % 2) * 512 - win + kb * 128
                    p.dma("sp", [b_kst[par]], [], lambda e, _kc0=kc0, _par=par,
                          _kd=kdram, _kt=ktiles: e.dma_start(
                        kst[:, _par * 2048:_par * 2048 + _kt * 128].rearrange(
                            "p (c k) -> p c k", c=_kt),
                        _kd[:, _kc0:_kc0 + 128].rearrange("(c p) k -> p c k", p=128)))
                    p.dma("sp", [b_vst[par]], [], lambda e, _kc0=kc0, _par=par,
                          _vd=vdram: e.dma_start(
                        vst[:, _par * 2048:_par * 2048 + 512],
                        _vd[_kc0:_kc0 + 128, :]))
                    for h in range(8):
                        sp_, spb_ = (ps["s0"], psb["s0"]) if h % 2 == 0 else (ps["s1"], psb["s1"])
                        if pass_i == 0:
                            koff = par * 2048 + (h // 4) * 128
                            kbase = (h % 4) * 32
                            p.pe_group([spb_], [b_kst[par], b_q],
                                       lambda te, _s=sp_, _ko=koff, _kb2=kbase, _h=h: te.matmul(
                                           _s[:, :], kst[_kb2:_kb2 + 32, _ko:_ko + 128],
                                           q_sb[_kb2:_kb2 + 32,
                                                (_h // 4) * 512:(_h // 4 + 1) * 512],
                                           start=True, stop=True, tile_position=(_kb2, 0)))
                        else:
                            koff = par * 2048 + h * 128
                            p.pe_group([spb_], [b_kst[par], b_q],
                                       lambda te, _s=sp_, _ko=koff, _h=h: te.matmul(
                                           _s[:, :], kst[:, _ko:_ko + 128],
                                           q_sb[:, 1024 + _h * 512:1024 + (_h + 1) * 512],
                                           start=True, stop=True))
                        slot = h % 4
                        dst = e_sb[:, slot * 512:(slot + 1) * 512]
                        p.op("act", [b_e[slot]], [spb_, b_csh],
                             lambda e, _s=sp_, _d=dst, _scl=scl: e.activation(
                                 _d, _s[:, :], AF.Exp, scale=_scl, bias=csh[:, 0:1]))
                        p.op("dve", [b_e[slot]], [b_e[slot], b_msk],
                             lambda e, _d=dst, _m=mb0 + kb: e.tensor_tensor(
                                 _d, _d, msk[:, _m * 512:(_m + 1) * 512], ALU.mult))
                        pvn = "pv" + str(h // 2)
                        p.pe_group([psb[pvn]], [b_vst[par], b_e[slot]],
                                   lambda te, _h=h, _e=slot, _f=fst, _l=lst, _par=par,
                                   _pv=ps[pvn]: te.matmul(
                                       _pv[(_h % 2) * 64:(_h % 2) * 64 + 64, :],
                                       vst[:, _par * 2048 + _h * 64:_par * 2048 + _h * 64 + 64],
                                       e_sb[:, _e * 512:(_e + 1) * 512],
                                       start=_f, stop=_l, tile_position=(0, (_h % 2) * 64)))
                        ut, ub = (ps["u0"], psb["u0"]) if h < 4 else (ps["u1"], psb["u1"])
                        strip = (h % 4) * 32
                        p.pe_group([ub], [b_e[slot], b_ones],
                                   lambda te, _st=strip, _e=slot, _f=fst, _l=lst, _u=ut: te.matmul(
                                       _u[_st:_st + 1, :], ones_col[:, :],
                                       e_sb[:, _e * 512:(_e + 1) * 512],
                                       start=_f, stop=_l, tile_position=(0, _st)))
                for ui in range(2):
                    p.op("dve", [b_r], [psb["u0" if ui == 0 else "u1"]],
                         lambda e, _u=ps["u0" if ui == 0 else "u1"], _ui=ui: e.reciprocal(
                             r_sb[:, _ui * 512:(_ui + 1) * 512], _u[:, :]))
                p.op("dve", [b_rb], [b_r], lambda e: e.tensor_copy(rb_sb[:, :], r_sb[:, :]))
                for pi in range(4):
                    rt, rtb = ps["s0"], psb["s0"]
                    p.pe_group([rtb], [b_rb, b_sel], lambda te, _s=pi % 2, _u=pi // 2: te.matmul(
                        rt[:, :], sel_sb[:, _s * 128:(_s + 1) * 128],
                        rb_sb[:, _u * 512:(_u + 1) * 512], start=True, stop=True))
                    p.op("act", [b_oraw], [psb[f"pv{pi}"]],
                         lambda e, _p=ps[f"pv{pi}"]: e.activation(
                             o_raw[:, 0:512], _p[:, :], AF.Copy, scale=1.0))
                    p.op("act", [b_oraw], [rtb], lambda e: e.activation(
                        o_raw[:, 512:1024], rt[:, :], AF.Copy, scale=1.0))
                    p.op("dve", [b_o], [b_oraw], lambda e, _pg=opair0 + pi: e.tensor_tensor(
                        o_sb[:, _pg * 512:(_pg + 1) * 512],
                        o_raw[:, 0:512], o_raw[:, 512:1024], ALU.mult))
            p.dma("sp", [b_xst], [], lambda e, _q=qt: e.dma_start(
                x_st[:].rearrange("p (g t) -> p g t", g=8),
                r2[:, _q * 512:(_q + 1) * 512].rearrange("(g p) t -> p g t", p=128)))
            for fo in range(8):
                pt, pb_ = (ps["s1"], psb["s1"]) if fo % 2 == 0 else (ps["pv0"], psb["pv0"])

                def emitc(te, _fo=fo, _pt=pt):
                    last = None
                    for k in range(8):
                        last = te.matmul(_pt[:, :],
                                         WS[:, k * 1024 + _fo * 128:k * 1024 + _fo * 128 + 128],
                                         o_sb[:, k * 512:(k + 1) * 512],
                                         start=(k == 0), stop=(k == 7))
                    return last
                p.pe_group([pb_], [b_wp, b_o], emitc)
                p.op("dve", [b_t], [pb_, b_xst], lambda e, _fo=fo, _pt=pt: e.tensor_tensor(
                    t_ar[:, _fo * 512:(_fo + 1) * 512], _pt[:, :],
                    x_st[:, _fo * 512:(_fo + 1) * 512], ALU.add))
            emit_postln_chunk(
                p, t_ap=t_ar, t_buf=b_t, col=0, scr_bf=scr_bf, scr_bf_buf=b_scrbf,
                rows=rows, rows_buf=b_rows, consts=bcols, consts_buf=b_bcols,
                ones_f32=ones_f32, ones_f32_buf=b_ones32, ones_bf=ones_col,
                ones_bf_buf=b_ones, psS=ps["u0"], psSb=psb["u0"], psQ=ps["u1"],
                psQb=psb["u1"], psA=ps["pv2"], psAb=psb["pv2"], psB=ps["pv3"],
                psBb=psb["pv3"],
                out_f32_ap=lambda g: (u_ar[:, g * 512:(g + 1) * 512], b_u),
                stride=512)
            out_toks.append(p.dma("sp", [r3regs[qt]], [b_u], lambda e, _q=qt: e.dma_start(
                r3_dram[:, _q * 512:(_q + 1) * 512].rearrange("(g p) t -> p g t", p=128),
                u_ar[:].rearrange("p (g t) -> p g t", g=8))))


        p.op("dve", [b_hA, b_hB], [b_q, b_msk, b_kst[0], b_kst[1], b_vst[0], b_vst[1],
                                    b_o, b_e[0], b_e[1], b_e[2], b_e[3]],
             lambda e: e.memset(hA_t[:, 0:8], 0.0))

        b_m = Buf("mload")
        for which, (msrc, Edst, b_E) in enumerate(((mupT, ET, b_ET), (mdnT, EDT, b_EDT))):
            exdeps = []
            for t in range(32):
                p.dma("pool", [b_m] + exdeps, [], lambda e, _t=t, _m=msrc: e.dma_start(
                    mtmp[:, 0:128], _m[_t * 128:(_t + 1) * 128, :]))
                exdeps = []
                p.op("dve", [b_E], [b_m], lambda e, _t=t, _E=Edst: e.tensor_copy(
                    _E[:, _t * 128:(_t + 1) * 128], mtmp[:, 0:128]))
                for s in range(1, 7):
                    p.dma("pool", [b_m], [b_m], lambda e, _t=t, _s=s, _m=msrc: e.dma_start(
                        mtmp[:, 0:128], _m[(_s * 32 + _t) * 128:(_s * 32 + _t + 1) * 128, :]))
                    p.pe_group([psb["s0"]], [b_m, b_E], lambda te, _t=t, _E=Edst: te.matmul(
                        ps["s0"][:, 0:128], mtmp[:, 0:128], _E[:, _t * 128:(_t + 1) * 128],
                        start=True, stop=True))
                    p.op("act", [b_E], [psb["s0"]], lambda e, _t=t, _E=Edst: e.activation(
                        _E[:, _t * 128:(_t + 1) * 128], ps["s0"][:, 0:128], AF.Copy, scale=1.0))
                p.pe_group([psb["s1"]], [b_E, b_id], lambda te, _t=t, _E=Edst: te.matmul(
                    ps["s1"][:, 0:128], _E[:, _t * 128:(_t + 1) * 128], id_sb[:, :],
                    start=True, stop=True))
                p.op("act", [b_E], [psb["s1"]], lambda e, _t=t, _E=Edst: e.activation(
                    _E[:, _t * 128:(_t + 1) * 128], ps["s1"][:, 0:128], AF.Copy, scale=1.0))

        def bfly7(src_ap, src_buf, dst_ap, dst_buf, w_off, stage_i):
            st = 2 ** stage_i
            for t in range(32):
                pt_ = t ^ st
                wc = w_off + (stage_i * 32 + t) * 2
                qd = "dve"
                sl = t % 4
                p.op(qd, [b_ev[sl]], [src_buf, b_w7], lambda e, _t=t, _wc=wc, _sl=sl: e.tensor_scalar(
                    ev[:, _sl * 512:(_sl + 1) * 512],
                    src_ap[:, _t * 512:(_t + 1) * 512], w7[:, _wc:_wc + 1], None, ALU.mult))
                p.op(qd, [dst_buf], [src_buf, b_w7, b_ev[sl]],
                     lambda e, _t=t, _pt=pt_, _wc=wc, _sl=sl: e.scalar_tensor_tensor(
                         dst_ap[:, _t * 512:(_t + 1) * 512],
                         src_ap[:, _pt * 512:(_pt + 1) * 512], w7[:, _wc + 1:_wc + 2],
                         ev[:, _sl * 512:(_sl + 1) * 512], ALU.mult, ALU.add))

        b_wld = Buf("wld")
        b_wcn = Buf("wcn")
        for nt in range(4):
            p.dma("pool", [b_r3bf], [r3regs[nt]], lambda e, _nt=nt: e.dma_start(
                r3bf[:].rearrange("p (k t) -> p k t", k=8),
                r3_dram[:, _nt * 512:(_nt + 1) * 512].rearrange("(k p) t -> p k t", p=128)))
            for mg in range(4):
                deps = [b_wp] if (nt == 0 and mg == 0) else []
                p.dma("pool", [b_wld] + deps, [], lambda e, _mg=mg: e.dma_start(
                    WS[:, 0:8192].rearrange("p (k m) -> p k m", k=8),
                    wexp[:, _mg * 1024:(_mg + 1) * 1024].rearrange("(k p) m -> p k m", p=128)))
                for mm in range(8):
                    m = mg * 8 + mm
                    pt, pb_ = (ps["s0"], psb["s0"]) if mm % 2 == 0 else (ps["s1"], psb["s1"])

                    def emite(te, _mm=mm, _pt=pt):
                        last = None
                        for k in range(8):
                            last = te.matmul(_pt[:, :],
                                             WS[:, k * 1024 + _mm * 128:k * 1024 + _mm * 128 + 128],
                                             r3bf[:, k * 512:(k + 1) * 512],
                                             start=(k == 0), stop=(k == 7))
                        return last
                    p.pe_group([pb_], [b_wld, b_r3bf], emite)
                    p.op("dve", [b_hA], [pb_, b_bcols], lambda e, _m=m, _pt=pt: e.tensor_scalar(
                        hA_t[:, _m * 512:(_m + 1) * 512], _pt[:, :],
                        bcols[:, 20 + _m:21 + _m], None, ALU.add))
            for t in range(32):
                pt, pb_ = (ps["s0"], psb["s0"]) if t % 2 == 0 else (ps["s1"], psb["s1"])
                p.pe_group([pb_], [b_ET, b_hA], lambda te, _t=t, _pt=pt: te.matmul(
                    _pt[:, :], ET[:, _t * 128:(_t + 1) * 128],
                    hA_t[:, _t * 512:(_t + 1) * 512], start=True, stop=True))
                p.op("act", [b_hA], [pb_], lambda e, _t=t, _pt=pt: e.activation(
                    hA_t[:, _t * 512:(_t + 1) * 512], _pt[:, :], AF.Copy, scale=1.0))
            bufs = [(hA_t, b_hA), (hB_t, b_hB)]
            cur = 0
            for si in range(5):
                bfly7(bufs[cur][0], bufs[cur][1], bufs[1 - cur][0], bufs[1 - cur][1], 0, si)
                cur = 1 - cur
            src_ap, src_b = bufs[cur]
            dst_ap, dst_b = bufs[1 - cur]
            for t in range(32):
                p.op("act", [dst_b], [src_b, b_bcols], lambda e, _t=t: e.activation(
                    dst_ap[:, _t * 512:(_t + 1) * 512], src_ap[:, _t * 512:(_t + 1) * 512],
                    AF.Gelu, bias=bcols[:, 52 + _t:53 + _t]))
            cur = 1 - cur
            src_ap, src_b = bufs[cur]
            for t in range(32):
                pt, pb_ = (ps["s0"], psb["s0"]) if t % 2 == 0 else (ps["s1"], psb["s1"])
                p.pe_group([pb_], [b_EDT, src_b], lambda te, _t=t, _pt=pt: te.matmul(
                    _pt[:, :], EDT[:, _t * 128:(_t + 1) * 128],
                    src_ap[:, _t * 512:(_t + 1) * 512], start=True, stop=True))
                p.op("act", [src_b], [pb_], lambda e, _t=t, _pt=pt: e.activation(
                    src_ap[:, _t * 512:(_t + 1) * 512], _pt[:, :], AF.Copy, scale=1.0))
            for si in range(5):
                bfly7(bufs[cur][0], bufs[cur][1], bufs[1 - cur][0], bufs[1 - cur][1], 320, si)
                cur = 1 - cur
            h_fin, b_hfin = bufs[cur]
            bank_names = ["pv0", "pv1", "pv2", "pv3", "s0", "s1", "u0", "u1"]
            for kg in range(4):
                p.dma("pool", [b_wcn], [], lambda e, _kg=kg: e.dma_start(
                    WS[:, 8192:16384].rearrange("p (k m) -> p k m", k=8),
                    wcon[_kg * 1024:(_kg + 1) * 1024, :].rearrange("(k p) m -> p k m", p=128)))
                for fo in range(8):
                    def emitw(te, _kg=kg, _fo=fo):
                        last = None
                        for k in range(8):
                            last = te.matmul(ps[bank_names[_fo]][:, :],
                                             WS[:, 8192 + k * 1024 + _fo * 128:
                                                8192 + k * 1024 + _fo * 128 + 128],
                                             h_fin[:, (_kg * 8 + k) * 512:(_kg * 8 + k + 1) * 512],
                                             start=(_kg == 0 and k == 0),
                                             stop=(_kg == 3 and k == 7))
                        return last
                    p.pe_group([psb[bank_names[fo]]], [b_wcn, b_hfin], emitw)
            p.dma("sp", [b_xst], [r3regs[nt]], lambda e, _q=nt: e.dma_start(
                x_st[:].rearrange("p (g t) -> p g t", g=8),
                r3_dram[:, _q * 512:(_q + 1) * 512].rearrange("(g p) t -> p g t", p=128)))
            for fo in range(8):
                p.op("dve", [b_t], [psb[bank_names[fo]], b_bcols], lambda e, _fo=fo: e.tensor_scalar(
                    t_ar[:, _fo * 512:(_fo + 1) * 512], ps[bank_names[_fo]][:, :],
                    bcols[:, 84 + _fo:85 + _fo], bcols[:, 92 + _fo:93 + _fo],
                    ALU.add, ALU.add))
                p.op("dve", [b_u], [b_t, b_xst], lambda e, _fo=fo: e.tensor_tensor(
                    u_ar[:, _fo * 512:(_fo + 1) * 512],
                    t_ar[:, _fo * 512:(_fo + 1) * 512],
                    x_st[:, _fo * 512:(_fo + 1) * 512], ALU.add))
            emit_postln_chunk(
                p, t_ap=u_ar, t_buf=b_u, col=0, scr_bf=scr_bf, scr_bf_buf=b_scrbf,
                rows=rows, rows_buf=b_rows, consts=bcols, consts_buf=b_bcols,
                ones_f32=ones_f32, ones_f32_buf=b_ones32, ones_bf=ones_col,
                ones_bf_buf=b_ones, psS=ps["pv2"], psSb=psb["pv2"], psQ=ps["pv3"],
                psQb=psb["pv3"], psA=ps["s0"], psAb=psb["s0"], psB=ps["s1"],
                psBb=psb["s1"],
                out_f32_ap=lambda g: (t_ar[:, g * 512:(g + 1) * 512], b_t),
                stride=512)
            out_toks.append(p.dma("sp", [], [b_t], lambda e, _q=nt: e.dma_start(
                out_f[:, _q * 512:(_q + 1) * 512].rearrange("(g p) t -> p g t", p=128),
                t_ar[:].rearrange("p (g t) -> p g t", g=8))))
        p.finish(out_toks)
    return nc


def host_l4_masks(core):
    _, lo, hi = core_chunks(core)
    qts = [2 * lo, 2 * lo + 1, 2 * hi, 2 * hi + 1]
    i = np.arange(128)[:, None]
    j = np.arange(512)[None, :]
    out = np.zeros((4, 18, 128, 512), np.float32)
    for qi, qt in enumerate(qts):
        q0 = qt * 512
        for pi, (win, nkb, kb0i) in enumerate(((256, NKB_S, 0), (1024, NKB_L, NKB_S))):
            for kb in range(nkb):
                k0 = q0 - win + kb * 128
                qpos = q0 + j
                kpos = k0 + i
                valid = (kpos >= 0) & (qpos - kpos >= 0) & (qpos - kpos < win)
                out[qi, kb0i + kb] = valid.astype(np.float32)
    return out.reshape(4 * 18 * 128, 512)


def host_sel2():
    sel = np.zeros((128, 2 * 128), np.float32)
    sel[0, 0:64] = 1.0
    sel[32, 64:128] = 1.0
    sel[64, 128:192] = 1.0
    sel[96, 192:256] = 1.0
    return sel


def host_butterfly_mats(w):
    w = np.asarray(w, np.float32)
    mT = np.zeros((7, 32, 128, 128), np.float32)
    for s in range(7):
        sz = 2 ** s
        M = np.zeros((4096, 4096), np.float32)
        for t in range(32):
            Mt = np.zeros((128, 128), np.float32)
            for p in range(128):
                f = t * 128 + p
                kpair = (f // (2 * sz)) * sz + (f % sz)
                if (f // sz) % 2 == 0:
                    a, b2 = p, p + sz
                    Mt[p, a] = w[s, kpair, 0, 0]
                    Mt[p, b2] = w[s, kpair, 0, 1]
                else:
                    a, b2 = p - sz, p
                    Mt[p, a] = w[s, kpair, 1, 0]
                    Mt[p, b2] = w[s, kpair, 1, 1]
            mT[s, t] = Mt if s == 0 else Mt.T
    w7 = np.zeros((5, 32, 2, 128), np.float32)
    for si in range(5):
        s = si + 7
        sz = 2 ** s
        st = 2 ** si
        for t in range(32):
            for p in range(128):
                f = t * 128 + p
                kpair = (f // (2 * sz)) * sz + (f % sz)
                if (f // sz) % 2 == 0:
                    w7[si, t, 0, p] = w[s, kpair, 0, 0]
                    w7[si, t, 1, p] = w[s, kpair, 0, 1]
                else:
                    w7[si, t, 0, p] = w[s, kpair, 1, 1]
                    w7[si, t, 1, p] = w[s, kpair, 1, 0]
    return (np.ascontiguousarray(mT.reshape(7 * 32 * 128, 128)),
            np.ascontiguousarray(w7.transpose(3, 0, 1, 2).reshape(128, 5 * 32 * 2)))


def host_l4_inputs(inputs, core, res_3b):
    b = core // 4
    cores = list(range(b * 4, b * 4 + 4))
    ks = assemble_full_fm(res_3b, "ks_out", cores)
    kl = assemble_full_fm(res_3b, "kl_out", cores)
    vs = assemble_full_tm(res_3b, "vs_out", cores)
    vl = assemble_full_tm(res_3b, "vl_out", cores)
    _, lo, hi = core_chunks(core)

    def pad(a):
        out = np.zeros((a.shape[0], 2 * 2048), a.dtype)
        for c2, ch in enumerate((lo, hi)):
            g0 = ch * 1024 - 1024
            src0 = max(g0, 0)
            out[:, c2 * 2048 + (src0 - g0):c2 * 2048 + 2048] = a[:, src0:ch * 1024 + 1024]
        return out

    def padr(a):
        out = np.zeros((2 * 2048, a.shape[1]), a.dtype)
        for c2, ch in enumerate((lo, hi)):
            g0 = ch * 1024 - 1024
            src0 = max(g0, 0)
            out[c2 * 2048 + (src0 - g0):c2 * 2048 + 2048, :] = a[src0:ch * 1024 + 1024, :]
        return out
    mupT, w7up = host_butterfly_mats(inputs["L1_ffn_up_weights"])
    mdnT, w7dn = host_butterfly_mats(inputs["L1_ffn_down_weights"])
    bias = np.concatenate([
        col_tiles(np.asarray(inputs["L1_ffn_b_exp"])),
        col_tiles(np.asarray(inputs["L1_ffn_up_bias"])),
        col_tiles(np.asarray(inputs["L1_ffn_b_con"])),
        col_tiles(np.asarray(inputs["L1_ffn_down_bias"]))], 1)
    return {
        "qs_in": res_3b[core]["qs_out"], "ql_in": res_3b[core]["ql_out"],
        "ks_pad": pad(ks), "vs_pad": padr(vs), "kl_pad": pad(kl), "vl_pad": padr(vl),
        "masksL4": host_l4_masks(core),
        "r2": res_3b[core]["r2_out"],
        "wproj": np.ascontiguousarray(np.asarray(inputs["L1_attn_Wproj"], np.float32)),
        "sel2": host_sel2(),
        "wexp": np.ascontiguousarray(np.asarray(inputs["L1_ffn_W_exp"], np.float32)),
        "mupT": mupT, "mdnT": mdnT, "w7up": w7up, "w7dn": w7dn,
        "bias_in": bias,
        "wcon": np.ascontiguousarray(np.asarray(inputs["L1_ffn_W_con"], np.float32)),
        "idbf": np.eye(128, dtype=np.float32),
    }



_NC_CACHE = {}


def _get_nc(name, builder):
    if name not in _NC_CACHE:
        _NC_CACHE[name] = builder()
    return _NC_CACHE[name]


def kernel(**inputs):
    cores = list(range(NCORE))
    nc1 = _get_nc("l1", build_l0a_qkv)
    res1 = run_bass_kernel_spmd(
        nc1, [host_l0a_qkv_inputs(inputs, c) for c in cores], core_ids=cores).results

    nc_ap = _get_nc("ap", build_attn_partial)
    full = {}
    for b in range(2):
        cs = cores[b * 4:b * 4 + 4]
        full[b] = dict(q=assemble_full_fm(res1, "q_out", cs),
                       k=assemble_full_fm(res1, "k_out", cs),
                       v=assemble_full_tm(res1, "v_out", cs))
    res2a = run_bass_kernel_spmd(nc_ap, [host_attn_partial_inputs(
        c, full[c // 4]["q"], full[c // 4]["k"], full[c // 4]["v"]) for c in cores],
        core_ids=cores).results

    nc2b = _get_nc("l2b", build_l2b)
    res2b = run_bass_kernel_spmd(
        nc2b, [host_l2b_inputs(inputs, c, res2a) for c in cores], core_ids=cores).results

    mla = {}
    for b in range(2):
        cs = cores[b * 4:b * 4 + 4]
        qq = assemble_full_fm(res2b, "qq_out", cs)
        kg = assemble_full_fm(res2b, "k_out2", cs)
        krg = assemble_full_fm(res2b, "kr_out", cs)
        kk = np.zeros((1024, S), kg.dtype)
        for h in range(16):
            kk[64 * h:64 * h + 32] = kg[32 * h:32 * h + 32]
            kk[64 * h + 32:64 * h + 64] = krg[32 * h:32 * h + 32]
        mla[b] = dict(q=qq, k=kk, v=assemble_full_tm(res2b, "vm_out", cs))
    res3a = run_bass_kernel_spmd(nc_ap, [host_attn_partial_inputs(
        c, mla[c // 4]["q"], mla[c // 4]["k"], mla[c // 4]["v"]) for c in cores],
        core_ids=cores).results

    nc3b = _get_nc("l3b", build_l3b)
    res3b = run_bass_kernel_spmd(
        nc3b, [host_l3b_inputs(inputs, c, res3a, res2b) for c in cores],
        core_ids=cores).results

    nc4 = _get_nc("l4", build_l4)
    res4 = run_bass_kernel_spmd(
        nc4, [host_l4_inputs(inputs, c, res3b) for c in cores], core_ids=cores).results

    out = np.zeros((B, S, D), np.float32)
    for c in cores:
        b, lo, hi = core_chunks(c)
        of = np.asarray(res4[c]["out_f"])
        out[b, lo * CH:(lo + 1) * CH] = of[:, 0:CH].T
        out[b, hi * CH:(hi + 1) * CH] = of[:, CH:2 * CH].T
    return out


# version 02
# speedup vs baseline: 1.1306x; 1.1306x over previous
import math
from contextlib import ExitStack

import numpy as np
import ml_dtypes
import concourse.bass as bass
import concourse.mybir as mybir
from concourse.bass_utils import run_bass_kernel_spmd

F32 = mybir.dt.float32
BF16 = mybir.dt.bfloat16
AF = mybir.ActivationFunctionType
ALU = mybir.AluOpType
AX = mybir.AxisListType

B, S, D = 2, 8192, 1024
NCORE = 8
CH = 1024
T = 2 * CH
NEG = -1.0e9
C_SHIFT = 10.0
EPS_LN = 1e-5
EPS_RMS = 1e-5

BF = ml_dtypes.bfloat16


def core_chunks(i):
    b, j = i // 4, i % 4
    return b, j, 7 - j



def fm(x):
    return np.ascontiguousarray(np.asarray(x).T)


def col_tiles(v):
    v = np.asarray(v, np.float32)
    G = v.shape[0] // 128
    return np.ascontiguousarray(v.reshape(G, 128).T)


def causal_mask_tiles():
    i = np.arange(128)[:, None]
    j = np.arange(512)[None, :]
    out = np.zeros((4, 128, 512), np.float32)
    for di, d in enumerate((0, 128, 256, 384)):
        out[di] = (j >= i + d).astype(np.float32)
    return out


def window_old_mask():
    i = np.arange(128)[:, None]
    j = np.arange(128)[None, :]
    return (j < i).astype(np.float32)


def rope_tables(positions):
    HH = 32
    inv_freq = 1.0 / (10000.0 ** (np.arange(0, HH, 2, dtype=np.float64) / HH))
    t = np.asarray(positions, np.float64)
    ang = t[None, :] * inv_freq[:, None]
    cos = np.cos(ang)
    sin = np.sin(ang)
    cos32 = np.concatenate([cos, cos], 0)
    sin32 = np.concatenate([-sin, sin], 0)
    return cos32.astype(np.float32), sin32.astype(np.float32)


def perm32():
    P = np.zeros((32, 32), np.float32)
    for d in range(16):
        P[d, d + 16] = 1.0
        P[d + 16, d] = 1.0
    return P



class Buf:
    __slots__ = ("name", "w", "r")

    def __init__(self, name):
        self.name = name
        self.w = []
        self.r = []


class Prog:
    ENGS = ("pe", "act", "dve", "pool", "sp")
    DMAQ = {"sp": "sp_d", "act": "act_d", "pool": "gp_d"}

    def __init__(self, nc, ctx):
        self.nc = nc
        self.ctx = ctx
        self.eng = {"pe": nc.tensor, "act": nc.scalar, "dve": nc.vector,
                    "pool": nc.gpsimd, "sp": nc.sync}
        self.sems = {}
        self.cnt = {}
        for name in ("pe", "act", "dve", "pool", "sp", "sp_d", "act_d", "gp_d"):
            self.sems[name] = ctx.enter_context(nc.semaphore("S_" + name))
            self.cnt[name] = 0
        self.waited = {}
        self._tiles = {}

    def sb(self, name, shape, dtype):
        t = self.ctx.enter_context(self.nc.sbuf_tensor(name, list(shape), dtype))
        b = Buf(name)
        self._tiles[name] = (t, b)
        return t, b

    def ps(self, name, shape, dtype=F32):
        t = self.ctx.enter_context(self.nc.psum_tensor(name, list(shape), dtype))
        b = Buf(name)
        self._tiles[name] = (t, b)
        return t, b

    def _wait(self, q, tokens):
        for (sname, cnt) in tokens:
            key = (q, sname)
            if self.waited.get(key, 0) < cnt:
                self.eng[q].wait_ge(self.sems[sname], cnt)
                self.waited[key] = cnt

    def _tok(self, sname, inst, step):
        inst.then_inc(self.sems[sname], step)
        self.cnt[sname] += step
        return (sname, self.cnt[sname])

    def op(self, q, outs, ins, emit):
        for b in ins:
            self._wait(q, b.w)
        for b in outs:
            self._wait(q, b.w)
            self._wait(q, b.r)
        inst = emit(self.eng[q])
        tok = self._tok(q, inst, 1)
        for b in ins:
            b.r.append(tok)
        for b in outs:
            b.w = [tok]
            b.r = []
        return tok

    def dma(self, q, outs, ins, emit):
        for b in ins:
            self._wait(q, b.w)
        for b in outs:
            self._wait(q, b.w)
            self._wait(q, b.r)
        inst = emit(self.eng[q])
        tok = self._tok(self.DMAQ[q], inst, 16)
        for b in ins:
            b.r.append(tok)
        for b in outs:
            b.w = [tok]
            b.r = []
        return tok

    def pe_group(self, outs, ins, emit):
        for b in ins:
            self._wait("pe", b.w)
        for b in outs:
            self._wait("pe", b.w)
            self._wait("pe", b.r)
        last = emit(self.nc.tensor)
        tok = self._tok("pe", last, 1)
        for b in ins:
            b.r.append(tok)
        for b in outs:
            b.w = [tok]
            b.r = []
        return tok

    def barrier_token(self):
        return [(n, c) for n, c in self.cnt.items() if c > 0]

    def finish(self, out_dma_tokens):
        for (sname, cnt) in out_dma_tokens:
            self._wait("sp", [(sname, cnt)])



def mm_kchain(p, psum_ap, lhsT_aps, rhs_aps, in_bufs, out_buf, tile_position=None):
    n = len(lhsT_aps)
    assert n == len(rhs_aps) and n >= 1

    def emit(te):
        last = None
        for i in range(n):
            last = te.matmul(psum_ap, lhsT_aps[i], rhs_aps[i],
                             start=(i == 0), stop=(i == n - 1),
                             tile_position=tile_position)
        return last
    return p.pe_group([out_buf], in_bufs, emit)



def build_l0a_qkv():
    nc = bass.Bass("TRN2", target_bir_lowering=False)
    xf = nc.dram_tensor("xf", [D, T], F32, kind="ExternalInput").ap()
    wqkv = nc.dram_tensor("wqkv", [D, 3 * D], F32, kind="ExternalInput").ap()
    bqkv = nc.dram_tensor("bqkv", [128, 16], F32, kind="ExternalInput").ap()
    bv = nc.dram_tensor("bv", [1, D], F32, kind="ExternalInput").ap()
    q_out = nc.dram_tensor("q_out", [D, T], BF16, kind="ExternalOutput").ap()
    k_out = nc.dram_tensor("k_out", [D, T], BF16, kind="ExternalOutput").ap()
    v_out = nc.dram_tensor("v_out", [T, D], BF16, kind="ExternalOutput").ap()

    with ExitStack() as ctx:
        p = Prog(nc, ctx)
        xb, b_xb = p.sb("xb", [128, 8 * T], BF16)
        wsb, b_wsb = p.sb("wsb", [128, 8 * 3 * D], BF16)
        bq, b_bq = p.sb("bq", [128, 16], F32)
        bvr, b_bvr = p.sb("bvr", [1, D], F32)
        bvb, b_bvb = p.sb("bvb", [1, D], BF16)
        ones_r, b_ones = p.sb("ones_r", [1, 128], BF16)
        ev, b_ev = p.sb("ev", [128, 4 * 512], BF16)
        ev_bufs = [Buf(f"ev{i}") for i in range(4)]
        psA, b_psA = p.ps("psA", [128, 512])
        psB, b_psB = p.ps("psB", [128, 512])
        ps_bufs = [b_psA, b_psB]
        ps_t = [psA, psB]

        p.dma("pool", [b_xb], [], lambda e: e.dma_start(
            xb[:].rearrange("p (c t) -> p c t", c=8),
            xf.rearrange("(c p) t -> p c t", p=128)))
        p.dma("pool", [b_wsb], [], lambda e: e.dma_start(
            wsb[:].rearrange("p (c n) -> p c n", c=8),
            wqkv.rearrange("(c p) n -> p c n", p=128)))
        p.dma("sp", [b_bq], [], lambda e: e.dma_start(bq[:, :], bqkv))
        p.dma("sp", [b_bvr], [], lambda e: e.dma_start(bvr[:, :], bv))
        p.op("dve", [b_bvb], [b_bvr], lambda e: e.tensor_copy(bvb[:, :], bvr[:, :]))
        p.op("dve", [b_ones], [], lambda e: e.memset(ones_r[:, :], 1.0))

        out_toks = []
        for sec, (base_col, bias_base, dst) in enumerate(
                ((0, 0, q_out), (D, 8, k_out))):
            for m in range(8):
                for ntile in range(4):
                    pb = ps_bufs[(m * 4 + ntile) % 2]
                    pt = ps_t[(m * 4 + ntile) % 2]
                    mm_kchain(
                        p, pt[:, :],
                        [wsb[:, k * 3 * D + base_col + m * 128:
                             k * 3 * D + base_col + m * 128 + 128] for k in range(8)],
                        [xb[:, k * T + ntile * 512: k * T + ntile * 512 + 512]
                         for k in range(8)],
                        [b_wsb, b_xb], pb)
                    eb = ev_bufs[(m * 4 + ntile) % 4]
                    esl = ev[:, ((m * 4 + ntile) % 4) * 512:((m * 4 + ntile) % 4) * 512 + 512]
                    p.op("dve", [eb], [pb, b_bq], lambda e, _pt=pt, _esl=esl, _m=m, _bb=bias_base: e.tensor_scalar(
                        _esl, _pt[:, :], bq[:, _bb + _m:_bb + _m + 1], None, ALU.add))
                    out_toks.append(p.dma("sp", [], [eb], lambda e, _esl=esl, _dst=dst, _m=m, _n=ntile: e.dma_start(
                        _dst[_m * 128:(_m + 1) * 128, _n * 512:(_n + 1) * 512], _esl)))

        for tt in range(16):
            for nv in range(2):
                idx = tt * 2 + nv
                pb = ps_bufs[idx % 2]
                pt = ps_t[idx % 2]

                def emit(te, _tt=tt, _nv=nv, _pt=pt):
                    for k in range(8):
                        te.matmul(_pt[:, :],
                                  xb[:, k * T + _tt * 128: k * T + _tt * 128 + 128],
                                  wsb[:, k * 3 * D + 2 * D + _nv * 512:
                                      k * 3 * D + 2 * D + _nv * 512 + 512],
                                  start=(k == 0), stop=False)
                    return te.matmul(_pt[:, :], ones_r[:, :],
                                     bvb[:, _nv * 512:_nv * 512 + 512],
                                     start=False, stop=True)
                p.pe_group([pb], [b_xb, b_wsb, b_ones, b_bvb], emit)
                eb = ev_bufs[idx % 4]
                esl = ev[:, (idx % 4) * 512:(idx % 4) * 512 + 512]
                p.op("act", [eb], [pb], lambda e, _pt=pt, _esl=esl: e.activation(
                    _esl, _pt[:, :], AF.Copy, scale=1.0))
                out_toks.append(p.dma("sp", [], [eb], lambda e, _esl=esl, _tt=tt, _nv=nv: e.dma_start(
                    v_out[_tt * 128:(_tt + 1) * 128, _nv * 512:(_nv + 1) * 512], _esl)))

        p.finish(out_toks)
    return nc


def host_l0a_qkv_inputs(inputs, core):
    b, lo, hi = core_chunks(core)
    x = np.asarray(inputs["L0_attn_x"][b])
    xc = np.concatenate([x[lo * CH:(lo + 1) * CH], x[hi * CH:(hi + 1) * CH]], 0)
    bqkv = np.asarray(inputs["L0_attn_b_qkv"], np.float32)
    return {
        "xf": fm(xc),
        "wqkv": np.ascontiguousarray(np.asarray(inputs["L0_attn_W_qkv"], np.float32)),
        "bqkv": np.concatenate([col_tiles(bqkv[0:D]), col_tiles(bqkv[D:2 * D])], 1),
        "bv": np.ascontiguousarray(bqkv[2 * D:3 * D].reshape(1, D)),
    }



ATTN_EXTENTS = [1, 2, 3, 4, 1, 2, 3, 4] + [4] * 12
N_SLOTS = len(ATTN_EXTENTS)


def attn_slot_map(j):
    spanA, spanB = (0, 3) if j < 2 else (1, 2)
    slots = []
    for e in range(1, 5):
        slots.append((spanA, spanA * 4 + e - 1))
    for e in range(1, 5):
        slots.append((spanB, spanB * 4 + e - 1))
    fulls = [(spanA, qt) for qt in range(spanA * 4 + 4, 16)]
    fulls += [(spanB, qt) for qt in range(spanB * 4 + 4, 16)]
    assert len(fulls) == 12
    slots += fulls
    return slots


def build_attn_partial():
    nc = bass.Bass("TRN2", target_bir_lowering=False)
    dr = lambda n, s, d, k: nc.dram_tensor(n, s, d, kind=k).ap()
    qp = dr("qp", [512, N_SLOTS * 512], BF16, "ExternalInput")
    kk_in = dr("kk_in", [512, N_SLOTS * 2048], BF16, "ExternalInput")
    v_in = dr("v_in", [N_SLOTS * 2048, 512], BF16, "ExternalInput")
    masks4 = dr("masks4", [128, 4 * 512], F32, "ExternalInput")
    o_part = dr("o_part", [512, N_SLOTS * 512], BF16, "ExternalOutput")
    sums_part = dr("sums_part", [8, N_SLOTS * 512], F32, "ExternalOutput")

    out_toks = []
    with ExitStack() as ctx:
        p = Prog(nc, ctx)
        q_sb, b_q = p.sb("q_sb", [128, 4 * 512], BF16)
        kk_st, b_k0 = p.sb("kk_st", [128, 2 * 2048], BF16)
        v_st, b_v0 = p.sb("v_st", [128, 2 * 2048], BF16)
        b_kst = [b_k0, Buf("kst1")]
        b_vst = [b_v0, Buf("vst1")]
        e_sb, b_e0 = p.sb("e_sb", [128, 4 * 512], BF16)
        b_e = [b_e0, Buf("e1"), Buf("e2"), Buf("e3")]
        masks_sb, b_masks = p.sb("masks_sb", [128, 4 * 512], BF16)
        ones_col, b_ones = p.sb("ones_col", [128, 1], BF16)
        oev, b_oev = p.sb("oev", [128, 4 * 512], BF16)
        suev, b_suev = p.sb("suev", [128, 2 * 512], F32)
        suev_t = suev
        ps, psb = {}, {}
        for n in ("pv0", "pv1", "pv2", "pv3", "s0", "s1", "u0", "u1"):
            ps[n], psb[n] = p.ps("ps_" + n, [128, 512])

        csh, b_csh = p.sb("csh", [128, 1], F32)
        p.dma("pool", [b_masks], [], lambda e: e.dma_start(masks_sb[:, :], masks4))
        p.op("dve", [b_ones], [], lambda e: e.memset(ones_col[:, :], 1.0))
        p.op("dve", [b_csh], [], lambda e: e.memset(csh[:, :], -C_SHIFT))

        kbctr = 0
        for u in range(N_SLOTS):
            ext = ATTN_EXTENTS[u]
            ragged = u < 8
            p.dma("sp", [b_q], [], lambda e, _u=u: e.dma_start(
                q_sb[:].rearrange("p (c t) -> p c t", c=4),
                qp[:, _u * 512:(_u + 1) * 512].rearrange("(c p) t -> p c t", p=128)))
            for kb in range(ext):
                par = kbctr % 2
                kbctr += 1
                fst = kb == 0
                lst = kb == ext - 1
                diag = ragged and lst
                p.dma("sp", [b_kst[par]], [], lambda e, _u=u, _k=kb, _par=par: e.dma_start(
                    kk_st[:, _par * 2048:(_par + 1) * 2048].rearrange("p (c k) -> p c k", c=4),
                    kk_in[:, _u * 2048 + _k * 512:_u * 2048 + (_k + 1) * 512].rearrange(
                        "(c p) k -> p c k", p=128)))
                p.dma("sp", [b_vst[par]], [], lambda e, _u=u, _k=kb, _par=par: e.dma_start(
                    v_st[:, _par * 2048:(_par + 1) * 2048].rearrange("p (c d) -> p c d", c=4),
                    v_in[_u * 2048 + _k * 512:_u * 2048 + (_k + 1) * 512, :].rearrange(
                        "(c p) d -> p c d", p=128)))
                def emit_pv(i):
                    pi, kc, hh = i // 8, (i // 2) % 4, i % 2
                    slot = i % 4
                    f_ = fst and kc == 0
                    l_ = lst and kc == 3
                    pvt, pvb = ps[f"pv{pi}"], psb[f"pv{pi}"]
                    vc_off = par * 2048 + kc * 512
                    dv = (pi * 2 + hh) * 64
                    p.pe_group([pvb], [b_vst[par], b_e[slot]],
                               lambda te, _dv=dv, _e=slot, _f=f_, _l=l_, _hh=hh,
                               _vo=vc_off, _pv=pvt: te.matmul(
                                   _pv[_hh * 64:(_hh + 1) * 64, :],
                                   v_st[:, _vo + _dv:_vo + _dv + 64],
                                   e_sb[:, _e * 512:(_e + 1) * 512],
                                   start=_f, stop=_l, tile_position=(0, _hh * 64)))
                    ut, ub = (ps["u0"], psb["u0"]) if pi < 2 else (ps["u1"], psb["u1"])
                    strip = ((pi % 2) * 2 + hh) * 32
                    p.pe_group([ub], [b_e[slot], b_ones],
                               lambda te, _st=strip, _e=slot, _f=f_, _l=l_, _u=ut: te.matmul(
                                   _u[_st:_st + 1, :], ones_col[:, :],
                                   e_sb[:, _e * 512:(_e + 1) * 512],
                                   start=_f, stop=_l, tile_position=(0, _st)))

                for i in range(32):
                    pi, kc, hh = i // 8, (i // 2) % 4, i % 2
                    slot = i % 4
                    kk_off = par * 2048 + pi * 512 + kc * 128
                    sp_, spb_ = (ps["s0"], psb["s0"]) if i % 2 == 0 else (ps["s1"], psb["s1"])
                    p.pe_group([spb_], [b_kst[par], b_q],
                               lambda te, _s=sp_, _o=kk_off, _h=hh, _pi=pi: te.matmul(
                                   _s[:, :], kk_st[_h * 64:(_h + 1) * 64, _o:_o + 128],
                                   q_sb[_h * 64:(_h + 1) * 64, _pi * 512:(_pi + 1) * 512],
                                   start=True, stop=True, tile_position=(_h * 64, 0)))
                    dst = e_sb[:, slot * 512:(slot + 1) * 512]
                    p.op("act", [b_e[slot]], [spb_, b_csh],
                         lambda e, _s=sp_, _d=dst: e.activation(
                             _d, _s[:, :], AF.Exp, scale=0.125, bias=csh[:, 0:1]))
                    if diag:
                        p.op("dve", [b_e[slot]], [b_e[slot], b_masks],
                             lambda e, _d=dst, _m=kc: e.tensor_tensor(
                                 _d, _d, masks_sb[:, _m * 512:(_m + 1) * 512], ALU.mult))
                    if i >= 2:
                        emit_pv(i - 2)
                emit_pv(30)
                emit_pv(31)
            for pi in range(4):
                p.op("act", [b_oev], [psb[f"pv{pi}"]],
                     lambda e, _p=ps[f"pv{pi}"], _pi=pi: e.activation(
                         oev[:, _pi * 512:(_pi + 1) * 512], _p[:, :], AF.Copy, scale=1.0))
            out_toks.append(p.dma("sp", [], [b_oev], lambda e, _u=u: e.dma_start(
                o_part[:, _u * 512:(_u + 1) * 512].rearrange("(c p) t -> p c t", p=128),
                oev[:].rearrange("p (c t) -> p c t", c=4))))
            for ui in range(2):
                p.op("dve", [b_suev], [psb["u0" if ui == 0 else "u1"]],
                     lambda e, _u=ps["u0" if ui == 0 else "u1"], _ui=ui: e.tensor_copy(
                         suev[:, _ui * 512:(_ui + 1) * 512], _u[:, :]))
            for ui in range(2):
                out_toks.append(p.dma("sp", [], [b_suev], lambda e, _u=u, _ui=ui: e.dma_start(
                    sums_part[_ui * 4:(_ui + 1) * 4, _u * 512:(_u + 1) * 512],
                    bass.AP(suev_t, _ui * 512, [[32 * 1024, 4], [1, 512]]))))
        p.finish(out_toks)
    return nc



def emit_postln_chunk(p, *, t_ap, t_buf, col, scr_bf, scr_bf_buf,
                      rows, rows_buf, consts, consts_buf, ones_f32, ones_f32_buf,
                      ones_bf, ones_bf_buf, psS, psSb, psQ, psQb, psA, psAb,
                      psB, psBb, out_f32_ap, stride=None):
    TW = stride if stride is not None else T
    sl = [Buf("lnsl%d" % i) for i in range(4)]
    for g in range(8):
        a, b2 = (g % 2) * 2, (g % 2) * 2 + 1
        p.op("dve", [sl[a]], [t_buf], lambda e, _g=g, _a=a: e.tensor_copy(
            scr_bf[:, _a * 512:(_a + 1) * 512], t_ap[:, _g * TW + col:_g * TW + col + 512]))
        p.op("dve", [sl[b2]], [sl[a]], lambda e, _a=a, _b=b2: e.tensor_tensor(
            scr_bf[:, _b * 512:(_b + 1) * 512],
            scr_bf[:, _a * 512:(_a + 1) * 512],
            scr_bf[:, _a * 512:(_a + 1) * 512], ALU.mult))
        p.pe_group([psSb], [sl[a], ones_bf_buf], lambda te, _g=g, _a=a: te.matmul(
            psS[0:1, :], ones_bf[:, :], scr_bf[:, _a * 512:(_a + 1) * 512],
            start=(_g == 0), stop=(_g == 7)))
        p.pe_group([psQb], [sl[b2], ones_bf_buf], lambda te, _g=g, _b=b2: te.matmul(
            psQ[0:1, :], ones_bf[:, :], scr_bf[:, _b * 512:(_b + 1) * 512],
            start=(_g == 0), stop=(_g == 7)))
    p.op("dve", [rows_buf], [psSb, consts_buf], lambda e: e.tensor_scalar(
        rows[:, 0:512], psS[0:1, :], consts[0:1, 16:17], None, ALU.mult))
    p.op("dve", [rows_buf], [psQb, consts_buf], lambda e: e.tensor_scalar(
        rows[:, 512:1024], psQ[0:1, :], consts[0:1, 16:17], None, ALU.mult))
    p.op("dve", [rows_buf], [rows_buf], lambda e: e.tensor_tensor(
        rows[:, 1024:1536], rows[:, 0:512], rows[:, 0:512], ALU.mult))
    p.op("dve", [rows_buf], [rows_buf], lambda e: e.tensor_tensor(
        rows[:, 1536:2048], rows[:, 512:1024], rows[:, 1024:1536], ALU.subtract))
    p.op("act", [rows_buf], [rows_buf, consts_buf], lambda e: e.activation(
        rows[:, 1024:1536], rows[:, 1536:2048], AF.Ln, bias=consts[0:1, 17:18]))
    p.op("act", [rows_buf], [rows_buf], lambda e: e.activation(
        rows[:, 512:1024], rows[:, 1024:1536], AF.Exp, scale=-0.5))
    p.op("dve", [rows_buf], [rows_buf], lambda e: e.tensor_tensor(
        rows[:, 1024:1536], rows[:, 0:512], rows[:, 512:1024], ALU.mult))
    p.pe_group([psAb], [rows_buf, ones_f32_buf], lambda te: te.matmul(
        psA[:, :], ones_f32[:, :], rows[:, 512:1024], start=True, stop=True))
    p.pe_group([psBb], [rows_buf, ones_f32_buf], lambda te: te.matmul(
        psB[:, :], ones_f32[:, :], rows[:, 1024:1536], start=True, stop=True))
    for g in range(8):
        dst, dbuf = out_f32_ap(g)
        p.op("dve", [dbuf], [t_buf, psAb], lambda e, _g=g, _d=dst: e.tensor_tensor(
            _d, t_ap[:, _g * TW + col:_g * TW + col + 512], psA[:, :], ALU.mult))
        p.op("dve", [dbuf], [dbuf, psBb], lambda e, _g=g, _d=dst: e.tensor_tensor(
            _d, _d, psB[:, :], ALU.subtract))


def emit_rms_chunk(p, *, src_ap, src_buf, col_src, scr_bf, scr_bf_buf,
                   rows, rows_buf, consts, consts_buf, ones_f32, ones_f32_buf,
                   ones_bf, ones_bf_buf, psQ, psQb, psA, psAb,
                   w_col, w_col_buf, xn_ap, xn_buf, col_dst, scr_f32, scr_f32_buf,
                   src_stride=512, dst_stride=None, w_off=0):
    dst_st = dst_stride if dst_stride is not None else T
    sl = [Buf("rmsl%d" % i) for i in range(4)]
    for g in range(8):
        a, b2 = (g % 2) * 2, (g % 2) * 2 + 1
        p.op("dve", [sl[a]], [src_buf], lambda e, _g=g, _a=a: e.tensor_copy(
            scr_bf[:, _a * 512:(_a + 1) * 512],
            src_ap[:, _g * src_stride + col_src:_g * src_stride + col_src + 512]))
        p.op("dve", [sl[b2]], [sl[a]], lambda e, _a=a, _b=b2: e.tensor_tensor(
            scr_bf[:, _b * 512:(_b + 1) * 512],
            scr_bf[:, _a * 512:(_a + 1) * 512],
            scr_bf[:, _a * 512:(_a + 1) * 512], ALU.mult))
        p.pe_group([psQb], [sl[b2], ones_bf_buf], lambda te, _g=g, _b=b2: te.matmul(
            psQ[0:1, :], ones_bf[:, :], scr_bf[:, _b * 512:(_b + 1) * 512],
            start=(_g == 0), stop=(_g == 7)))
    p.op("dve", [rows_buf], [psQb, consts_buf], lambda e: e.tensor_scalar(
        rows[:, 0:512], psQ[0:1, :], consts[0:1, 16:17], None, ALU.mult))
    p.op("act", [rows_buf], [rows_buf], lambda e: e.activation(
        rows[:, 512:1024], rows[:, 0:512], AF.Ln))
    p.op("act", [rows_buf], [rows_buf], lambda e: e.activation(
        rows[:, 0:512], rows[:, 512:1024], AF.Exp, scale=0.5))
    p.op("dve", [rows_buf], [rows_buf, consts_buf], lambda e: e.tensor_scalar(
        rows[:, 512:1024], rows[:, 0:512], consts[0:1, 17:18], None, ALU.add))
    p.op("dve", [rows_buf], [rows_buf], lambda e: e.reciprocal(
        rows[:, 0:512], rows[:, 512:1024]))
    p.pe_group([psAb], [rows_buf, ones_f32_buf], lambda te: te.matmul(
        psA[:, :], ones_f32[:, :], rows[:, 0:512], start=True, stop=True))
    for g in range(8):
        p.op("dve", [scr_f32_buf], [src_buf, psAb], lambda e, _g=g: e.tensor_tensor(
            scr_f32[:, _g * 512:(_g + 1) * 512],
            src_ap[:, _g * src_stride + col_src:_g * src_stride + col_src + 512],
            psA[:, :], ALU.mult))
        p.op("dve", [xn_buf], [scr_f32_buf, w_col_buf], lambda e, _g=g: e.tensor_scalar(
            xn_ap[:, _g * dst_st + col_dst:_g * dst_st + col_dst + 512],
            scr_f32[:, _g * 512:(_g + 1) * 512], w_col[:, w_off + _g:w_off + _g + 1],
            None, ALU.mult))



MLA_OFF = dict(wkvd=0, wqd=2048, wqq=4096, wku=6144, wrk=7168, wvu=11264)


def build_l2b():
    nc = bass.Bass("TRN2", target_bir_lowering=False)
    dr = lambda n, s, d, k: nc.dram_tensor(n, s, d, kind=k).ap()
    o_parts = dr("o_parts", [4 * 8 * 512, 512], BF16, "ExternalInput")
    sums_in = dr("sums_in", [4 * 16 * 4, 512], F32, "ExternalInput")
    xf = dr("xf", [D, T], F32, "ExternalInput")
    wproj = dr("wproj", [D, D], F32, "ExternalInput")
    sel8 = dr("sel8", [16, 8 * 128], F32, "ExternalInput")
    bcols_in = dr("bcols_in", [128, 16], F32, "ExternalInput")
    wkvd = dr("wkvd", [D, 256], F32, "ExternalInput")
    wqd = dr("wqd", [D, 256], F32, "ExternalInput")
    wqq = dr("wqq", [256, D], F32, "ExternalInput")
    wku = dr("wku", [256, 512], F32, "ExternalInput")
    wrk = dr("wrk", [D, 512], F32, "ExternalInput")
    wvu = dr("wvu", [256, D], F32, "ExternalInput")
    ropeCq = dr("ropeCq", [128, T], F32, "ExternalInput")
    ropeSq = dr("ropeSq", [128, T], F32, "ExternalInput")
    ropeCk = dr("ropeCk", [128, T], F32, "ExternalInput")
    ropeSk = dr("ropeSk", [128, T], F32, "ExternalInput")
    permq = dr("permq", [128, 128], F32, "ExternalInput")
    permk = dr("permk", [128, 128], F32, "ExternalInput")
    r1_out = dr("r1_out", [D, T], F32, "ExternalOutput")
    qq_out = dr("qq_out", [D, T], BF16, "ExternalOutput")
    k_out2 = dr("k_out2", [512, T], BF16, "ExternalOutput")
    kr_out = dr("kr_out", [512, T], BF16, "ExternalOutput")
    vm_out = dr("vm_out", [T, D], BF16, "ExternalOutput")

    out_toks = []
    with ExitStack() as ctx:
        p = Prog(nc, ctx)
        oc_sb, b_oc = p.sb("oc_sb", [128, 16 * 512], BF16)
        o_full, b_of = p.sb("o_full", [128, 8 * 512], BF16)
        onorm, b_on = p.sb("onorm", [128, 8 * 512], BF16)
        oscr, b_oscr = p.sb("oscr", [128, 2 * 512], BF16)
        s_sb, b_s = p.sb("s_sb", [16, 4 * 512], F32)
        srow, b_srow = p.sb("srow", [16, 4 * 512], F32)
        sbf, b_sbf = p.sb("sbf", [16, 512], BF16)
        sel_sb, b_sel = p.sb("sel_sb", [16, 8 * 128], BF16)
        rt_sb, b_rt = p.sb("rt_sb", [128, 512], BF16)
        wbig, b_wbig = p.sb("wbig", [128, 13312], BF16)
        bcols, b_bcols = p.sb("bcols", [128, 18], F32)
        x_st, b_xst = p.sb("x_st", [128, 8 * 512], F32)
        t_ar, b_t = p.sb("t_ar", [128, 8 * 512], F32)
        scr_bf, b_scrbf = p.sb("scr_bf", [128, 4 * 512], BF16)
        rows, b_rows = p.sb("rows", [1, 4 * 512], F32)
        r1_ar, b_r1 = p.sb("r1_ar", [128, 8 * 512], F32)
        xn_sb, b_xn = p.sb("xn_sb", [128, 8 * T], BF16)
        lat_sb, b_lat = p.sb("lat_sb", [128, 2 * T], BF16)
        qlat_sb, b_qlat = p.sb("qlat_sb", [128, 2 * T], BF16)
        rope_sb, b_rope = p.sb("rope_sb", [128, 2 * 512], F32)
        perm_sb, b_perm = p.sb("perm_sb", [128, 2 * 128], BF16)
        rsc, b_rsc = p.sb("rsc", [128, 3 * 512], F32)
        rbf, b_rbf = p.sb("rbf", [128, 512], BF16)
        ev, b_ev0 = p.sb("ev", [128, 4 * 512], BF16)
        b_ev = [b_ev0] + [Buf(f"ev{i}") for i in range(1, 4)]
        ones_col, b_ones = p.sb("ones_col", [128, 1], BF16)
        ones_f32, b_ones32 = p.sb("ones_f32", [1, 128], F32)

        ps, psb = {}, {}
        for n in ("pv0", "pv1", "pv2", "pv3", "s0", "s1", "u0", "u1"):
            ps[n], psb[n] = p.ps("ps_" + n, [128, 512])

        p.dma("pool", [b_sel], [], lambda e: e.dma_start(sel_sb[:, :], sel8))
        p.dma("pool", [b_perm], [], lambda e: e.dma_start(perm_sb[:, 0:128], permq))
        p.dma("pool", [b_perm], [b_perm], lambda e: e.dma_start(perm_sb[:, 128:256], permk))
        p.dma("sp", [b_bcols], [], lambda e: e.dma_start(bcols[:, 0:16], bcols_in))
        p.op("dve", [b_bcols], [b_bcols], lambda e: e.memset(bcols[:, 16:17], 1.0 / 1024.0))
        p.op("dve", [b_bcols], [b_bcols], lambda e: e.memset(bcols[:, 17:18], 1e-5))
        p.op("dve", [b_ones], [], lambda e: e.memset(ones_col[:, :], 1.0))
        p.op("dve", [b_ones32], [], lambda e: e.memset(ones_f32[:, :], 1.0))
        p.dma("pool", [b_wbig], [], lambda e: e.dma_start(
            wbig[:, 0:8192].rearrange("p (k m) -> p k m", k=8),
            wproj.rearrange("(k p) m -> p k m", p=128)))

        for qt in range(4):
            p.dma("sp", [b_xst], [], lambda e, _q=qt: e.dma_start(
                x_st[:].rearrange("p (g t) -> p g t", g=8),
                xf[:, _q * 512:(_q + 1) * 512].rearrange("(g p) t -> p g t", p=128)))
            p.dma("sp", [b_s], [], lambda e, _q=qt: e.dma_start(
                s_sb[:].rearrange("h (s t) -> h s t", s=4),
                sums_in[_q * 64:(_q + 1) * 64, :].rearrange("(h s) t -> h s t", s=4)))
            p.op("dve", [b_srow], [b_s], lambda e: e.tensor_tensor(
                srow[:, 0:512], s_sb[:, 0:512], s_sb[:, 512:1024], ALU.add))
            p.op("dve", [b_srow], [b_s, b_srow], lambda e: e.tensor_tensor(
                srow[:, 512:1024], s_sb[:, 1024:1536], s_sb[:, 1536:2048], ALU.add))
            p.op("dve", [b_srow], [b_srow], lambda e: e.tensor_tensor(
                srow[:, 1024:1536], srow[:, 0:512], srow[:, 512:1024], ALU.add))
            p.op("dve", [b_srow], [b_srow], lambda e: e.reciprocal(
                srow[:, 1536:2048], srow[:, 1024:1536]))
            p.op("dve", [b_sbf], [b_srow], lambda e: e.tensor_copy(
                sbf[:, :], srow[:, 1536:2048]))
            for hh in range(2):
                p.dma("sp", [b_oc], [], lambda e, _q=qt, _h=hh: e.dma_start(
                    oc_sb[:].rearrange("p (s c t) -> p s c t", s=4, c=4),
                    o_parts[_q * 4096 + _h * 2048:_q * 4096 + (_h + 1) * 2048, :].rearrange(
                        "(s c p) t -> p s c t", p=128, c=4)))
                for pi in range(4):
                    pg = hh * 4 + pi
                    p.op("dve", [b_oscr], [b_oc], lambda e, _pi=pi: e.tensor_tensor(
                        oscr[:, 0:512], oc_sb[:, (0 * 4 + _pi) * 512:(0 * 4 + _pi + 1) * 512],
                        oc_sb[:, (1 * 4 + _pi) * 512:(1 * 4 + _pi + 1) * 512], ALU.add))
                    p.op("dve", [b_oscr], [b_oc, b_oscr], lambda e, _pi=pi: e.tensor_tensor(
                        oscr[:, 512:1024], oc_sb[:, (2 * 4 + _pi) * 512:(2 * 4 + _pi + 1) * 512],
                        oc_sb[:, (3 * 4 + _pi) * 512:(3 * 4 + _pi + 1) * 512], ALU.add))
                    p.op("dve", [b_of], [b_oscr], lambda e, _pg=pg: e.tensor_tensor(
                        o_full[:, _pg * 512:(_pg + 1) * 512],
                        oscr[:, 0:512], oscr[:, 512:1024], ALU.add))
            for pg in range(8):
                p.pe_group([psb["s0"]], [b_sbf, b_sel], lambda te, _pg=pg: te.matmul(
                    ps["s0"][:, :], sel_sb[:, _pg * 128:(_pg + 1) * 128],
                    sbf[:, :], start=True, stop=True))
                p.op("act", [b_rt], [psb["s0"]], lambda e: e.activation(
                    rt_sb[:, :], ps["s0"][:, :], AF.Copy, scale=1.0))
                p.op("dve", [b_on], [b_of, b_rt], lambda e, _pg=pg: e.tensor_tensor(
                    onorm[:, _pg * 512:(_pg + 1) * 512],
                    o_full[:, _pg * 512:(_pg + 1) * 512], rt_sb[:, :], ALU.mult))
            for fo in range(8):
                pt, pb_ = (ps["s1"], psb["s1"]) if fo % 2 == 0 else (ps["pv0"], psb["pv0"])

                def emitc(te, _fo=fo, _pt=pt):
                    last = None
                    for k in range(8):
                        last = te.matmul(_pt[:, :],
                                         wbig[:, k * 1024 + _fo * 128:k * 1024 + _fo * 128 + 128],
                                         onorm[:, k * 512:(k + 1) * 512],
                                         start=(k == 0), stop=(k == 7))
                    return last
                p.pe_group([pb_], [b_wbig, b_on], emitc)
                p.op("dve", [b_t], [pb_, b_bcols, b_xst], lambda e, _fo=fo, _pt=pt: e.scalar_tensor_tensor(
                    t_ar[:, _fo * 512:(_fo + 1) * 512], _pt[:, :], bcols[:, _fo:_fo + 1],
                    x_st[:, _fo * 512:(_fo + 1) * 512], ALU.add, ALU.add))
            emit_postln_chunk(
                p, t_ap=t_ar, t_buf=b_t, col=0, scr_bf=scr_bf, scr_bf_buf=b_scrbf,
                rows=rows, rows_buf=b_rows, consts=bcols, consts_buf=b_bcols,
                ones_f32=ones_f32, ones_f32_buf=b_ones32, ones_bf=ones_col,
                ones_bf_buf=b_ones, psS=ps["u0"], psSb=psb["u0"], psQ=ps["u1"],
                psQb=psb["u1"], psA=ps["pv2"], psAb=psb["pv2"], psB=ps["pv3"],
                psBb=psb["pv3"],
                out_f32_ap=lambda g: (r1_ar[:, g * 512:(g + 1) * 512], b_r1),
                stride=512)
            out_toks.append(p.dma("sp", [], [b_r1], lambda e, _q=qt: e.dma_start(
                r1_out[:, _q * 512:(_q + 1) * 512].rearrange("(g p) t -> p g t", p=128),
                r1_ar[:].rearrange("p (g t) -> p g t", g=8))))
            emit_rms_chunk(
                p, src_ap=r1_ar, src_buf=b_r1, col_src=0, scr_bf=scr_bf,
                scr_bf_buf=b_scrbf, rows=rows, rows_buf=b_rows, consts=bcols,
                consts_buf=b_bcols, ones_f32=ones_f32, ones_f32_buf=b_ones32,
                ones_bf=ones_col, ones_bf_buf=b_ones, psQ=ps["u1"], psQb=psb["u1"],
                psA=ps["pv2"], psAb=psb["pv2"], w_col=bcols, w_col_buf=b_bcols,
                xn_ap=xn_sb, xn_buf=b_xn, col_dst=qt * 512, scr_f32=t_ar,
                scr_f32_buf=b_t, src_stride=512, dst_stride=T, w_off=8)

        p.dma("pool", [b_wbig], [b_wbig], lambda e: e.dma_start(
            wbig[:, 0:2048].rearrange("p (k m) -> p k m", k=8),
            wkvd.rearrange("(k p) m -> p k m", p=128)))
        p.dma("pool", [b_wbig], [b_wbig], lambda e: e.dma_start(
            wbig[:, 2048:4096].rearrange("p (k m) -> p k m", k=8),
            wqd.rearrange("(k p) m -> p k m", p=128)))
        p.dma("pool", [b_wbig], [b_wbig], lambda e: e.dma_start(
            wbig[:, 4096:6144].rearrange("p (k m) -> p k m", k=2),
            wqq.rearrange("(k p) m -> p k m", p=128)))
        p.dma("pool", [b_wbig], [b_wbig], lambda e: e.dma_start(
            wbig[:, 6144:7168].rearrange("p (k m) -> p k m", k=2),
            wku.rearrange("(k p) m -> p k m", p=128)))
        p.dma("pool", [b_wbig], [b_wbig], lambda e: e.dma_start(
            wbig[:, 7168:11264].rearrange("p (k m) -> p k m", k=8),
            wrk.rearrange("(k p) m -> p k m", p=128)))
        p.dma("pool", [b_wbig], [b_wbig], lambda e: e.dma_start(
            wbig[:, 11264:13312].rearrange("p (k m) -> p k m", k=2),
            wvu.rearrange("(k p) m -> p k m", p=128)))

        def proj(dst_sb, dst_buf, woff, nk, kstride, src_sb, src_buf, m, nt, pt, pb_):
            def emitc(te):
                last = None
                for k in range(nk):
                    last = te.matmul(pt[:, :],
                                     wbig[:, woff + k * kstride + m * 128:
                                          woff + k * kstride + m * 128 + 128],
                                     src_sb[:, k * T + nt * 512:k * T + nt * 512 + 512],
                                     start=(k == 0), stop=(k == nk - 1))
                return last
            p.pe_group([pb_], [b_wbig, src_buf], emitc)
            p.op("act", [dst_buf], [pb_], lambda e: e.activation(
                dst_sb[:, m * T + nt * 512:m * T + nt * 512 + 512], pt[:, :], AF.Copy, scale=1.0))

        pscyc = [("s0", psb["s0"]), ("s1", psb["s1"])]
        ci = 0
        for m in range(2):
            for nt in range(4):
                pn, pb_ = pscyc[ci % 2]; ci += 1
                proj(lat_sb, b_lat, MLA_OFF["wkvd"], 8, 256, xn_sb, b_xn, m, nt, ps[pn], pb_)
        for m in range(2):
            for nt in range(4):
                pn, pb_ = pscyc[ci % 2]; ci += 1
                proj(qlat_sb, b_qlat, MLA_OFF["wqd"], 8, 256, xn_sb, b_xn, m, nt, ps[pn], pb_)
        for m in range(4):
            for nt in range(4):
                pn, pb_ = pscyc[ci % 2]; ci += 1
                evi = (m * 4 + nt) % 4

                def emitk(te, _m=m, _nt=nt, _pt=ps[pn]):
                    last = None
                    for k in range(2):
                        last = te.matmul(_pt[:, :],
                                         wbig[:, MLA_OFF["wku"] + k * 512 + _m * 128:
                                              MLA_OFF["wku"] + k * 512 + _m * 128 + 128],
                                         lat_sb[:, k * T + _nt * 512:k * T + _nt * 512 + 512],
                                         start=(k == 0), stop=(k == 1))
                    return last
                p.pe_group([pb_], [b_wbig, b_lat], emitk)
                p.op("act", [b_ev[evi]], [pb_], lambda e, _e=evi, _pt=ps[pn]: e.activation(
                    ev[:, _e * 512:(_e + 1) * 512], _pt[:, :], AF.Copy, scale=1.0))
                out_toks.append(p.dma("sp", [], [b_ev[evi]], lambda e, _m=m, _nt=nt, _e=evi: e.dma_start(
                    k_out2[_m * 128:(_m + 1) * 128, _nt * 512:(_nt + 1) * 512],
                    ev[:, _e * 512:(_e + 1) * 512])))
        for tt in range(16):
            for nv in range(2):
                pn, pb_ = pscyc[ci % 2]; ci += 1
                evi = (tt * 2 + nv) % 4

                def emitv(te, _tt=tt, _nv=nv, _pt=ps[pn]):
                    last = None
                    for k in range(2):
                        last = te.matmul(_pt[:, :],
                                         lat_sb[:, k * T + _tt * 128:k * T + _tt * 128 + 128],
                                         wbig[:, MLA_OFF["wvu"] + k * 1024 + _nv * 512:
                                              MLA_OFF["wvu"] + k * 1024 + _nv * 512 + 512],
                                         start=(k == 0), stop=(k == 1))
                    return last
                p.pe_group([pb_], [b_wbig, b_lat], emitv)
                p.op("act", [b_ev[evi]], [pb_], lambda e, _e=evi, _pt=ps[pn]: e.activation(
                    ev[:, _e * 512:(_e + 1) * 512], _pt[:, :], AF.Copy, scale=1.0))
                out_toks.append(p.dma("sp", [], [b_ev[evi]], lambda e, _tt=tt, _nv=nv, _e=evi: e.dma_start(
                    vm_out[_tt * 128:(_tt + 1) * 128, _nv * 512:(_nv + 1) * 512],
                    ev[:, _e * 512:(_e + 1) * 512])))

        def rope_block(src_kind, m_count, woff, nk, kstride, src_sb, src_buf,
                       ctab_off, perm_off, dst_dram):
            nonlocal ci
            for nt in range(4):
                p.dma("sp", [b_rope], [], lambda e, _nt=nt, _c=ctab_off: e.dma_start(
                    rope_sb[:, 0:512], (ropeCq if _c == 0 else ropeCk)[:, _nt * 512:(_nt + 1) * 512]))
                p.dma("sp", [b_rope], [b_rope], lambda e, _nt=nt, _c=ctab_off: e.dma_start(
                    rope_sb[:, 512:1024], (ropeSq if _c == 0 else ropeSk)[:, _nt * 512:(_nt + 1) * 512]))
                for m in range(m_count):
                    pn, pb_ = pscyc[ci % 2]; ci += 1
                    pt = ps[pn]

                    def emitr(te, _m=m, _nt=nt, _pt=pt):
                        last = None
                        for k in range(nk):
                            last = te.matmul(_pt[:, :],
                                             wbig[:, woff + k * kstride + _m * 128:
                                                  woff + k * kstride + _m * 128 + 128],
                                             src_sb[:, k * T + _nt * 512:k * T + _nt * 512 + 512],
                                             start=(k == 0), stop=(k == nk - 1))
                        return last
                    p.pe_group([pb_], [b_wbig, src_buf], emitr)
                    p.op("act", [b_rsc], [pb_], lambda e, _pt=pt: e.activation(
                        rsc[:, 0:512], _pt[:, :], AF.Copy, scale=1.0))
                    p.op("dve", [b_rbf], [b_rsc], lambda e: e.tensor_copy(
                        rbf[:, :], rsc[:, 0:512]))
                    pvn = "pv" + str(ci % 2)
                    p.pe_group([psb[pvn]], [b_perm, b_rbf], lambda te, _pv=ps[pvn], _po=perm_off: te.matmul(
                        _pv[:, :], perm_sb[:, _po:_po + 128], rbf[:, :], start=True, stop=True))
                    p.op("dve", [b_rsc], [b_rsc, b_rope], lambda e: e.tensor_tensor(
                        rsc[:, 512:1024], rsc[:, 0:512], rope_sb[:, 0:512], ALU.mult))
                    p.op("dve", [b_rsc], [b_rsc, psb[pvn], b_rope], lambda e, _pv=ps[pvn]: e.tensor_tensor(
                        rsc[:, 1024:1536], _pv[:, :], rope_sb[:, 512:1024], ALU.mult))
                    evi = ci % 4
                    p.op("dve", [b_ev[evi]], [b_rsc], lambda e, _e=evi: e.tensor_tensor(
                        ev[:, _e * 512:(_e + 1) * 512], rsc[:, 512:1024],
                        rsc[:, 1024:1536], ALU.add))
                    out_toks.append(p.dma("sp", [], [b_ev[evi]], lambda e, _m=m, _nt=nt, _e=evi: e.dma_start(
                        dst_dram[_m * 128:(_m + 1) * 128, _nt * 512:(_nt + 1) * 512],
                        ev[:, _e * 512:(_e + 1) * 512])))

        rope_block("kr", 4, MLA_OFF["wrk"], 8, 512, xn_sb, b_xn, 1, 128, kr_out)
        rope_block("qq", 8, MLA_OFF["wqq"], 2, 1024, qlat_sb, b_qlat, 0, 0, qq_out)
        p.finish(out_toks)
    return nc



def host_attn_partial_inputs(core, q_full, k_full, v_full):
    j = core % 4
    hh = j & 1
    slots = attn_slot_map(j)
    qp = np.zeros((512, N_SLOTS * 512), BF)
    kk = np.zeros((512, N_SLOTS * 2048), BF)
    vv = np.zeros((N_SLOTS * 2048, 512), BF)
    for u, (span, qt) in enumerate(slots):
        qp[:, u * 512:(u + 1) * 512] = q_full[hh * 512:(hh + 1) * 512, qt * 512:(qt + 1) * 512]
        kk[:, u * 2048:(u + 1) * 2048] = k_full[hh * 512:(hh + 1) * 512,
                                                span * 2048:(span + 1) * 2048]
        vv[u * 2048:(u + 1) * 2048, :] = v_full[span * 2048:(span + 1) * 2048,
                                                hh * 512:(hh + 1) * 512]
    return {"qp": qp, "kk_in": kk, "v_in": vv,
            "masks4": causal_mask_tiles().transpose(1, 0, 2).reshape(128, 4 * 512)}


def host_route_partials(core, res_2a):
    b, lo, hi = core_chunks(core)
    qtiles = [2 * lo, 2 * lo + 1, 2 * hi, 2 * hi + 1]
    o_parts = np.zeros((4 * 8 * 512, 512), BF)
    sums_in = np.zeros((256, 512), np.float32)
    for qi, qt in enumerate(qtiles):
        smax = qt // 4
        for hh in range(2):
            for s in range(min(smax + 1, 4)):
                jj = (0 if s in (0, 3) else 2) + hh
                cc = b * 4 + jj
                u = attn_slot_map(jj).index((s, qt))
                o_parts[(qi * 8 + hh * 4 + s) * 512:(qi * 8 + hh * 4 + s + 1) * 512, :] = \
                    res_2a[cc]["o_part"][:, u * 512:(u + 1) * 512]
                for hl in range(8):
                    h = hh * 8 + hl
                    sums_in[qi * 64 + h * 4 + s, :] = \
                        res_2a[cc]["sums_part"][hl, u * 512:(u + 1) * 512]
    return o_parts, sums_in


def host_sel8():
    sel = np.zeros((16, 8 * 128), np.float32)
    for pg in range(8):
        sel[2 * pg, pg * 128:pg * 128 + 64] = 1.0
        sel[2 * pg + 1, pg * 128 + 64:pg * 128 + 128] = 1.0
    return sel


def host_rope_mats(core):
    b, lo, hi = core_chunks(core)
    pos = np.concatenate([np.arange(lo * CH, (lo + 1) * CH),
                          np.arange(hi * CH, (hi + 1) * CH)])
    cos32, sin32 = rope_tables(pos)
    Cq = np.zeros((128, T), np.float32)
    Sq = np.zeros((128, T), np.float32)
    Cq[0:32] = 1.0
    Cq[64:96] = 1.0
    Cq[32:64] = cos32
    Cq[96:128] = cos32
    Sq[32:64] = sin32
    Sq[96:128] = sin32
    Ck = np.tile(cos32, (4, 1))
    Sk = np.tile(sin32, (4, 1))
    P = perm32()
    permq = np.zeros((128, 128), np.float32)
    permq[32:64, 32:64] = P
    permq[96:128, 96:128] = P
    permk = np.zeros((128, 128), np.float32)
    for i in range(4):
        permk[32 * i:32 * (i + 1), 32 * i:32 * (i + 1)] = P
    return Cq, Sq, Ck, Sk, permq, permk


def host_l2b_inputs(inputs, core, res_2a):
    b, lo, hi = core_chunks(core)
    x = np.asarray(inputs["L0_attn_x"][b])
    xc = np.concatenate([x[lo * CH:(lo + 1) * CH], x[hi * CH:(hi + 1) * CH]], 0)
    o_parts, sums_in = host_route_partials(core, res_2a)
    wqu = np.asarray(inputs["L0_ffn_W_qu"], np.float32)
    wrq = np.asarray(inputs["L0_ffn_W_ropeq"], np.float32)
    wqq = np.zeros((256, D), np.float32)
    for h in range(16):
        wqq[:, h * 64:h * 64 + 32] = wqu[:, h * 32:(h + 1) * 32]
        wqq[:, h * 64 + 32:h * 64 + 64] = wrq[:, h * 32:(h + 1) * 32]
    Cq, Sq, Ck, Sk, permq, permk = host_rope_mats(core)
    return {
        "o_parts": o_parts, "sums_in": sums_in, "xf": fm(xc),
        "wproj": np.ascontiguousarray(np.asarray(inputs["L0_attn_W_proj"], np.float32)),
        "sel8": host_sel8(),
        "bcols_in": np.concatenate([col_tiles(np.asarray(inputs["L0_attn_b_proj"])),
                                    col_tiles(np.asarray(inputs["L0_ffn_w_ln1"]))], 1),
        "wkvd": np.ascontiguousarray(np.asarray(inputs["L0_ffn_W_kvd"], np.float32)),
        "wqd": np.ascontiguousarray(np.asarray(inputs["L0_ffn_W_qd"], np.float32)),
        "wqq": wqq,
        "wku": np.ascontiguousarray(np.asarray(inputs["L0_ffn_W_ku"], np.float32)),
        "wrk": np.ascontiguousarray(np.asarray(inputs["L0_ffn_W_ropek"], np.float32)),
        "wvu": np.ascontiguousarray(np.asarray(inputs["L0_ffn_W_vu"], np.float32)),
        "ropeCq": Cq, "ropeSq": Sq, "ropeCk": Ck, "ropeSk": Sk,
        "permq": permq, "permk": permk,
    }


def assemble_full_fm(res, key, cores):
    Dp = res[cores[0]][key].shape[0]
    out = np.zeros((Dp, S), res[cores[0]][key].dtype)
    for c in cores:
        _, lo, hi = core_chunks(c)
        out[:, lo * CH:(lo + 1) * CH] = res[c][key][:, 0:CH]
        out[:, hi * CH:(hi + 1) * CH] = res[c][key][:, CH:2 * CH]
    return out


def assemble_full_tm(res, key, cores):
    Dp = res[cores[0]][key].shape[1]
    out = np.zeros((S, Dp), res[cores[0]][key].dtype)
    for c in cores:
        _, lo, hi = core_chunks(c)
        out[lo * CH:(lo + 1) * CH, :] = res[c][key][0:CH, :]
        out[hi * CH:(hi + 1) * CH, :] = res[c][key][CH:2 * CH, :]
    return out



def build_l3b():
    nc = bass.Bass("TRN2", target_bir_lowering=False)
    dr = lambda n, s, d, k: nc.dram_tensor(n, s, d, kind=k).ap()
    o_parts = dr("o_parts", [4 * 8 * 512, 512], BF16, "ExternalInput")
    sums_in = dr("sums_in", [4 * 16 * 4, 512], F32, "ExternalInput")
    r1 = dr("r1", [D, T], F32, "ExternalInput")
    w_o = dr("w_o", [D, D], F32, "ExternalInput")
    sel8 = dr("sel8", [16, 8 * 128], F32, "ExternalInput")
    bcols_in = dr("bcols_in", [128, 8], F32, "ExternalInput")
    wg_s = dr("wg_s", [D, D], F32, "ExternalInput")
    wu_s = dr("wu_s", [D, D], F32, "ExternalInput")
    wd_s = dr("wd_s", [D, D], F32, "ExternalInput")
    wg_r = dr("wg_r", [7, D, D], F32, "ExternalInput")
    wu_r = dr("wu_r", [7, D, D], F32, "ExternalInput")
    wd_r = dr("wd_r", [7, D, D], F32, "ExternalInput")
    w_router = dr("w_router", [D, 7], F32, "ExternalInput")
    rb_in = dr("rb_in", [1, 7], F32, "ExternalInput")
    idf32 = dr("idf32", [128, 128], F32, "ExternalInput")
    esel_in = dr("esel_in", [7, 7 * 128], F32, "ExternalInput")
    wqk_s_in = dr("wqk_s_in", [D, 512], F32, "ExternalInput")
    wv_s_in = dr("wv_s_in", [D, 512], F32, "ExternalInput")
    wqk_l_in = dr("wqk_l_in", [D, 2048], F32, "ExternalInput")
    wv_l_in = dr("wv_l_in", [D, 512], F32, "ExternalInput")
    r2_out = dr("r2_out", [D, T], F32, "ExternalOutput")
    qs_out = dr("qs_out", [256, T], BF16, "ExternalOutput")
    ks_out = dr("ks_out", [256, T], BF16, "ExternalOutput")
    vs_out = dr("vs_out", [T, 512], BF16, "ExternalOutput")
    ql_out = dr("ql_out", [D, T], BF16, "ExternalOutput")
    kl_out = dr("kl_out", [D, T], BF16, "ExternalOutput")
    vl_out = dr("vl_out", [T, 512], BF16, "ExternalOutput")
    x2r_dram = nc.dram_tensor("x2r_dram", [D, T], F32, kind="Internal").ap()
    y_dram = nc.dram_tensor("y_dram", [D, T], F32, kind="Internal").ap()

    out_toks = []
    with ExitStack() as ctx:
        p = Prog(nc, ctx)
        SH, b_sh0 = p.sb("SH", [128, 24576], BF16)
        xn2, b_xn2 = p.sb("xn2", [128, 8 * T], BF16)
        scr_bf, b_scrbf = p.sb("scr_bf", [128, 4 * 512], BF16)
        rows, b_rows = p.sb("rows", [1, 4 * 512], F32)
        bcols, b_bcols = p.sb("bcols", [128, 18], F32)
        t_ar, b_t = p.sb("t_ar", [128, 8 * 512], F32)
        x2_ar, b_x2 = p.sb("x2_ar", [128, 8 * 512], F32)
        r1_st, b_r1st = p.sb("r1_st", [128, 8 * 512], F32)
        s_sb, b_s = p.sb("s_sb", [16, 4 * 512], F32)
        srow, b_srow = p.sb("srow", [16, 4 * 512], F32)
        sbf, b_sbf = p.sb("sbf", [16, 512], BF16)
        sel_sb, b_sel = p.sb("sel_sb", [16, 8 * 128], BF16)
        rt_sb, b_rt = p.sb("rt_sb", [128, 512], BF16)
        ones_col, b_ones = p.sb("ones_col", [128, 1], BF16)
        ones_f32, b_ones32 = p.sb("ones_f32", [1, 128], F32)
        ones_r, b_onesr = p.sb("ones_r", [1, 128], BF16)
        wrt_sb, b_wrt = p.sb("wrt_sb", [128, 64], BF16)
        rbb, b_rbb = p.sb("rbb", [1, 8], BF16)
        id_sb, b_id = p.sb("id_sb", [128, 128], F32)
        lgs, b_lgs = p.sb("lgs", [128, 32], F32)
        esel_sb, b_esel = p.sb("esel_sb", [7, 7 * 128], F32)
        wts_fm, b_wts = p.sb("wts_fm", [7, 2048], F32)
        gu_sb, b_gu = p.sb("gu_sb", [128, 3 * 512], BF16)
        wb_sb, b_wb = p.sb("wb_sb", [128, 512], BF16)
        hmid, b_hm = p.sb("hmid", [128, 8 * 512], BF16)
        yev, b_yev0 = p.sb("yev", [128, 2 * 512], F32)
        b_yev = [b_yev0, Buf("yev1")]
        ev, b_ev0 = p.sb("ev", [128, 4 * 512], BF16)
        b_ev = [b_ev0] + [Buf(f"ev{i}") for i in range(1, 4)]
        r2st, b_r2st = p.sb("r2st", [128, 8 * 512], BF16)

        ps, psb = {}, {}
        for n in ("pv0", "pv1", "pv2", "pv3", "s0", "s1", "u0", "u1"):
            ps[n], psb[n] = p.ps("ps_" + n, [128, 512])

        p.dma("pool", [b_sel], [], lambda e: e.dma_start(sel_sb[:, :], sel8))
        p.dma("sp", [b_bcols], [], lambda e: e.dma_start(bcols[:, 0:8], bcols_in))
        p.op("dve", [b_bcols], [b_bcols], lambda e: e.memset(bcols[:, 16:17], 1.0 / 1024.0))
        p.op("dve", [b_bcols], [b_bcols], lambda e: e.memset(bcols[:, 17:18], 1e-5))
        p.op("dve", [b_ones], [], lambda e: e.memset(ones_col[:, :], 1.0))
        p.op("dve", [b_ones32], [], lambda e: e.memset(ones_f32[:, :], 1.0))
        p.op("dve", [b_onesr], [], lambda e: e.memset(ones_r[:, :], 1.0))
        p.dma("pool", [b_rbb], [], lambda e: e.dma_start(rbb[:, 0:7], rb_in))
        p.dma("sp", [b_id], [], lambda e: e.dma_start(id_sb[:, :], idf32))
        p.dma("sp", [b_esel], [], lambda e: e.dma_start(esel_sb[:, :], esel_in))
        p.dma("pool", [b_wrt], [], lambda e: e.dma_start(
            wrt_sb[:].rearrange("p (k m) -> p k m", k=8)[:, :, 0:7],
            w_router.rearrange("(k p) m -> p k m", p=128)))
        b_wbig = Buf("wo")
        wbig = SH[:, 13824:22016]
        p.dma("pool", [b_wbig], [], lambda e: e.dma_start(
            wbig.rearrange("p (k m) -> p k m", k=8),
            w_o.rearrange("(k p) m -> p k m", p=128)))

        b_oc, b_of, b_on = Buf("oc"), Buf("ofull"), Buf("onorm")
        oc_sb = SH[:, 0:8192]
        o_scr = SH[:, 8192:9728]
        onorm = SH[:, 9728:13824]

        for qt in range(4):
            p.dma("sp", [b_r1st], [], lambda e, _q=qt: e.dma_start(
                r1_st[:].rearrange("p (g t) -> p g t", g=8),
                r1[:, _q * 512:(_q + 1) * 512].rearrange("(g p) t -> p g t", p=128)))
            p.dma("sp", [b_s], [], lambda e, _q=qt: e.dma_start(
                s_sb[:].rearrange("h (s t) -> h s t", s=4),
                sums_in[_q * 64:(_q + 1) * 64, :].rearrange("(h s) t -> h s t", s=4)))
            p.op("dve", [b_srow], [b_s], lambda e: e.tensor_tensor(
                srow[:, 0:512], s_sb[:, 0:512], s_sb[:, 512:1024], ALU.add))
            p.op("dve", [b_srow], [b_s, b_srow], lambda e: e.tensor_tensor(
                srow[:, 512:1024], s_sb[:, 1024:1536], s_sb[:, 1536:2048], ALU.add))
            p.op("dve", [b_srow], [b_srow], lambda e: e.tensor_tensor(
                srow[:, 1024:1536], srow[:, 0:512], srow[:, 512:1024], ALU.add))
            p.op("dve", [b_srow], [b_srow], lambda e: e.reciprocal(
                srow[:, 1536:2048], srow[:, 1024:1536]))
            p.op("dve", [b_sbf], [b_srow], lambda e: e.tensor_copy(
                sbf[:, :], srow[:, 1536:2048]))
            for hh in range(2):
                p.dma("sp", [b_oc], [], lambda e, _q=qt, _h=hh: e.dma_start(
                    oc_sb.rearrange("p (s c t) -> p s c t", s=4, c=4),
                    o_parts[_q * 4096 + _h * 2048:_q * 4096 + (_h + 1) * 2048, :].rearrange(
                        "(s c p) t -> p s c t", p=128, c=4)))
                for pi in range(4):
                    pg = hh * 4 + pi
                    p.op("dve", [b_of], [b_oc], lambda e, _pi=pi: e.tensor_tensor(
                        o_scr[:, 0:512],
                        oc_sb[:, (0 * 4 + _pi) * 512:(0 * 4 + _pi + 1) * 512],
                        oc_sb[:, (1 * 4 + _pi) * 512:(1 * 4 + _pi + 1) * 512], ALU.add))
                    p.op("dve", [b_of], [b_oc, b_of], lambda e, _pi=pi: e.tensor_tensor(
                        o_scr[:, 512:1024],
                        oc_sb[:, (2 * 4 + _pi) * 512:(2 * 4 + _pi + 1) * 512],
                        oc_sb[:, (3 * 4 + _pi) * 512:(3 * 4 + _pi + 1) * 512], ALU.add))
                    p.op("dve", [b_of], [b_of], lambda e: e.tensor_tensor(
                        o_scr[:, 1024:1536], o_scr[:, 0:512], o_scr[:, 512:1024], ALU.add))
                    p.pe_group([psb["s0"]], [b_sbf, b_sel], lambda te, _pg=pg: te.matmul(
                        ps["s0"][:, :], sel_sb[:, _pg * 128:(_pg + 1) * 128],
                        sbf[:, :], start=True, stop=True))
                    p.op("act", [b_rt], [psb["s0"]], lambda e: e.activation(
                        rt_sb[:, :], ps["s0"][:, :], AF.Copy, scale=1.0))
                    p.op("dve", [b_on], [b_of, b_rt], lambda e, _pg=pg: e.tensor_tensor(
                        onorm[:, _pg * 512:(_pg + 1) * 512],
                        o_scr[:, 1024:1536], rt_sb[:, :], ALU.mult))
            for fo in range(8):
                pt, pb_ = (ps["s1"], psb["s1"]) if fo % 2 == 0 else (ps["pv0"], psb["pv0"])

                def emitc(te, _fo=fo, _pt=pt):
                    last = None
                    for k in range(8):
                        last = te.matmul(_pt[:, :],
                                         SH[:, 13824 + k * 1024 + _fo * 128:
                                            13824 + k * 1024 + _fo * 128 + 128],
                                         onorm[:, k * 512:(k + 1) * 512],
                                         start=(k == 0), stop=(k == 7))
                    return last
                p.pe_group([pb_], [b_wbig, b_on], emitc)
                p.op("dve", [b_x2], [pb_, b_r1st], lambda e, _fo=fo, _pt=pt: e.tensor_tensor(
                    x2_ar[:, _fo * 512:(_fo + 1) * 512], _pt[:, :],
                    r1_st[:, _fo * 512:(_fo + 1) * 512], ALU.add))
                p.op("dve", [b_t], [b_x2, b_r1st], lambda e, _fo=fo: e.tensor_tensor(
                    t_ar[:, _fo * 512:(_fo + 1) * 512],
                    x2_ar[:, _fo * 512:(_fo + 1) * 512],
                    r1_st[:, _fo * 512:(_fo + 1) * 512], ALU.add))
            out_toks.append(p.dma("sp", [], [b_t], lambda e, _q=qt: e.dma_start(
                x2r_dram[:, _q * 512:(_q + 1) * 512].rearrange("(g p) t -> p g t", p=128),
                t_ar[:].rearrange("p (g t) -> p g t", g=8))))
            emit_rms_chunk(
                p, src_ap=x2_ar, src_buf=b_x2, col_src=0, scr_bf=scr_bf,
                scr_bf_buf=b_scrbf, rows=rows, rows_buf=b_rows, consts=bcols,
                consts_buf=b_bcols, ones_f32=ones_f32, ones_f32_buf=b_ones32,
                ones_bf=ones_col, ones_bf_buf=b_ones, psQ=ps["u1"], psQb=psb["u1"],
                psA=ps["pv2"], psAb=psb["pv2"], w_col=bcols, w_col_buf=b_bcols,
                xn_ap=xn2, xn_buf=b_xn2, col_dst=qt * 512, scr_f32=t_ar,
                scr_f32_buf=b_t, src_stride=512, dst_stride=T, w_off=0)

        for tt in range(16):
            def emitr(te, _tt=tt):
                for k in range(8):
                    te.matmul(ps["s1"][0:128, 0:7],
                              xn2[:, k * T + _tt * 128:k * T + _tt * 128 + 128],
                              wrt_sb[:, k * 8:k * 8 + 7],
                              start=(k == 0), stop=False)
                return te.matmul(ps["s1"][0:128, 0:7], ones_r[:, :], rbb[:, 0:7],
                                 start=False, stop=True)
            p.pe_group([psb["s1"]], [b_xn2, b_wrt, b_onesr, b_rbb], emitr)
            p.op("act", [b_lgs], [psb["s1"]], lambda e: e.activation(
                lgs[:, 0:7], ps["s1"][0:128, 0:7], AF.Exp, scale=-1.0))
            p.op("dve", [b_lgs], [b_lgs], lambda e: e.tensor_scalar(
                lgs[:, 8:15], lgs[:, 0:7], 1.0, None, ALU.add))
            p.op("dve", [b_lgs], [b_lgs], lambda e: e.reciprocal(
                lgs[:, 16:23], lgs[:, 8:15]))
            p.op("dve", [b_lgs], [b_lgs], lambda e: e.reduce_max(
                lgs[:, 24:25], lgs[:, 16:23], axis=AX.X))
            p.op("dve", [b_lgs], [b_lgs], lambda e: e.tensor_scalar(
                lgs[:, 0:7], lgs[:, 16:23], lgs[:, 24:25], None, ALU.is_ge))
            p.op("dve", [b_lgs], [b_lgs], lambda e: e.scalar_tensor_tensor(
                lgs[:, 8:15], lgs[:, 0:7], -1e9, lgs[:, 16:23], ALU.mult, ALU.add))
            p.op("dve", [b_lgs], [b_lgs], lambda e: e.reduce_max(
                lgs[:, 25:26], lgs[:, 8:15], axis=AX.X))
            p.op("dve", [b_lgs], [b_lgs], lambda e: e.tensor_tensor(
                lgs[:, 26:27], lgs[:, 24:25], lgs[:, 25:26], ALU.add))
            p.op("dve", [b_lgs], [b_lgs], lambda e: e.reciprocal(
                lgs[:, 27:28], lgs[:, 26:27]))
            p.op("dve", [b_lgs], [b_lgs], lambda e: e.tensor_scalar(
                lgs[:, 0:7], lgs[:, 16:23], lgs[:, 25:26], None, ALU.is_ge))
            p.op("dve", [b_lgs], [b_lgs], lambda e: e.tensor_tensor(
                lgs[:, 8:15], lgs[:, 16:23], lgs[:, 0:7], ALU.mult))
            p.op("dve", [b_lgs], [b_lgs], lambda e: e.tensor_scalar(
                lgs[:, 0:7], lgs[:, 8:15], lgs[:, 27:28], None, ALU.mult))
            p.pe_group([psb["u0"]], [b_lgs, b_id], lambda te: te.transpose(
                ps["u0"][0:7, 0:128], lgs[:, 0:7], id_sb[:, :]))
            p.op("dve", [b_wts], [psb["u0"]], lambda e, _tt=tt: e.tensor_copy(
                wts_fm[:, _tt * 128:(_tt + 1) * 128], ps["u0"][0:7, 0:128]))

        b_wg, b_wu, b_wd = Buf("wg"), Buf("wu"), Buf("wd")
        wg_a = SH[:, 0:8192]
        wu_a = SH[:, 8192:16384]
        wd_a = SH[:, 16384:24576]
        yregs = [Buf(f"yreg{i}") for i in range(32)]
        for e in range(8):
            for (dst, bufs, src) in ((wg_a, b_wg, wg_s if e == 0 else wg_r),
                                     (wu_a, b_wu, wu_s if e == 0 else wu_r),
                                     (wd_a, b_wd, wd_s if e == 0 else wd_r)):
                extra = [b_oc, b_of, b_on, b_sh0, b_wbig] if e == 0 else []
                if e == 0:
                    p.dma("pool", [bufs] + extra, [], lambda ee, _d=dst, _s=src: ee.dma_start(
                        _d.rearrange("p (k m) -> p k m", k=8),
                        _s.rearrange("(k p) m -> p k m", p=128)))
                else:
                    p.dma("pool", [bufs], [], lambda ee, _d=dst, _s=src, _e=e: ee.dma_start(
                        _d.rearrange("p (k m) -> p k m", k=8),
                        _s[_e - 1:_e].rearrange("one (k p) m -> p (one k) m", p=128)))
            for nt in range(4):
                if e > 0:
                    p.pe_group([psb["u1"]], [b_wts, b_esel], lambda te, _e=e, _nt=nt: te.matmul(
                        ps["u1"][:, :], esel_sb[:, (_e - 1) * 128:_e * 128],
                        wts_fm[0:7, _nt * 512:(_nt + 1) * 512], start=True, stop=True))
                    p.op("act", [b_wb], [psb["u1"]], lambda e_: e_.activation(
                        wb_sb[:, :], ps["u1"][:, :], AF.Copy, scale=1.0))
                for i in range(8):
                    pg_, pgb = (ps["pv0"], psb["pv0"]) if i % 2 == 0 else (ps["pv1"], psb["pv1"])
                    pu_, pub = (ps["s0"], psb["s0"]) if i % 2 == 0 else (ps["s1"], psb["s1"])

                    def emitg(te, _i=i, _nt=nt, _pt=pg_):
                        last = None
                        for k in range(8):
                            last = te.matmul(_pt[:, :],
                                             wg_a[:, k * 1024 + _i * 128:k * 1024 + _i * 128 + 128],
                                             xn2[:, k * T + _nt * 512:k * T + _nt * 512 + 512],
                                             start=(k == 0), stop=(k == 7))
                        return last
                    p.pe_group([pgb], [b_wg, b_xn2], emitg)

                    def emitu(te, _i=i, _nt=nt, _pt=pu_):
                        last = None
                        for k in range(8):
                            last = te.matmul(_pt[:, :],
                                             wu_a[:, k * 1024 + _i * 128:k * 1024 + _i * 128 + 128],
                                             xn2[:, k * T + _nt * 512:k * T + _nt * 512 + 512],
                                             start=(k == 0), stop=(k == 7))
                        return last
                    p.pe_group([pub], [b_wu, b_xn2], emitu)
                    p.op("act", [b_gu], [pgb], lambda e_, _pt=pg_: e_.activation(
                        gu_sb[:, 0:512], _pt[:, :], AF.Silu, scale=1.0))
                    p.op("act", [b_gu], [pub, b_gu], lambda e_, _pt=pu_: e_.activation(
                        gu_sb[:, 512:1024], _pt[:, :], AF.Copy, scale=1.0))
                    if e == 0:
                        p.op("dve", [b_hm], [b_gu], lambda e_, _i=i: e_.tensor_tensor(
                            hmid[:, _i * 512:(_i + 1) * 512],
                            gu_sb[:, 0:512], gu_sb[:, 512:1024], ALU.mult))
                    else:
                        p.op("dve", [b_gu], [b_gu], lambda e_: e_.tensor_tensor(
                            gu_sb[:, 1024:1536], gu_sb[:, 0:512], gu_sb[:, 512:1024], ALU.mult))
                        p.op("dve", [b_hm], [b_gu, b_wb], lambda e_, _i=i: e_.tensor_tensor(
                            hmid[:, _i * 512:(_i + 1) * 512],
                            gu_sb[:, 1024:1536], wb_sb[:, :], ALU.mult))
                for fo in range(8):
                    pd_, pdb = (ps["pv2"], psb["pv2"]) if fo % 2 == 0 else (ps["pv3"], psb["pv3"])

                    def emitd(te, _fo=fo, _pt=pd_):
                        last = None
                        for i in range(8):
                            last = te.matmul(_pt[:, :],
                                             wd_a[:, i * 1024 + _fo * 128:i * 1024 + _fo * 128 + 128],
                                             hmid[:, i * 512:(i + 1) * 512],
                                             start=(i == 0), stop=(i == 7))
                        return last
                    p.pe_group([pdb], [b_wd, b_hm], emitd)
                    yv = b_yev[fo % 2]
                    p.op("dve", [yv], [pdb], lambda e_, _fo=fo, _pt=pd_: e_.tensor_copy(
                        yev[:, (_fo % 2) * 512:(_fo % 2 + 1) * 512], _pt[:, :]))
                    yr = yregs[fo * 4 + nt]
                    p.dma("pool", [yr], [yv], lambda e_, _fo=fo, _nt=nt, _e=e: e_.dma_start(
                        y_dram[_fo * 128:(_fo + 1) * 128, _nt * 512:(_nt + 1) * 512],
                        yev[:, (_fo % 2) * 512:(_fo % 2 + 1) * 512],
                        accum_op=(ALU.bypass if _e == 0 else ALU.add)))

        r2regs = [Buf(f"r2reg{i}") for i in range(4)]
        for qt in range(4):
            p.dma("sp", [b_x2], list(yregs[qt::4]), lambda e, _q=qt: e.dma_start(
                x2_ar[:].rearrange("p (g t) -> p g t", g=8),
                y_dram[:, _q * 512:(_q + 1) * 512].rearrange("(g p) t -> p g t", p=128)))
            p.dma("sp", [b_r1st], [], lambda e, _q=qt: e.dma_start(
                r1_st[:].rearrange("p (g t) -> p g t", g=8),
                x2r_dram[:, _q * 512:(_q + 1) * 512].rearrange("(g p) t -> p g t", p=128)))
            for g in range(8):
                p.op("dve", [b_t], [b_x2, b_r1st], lambda e, _g=g: e.tensor_tensor(
                    t_ar[:, _g * 512:(_g + 1) * 512],
                    x2_ar[:, _g * 512:(_g + 1) * 512],
                    r1_st[:, _g * 512:(_g + 1) * 512], ALU.add))
            emit_postln_chunk(
                p, t_ap=t_ar, t_buf=b_t, col=0, scr_bf=scr_bf, scr_bf_buf=b_scrbf,
                rows=rows, rows_buf=b_rows, consts=bcols, consts_buf=b_bcols,
                ones_f32=ones_f32, ones_f32_buf=b_ones32, ones_bf=ones_col,
                ones_bf_buf=b_ones, psS=ps["u0"], psSb=psb["u0"], psQ=ps["u1"],
                psQb=psb["u1"], psA=ps["pv2"], psAb=psb["pv2"], psB=ps["pv3"],
                psBb=psb["pv3"],
                out_f32_ap=lambda g: (x2_ar[:, g * 512:(g + 1) * 512], b_x2),
                stride=512)
            out_toks.append(p.dma("sp", [r2regs[qt]], [b_x2], lambda e, _q=qt: e.dma_start(
                r2_out[:, _q * 512:(_q + 1) * 512].rearrange("(g p) t -> p g t", p=128),
                x2_ar[:].rearrange("p (g t) -> p g t", g=8))))

        b_w3 = Buf("w3a")
        p.dma("pool", [b_w3, b_wg, b_wu, b_wd], [], lambda e: e.dma_start(
            SH[:, 0:4096].rearrange("p (k m) -> p k m", k=8),
            wqk_s_in.rearrange("(k p) m -> p k m", p=128)))
        p.dma("pool", [b_w3], [b_w3], lambda e: e.dma_start(
            SH[:, 4096:8192].rearrange("p (k m) -> p k m", k=8),
            wv_s_in.rearrange("(k p) m -> p k m", p=128)))
        p.dma("pool", [b_w3], [b_w3], lambda e: e.dma_start(
            SH[:, 8192:12288].rearrange("p (k m) -> p k m", k=8),
            wv_l_in.rearrange("(k p) m -> p k m", p=128)))
        ci = 0
        pscyc = [(ps["s0"], psb["s0"]), (ps["s1"], psb["s1"])]
        for nt in range(4):
            p.dma("pool", [b_r2st], [r2regs[nt]], lambda e, _nt=nt: e.dma_start(
                r2st[:].rearrange("p (k t) -> p k t", k=8),
                r2_out[:, _nt * 512:(_nt + 1) * 512].rearrange("(k p) t -> p k t", p=128)))
            for sec, dst in ((0, qs_out), (256, ks_out)):
                for m in range(2):
                    pt, pb_ = pscyc[ci % 2]; ci += 1

                    def emits(te, _m=m, _nt=nt, _sec=sec, _pt=pt):
                        last = None
                        for k in range(8):
                            last = te.matmul(_pt[:, :],
                                             SH[:, k * 512 + _sec + _m * 128:
                                                k * 512 + _sec + _m * 128 + 128],
                                             r2st[:, k * 512:(k + 1) * 512],
                                             start=(k == 0), stop=(k == 7))
                        return last
                    p.pe_group([pb_], [b_w3, b_r2st], emits)
                    evi = ci % 4
                    p.op("act", [b_ev[evi]], [pb_], lambda e, _e=evi, _pt=pt: e.activation(
                        ev[:, _e * 512:(_e + 1) * 512], _pt[:, :], AF.Copy, scale=1.0))
                    out_toks.append(p.dma("sp", [], [b_ev[evi]],
                                          lambda e, _m=m, _nt=nt, _e=evi, _d=dst: e.dma_start(
                                              _d[_m * 128:(_m + 1) * 128, _nt * 512:(_nt + 1) * 512],
                                              ev[:, _e * 512:(_e + 1) * 512])))
            for off, dst in ((4096, vs_out), (8192, vl_out)):
                for tl in range(4):
                    pt, pb_ = pscyc[ci % 2]; ci += 1

                    def emitv(te, _tl=tl, _nt=nt, _off=off, _pt=pt):
                        last = None
                        for k in range(8):
                            last = te.matmul(_pt[:, :],
                                             r2st[:, k * 512 + _tl * 128:k * 512 + _tl * 128 + 128],
                                             SH[:, _off + k * 512:_off + (k + 1) * 512],
                                             start=(k == 0), stop=(k == 7))
                        return last
                    p.pe_group([pb_], [b_w3, b_r2st], emitv)
                    evi = ci % 4
                    p.op("act", [b_ev[evi]], [pb_], lambda e, _e=evi, _pt=pt: e.activation(
                        ev[:, _e * 512:(_e + 1) * 512], _pt[:, :], AF.Copy, scale=1.0))
                    out_toks.append(p.dma("sp", [], [b_ev[evi]],
                                          lambda e, _tl=tl, _nt=nt, _e=evi, _d=dst: e.dma_start(
                                              _d[_nt * 512 + _tl * 128:_nt * 512 + (_tl + 1) * 128, :],
                                              ev[:, _e * 512:(_e + 1) * 512])))
        b_w3b = Buf("w3b")
        p.dma("pool", [b_w3b, b_w3], [], lambda e: e.dma_start(
            SH[:, 0:16384].rearrange("p (k m) -> p k m", k=8),
            wqk_l_in.rearrange("(k p) m -> p k m", p=128)))
        for nt in range(4):
            p.dma("pool", [b_r2st], [r2regs[nt]], lambda e, _nt=nt: e.dma_start(
                r2st[:].rearrange("p (k t) -> p k t", k=8),
                r2_out[:, _nt * 512:(_nt + 1) * 512].rearrange("(k p) t -> p k t", p=128)))
            for sec, dst in ((0, ql_out), (1024, kl_out)):
                for m in range(8):
                    pt, pb_ = pscyc[ci % 2]; ci += 1

                    def emitl(te, _m=m, _nt=nt, _sec=sec, _pt=pt):
                        last = None
                        for k in range(8):
                            last = te.matmul(_pt[:, :],
                                             SH[:, k * 2048 + _sec + _m * 128:
                                                k * 2048 + _sec + _m * 128 + 128],
                                             r2st[:, k * 512:(k + 1) * 512],
                                             start=(k == 0), stop=(k == 7))
                        return last
                    p.pe_group([pb_], [b_w3b, b_r2st], emitl)
                    evi = ci % 4
                    p.op("act", [b_ev[evi]], [pb_], lambda e, _e=evi, _pt=pt: e.activation(
                        ev[:, _e * 512:(_e + 1) * 512], _pt[:, :], AF.Copy, scale=1.0))
                    out_toks.append(p.dma("sp", [], [b_ev[evi]],
                                          lambda e, _m=m, _nt=nt, _e=evi, _d=dst: e.dma_start(
                                              _d[_m * 128:(_m + 1) * 128, _nt * 512:(_nt + 1) * 512],
                                              ev[:, _e * 512:(_e + 1) * 512])))
        p.finish(out_toks)
    return nc


def host_l3b_inputs(inputs, core, res_3a, res_2b):
    o_parts, sums_in = host_route_partials(core, res_3a)
    return {
        "o_parts": o_parts, "sums_in": sums_in,
        "r1": res_2b[core]["r1_out"],
        "w_o": np.ascontiguousarray(np.asarray(inputs["L0_ffn_W_o"], np.float32)),
        "sel8": host_sel8(),
        "bcols_in": col_tiles(np.asarray(inputs["L0_ffn_w_ln2"])),
        "wg_s": np.ascontiguousarray(np.asarray(inputs["L0_ffn_Wg_s"], np.float32)),
        "wu_s": np.ascontiguousarray(np.asarray(inputs["L0_ffn_Wu_s"], np.float32)),
        "wd_s": np.ascontiguousarray(np.asarray(inputs["L0_ffn_Wd_s"], np.float32)),
        "wg_r": np.ascontiguousarray(np.asarray(inputs["L0_ffn_Wg_r"], np.float32)),
        "wu_r": np.ascontiguousarray(np.asarray(inputs["L0_ffn_Wu_r"], np.float32)),
        "wd_r": np.ascontiguousarray(np.asarray(inputs["L0_ffn_Wd_r"], np.float32)),
        "w_router": np.ascontiguousarray(np.asarray(inputs["L0_ffn_W_router"], np.float32)),
        "rb_in": np.asarray(inputs["L0_ffn_routing_bias"], np.float32).reshape(1, 7),
        "idf32": np.eye(128, dtype=np.float32),
        "esel_in": np.concatenate([np.tile((np.arange(7) == e).astype(np.float32)[:, None],
                                           (1, 128)) for e in range(7)], 1),
        "wqk_s_in": np.ascontiguousarray(np.asarray(inputs["L1_attn_Wqk_short"], np.float32)),
        "wv_s_in": np.ascontiguousarray(np.asarray(inputs["L1_attn_Wv_short"], np.float32)),
        "wqk_l_in": np.ascontiguousarray(np.asarray(inputs["L1_attn_Wqk_long"], np.float32)),
        "wv_l_in": np.ascontiguousarray(np.asarray(inputs["L1_attn_Wv_long"], np.float32)),
    }



NKB_S, NKB_L = 6, 12


def build_l4():
    nc = bass.Bass("TRN2", target_bir_lowering=False)
    dr = lambda n, s, d, k: nc.dram_tensor(n, s, d, kind=k).ap()
    qs_in = dr("qs_in", [256, T], BF16, "ExternalInput")
    ql_in = dr("ql_in", [D, T], BF16, "ExternalInput")
    ks_pad = dr("ks_pad", [256, 2 * 2048], BF16, "ExternalInput")
    vs_pad = dr("vs_pad", [2 * 2048, 512], BF16, "ExternalInput")
    kl_pad = dr("kl_pad", [D, 2 * 2048], BF16, "ExternalInput")
    vl_pad = dr("vl_pad", [2 * 2048, 512], BF16, "ExternalInput")
    masksL4 = dr("masksL4", [4 * 18 * 128, 512], F32, "ExternalInput")
    r2 = dr("r2", [D, T], F32, "ExternalInput")
    wproj = dr("wproj", [D, D], F32, "ExternalInput")
    sel2 = dr("sel2", [128, 2 * 128], F32, "ExternalInput")
    wexp = dr("wexp", [D, 4096], F32, "ExternalInput")
    mupT = dr("mupT", [7 * 32 * 128, 128], F32, "ExternalInput")
    mdnT = dr("mdnT", [7 * 32 * 128, 128], F32, "ExternalInput")
    w7up = dr("w7up", [128, 5 * 32 * 2], F32, "ExternalInput")
    w7dn = dr("w7dn", [128, 5 * 32 * 2], F32, "ExternalInput")
    bias_in = dr("bias_in", [128, 80], F32, "ExternalInput")
    wcon = dr("wcon", [4096, D], F32, "ExternalInput")
    idbf = dr("idbf", [128, 128], F32, "ExternalInput")
    out_f = dr("out_f", [D, T], F32, "ExternalOutput")
    r3_dram = nc.dram_tensor("r3_dram", [D, T], F32, kind="Internal").ap()

    out_toks = []
    with ExitStack() as ctx:
        p = Prog(nc, ctx)
        WS, b_ws = p.sb("WS", [128, 16384], BF16)
        hA_t, b_hA = p.sb("hA", [128, 32 * 512], BF16)
        hB_t, b_hB = p.sb("hB", [128, 32 * 512], BF16)
        ET, b_ET = p.sb("ET", [128, 32 * 128], BF16)
        EDT, b_EDT = p.sb("EDT", [128, 32 * 128], BF16)
        x_st, b_xst = p.sb("x_st", [128, 8 * 512], F32)
        t_ar, b_t = p.sb("t_ar", [128, 8 * 512], F32)
        u_ar, b_u = p.sb("u_ar", [128, 8 * 512], F32)
        r3bf, b_r3bf = p.sb("r3bf", [128, 8 * 512], BF16)
        scr_bf, b_scrbf = p.sb("scr_bf", [128, 4 * 512], BF16)
        rows, b_rows = p.sb("rows", [1, 4 * 512], F32)
        b_e = [Buf("e0"), Buf("e1"), Buf("e2"), Buf("e3")]
        b_o = Buf("ocarve")
        o_raw, b_oraw = p.sb("o_raw", [128, 1024], BF16)
        r_sb, b_r = p.sb("r_sb", [128, 1024], F32)
        rb_sb, b_rb = p.sb("rb_sb", [128, 1024], BF16)
        sel_sb, b_sel = p.sb("sel_sb", [128, 256], BF16)
        csh, b_csh = p.sb("csh", [128, 1], F32)
        ones_col, b_ones = p.sb("ones_col", [128, 1], BF16)
        ones_f32, b_ones32 = p.sb("ones_f32", [1, 128], F32)
        bcols, b_bcols = p.sb("bcols", [128, 100], F32)
        w7, b_w7 = p.sb("w7", [128, 2 * 320], F32)
        mtmp, b_mtmp = p.sb("mtmp", [128, 128], BF16)
        id_sb, b_id = p.sb("id_sb", [128, 128], BF16)
        ev, b_ev0 = p.sb("ev", [128, 4 * 512], BF16)
        b_ev = [b_ev0] + [Buf(f"ev{i}") for i in range(1, 4)]

        b_q, b_msk = Buf("qcarve"), Buf("mskcarve")
        b_kst = [Buf("kst0"), Buf("kst1")]
        b_vst = [Buf("vst0"), Buf("vst1")]
        q_sb = hA_t[:, 0:5120]
        msk = hA_t[:, 5120:14336]
        kst = hB_t[:, 0:4096]
        vst = hB_t[:, 4096:8192]
        o_sb = hB_t[:, 8192:12288]
        e_sb = hB_t[:, 12288:14336]

        ps, psb = {}, {}
        for n in ("pv0", "pv1", "pv2", "pv3", "s0", "s1", "u0", "u1"):
            ps[n], psb[n] = p.ps("ps_" + n, [128, 512])

        p.dma("pool", [b_sel], [], lambda e: e.dma_start(sel_sb[:, :], sel2))
        p.dma("pool", [b_id], [], lambda e: e.dma_start(id_sb[:, :], idbf))
        p.op("dve", [b_csh], [], lambda e: e.memset(csh[:, :], -C_SHIFT))
        p.op("dve", [b_ones], [], lambda e: e.memset(ones_col[:, :], 1.0))
        p.op("dve", [b_ones32], [], lambda e: e.memset(ones_f32[:, :], 1.0))
        p.dma("sp", [b_bcols], [], lambda e: e.dma_start(bcols[:, 20:100], bias_in))
        p.op("dve", [b_bcols], [b_bcols], lambda e: e.memset(bcols[:, 16:17], 1.0 / 1024.0))
        p.op("dve", [b_bcols], [b_bcols], lambda e: e.memset(bcols[:, 17:18], 1e-5))
        p.dma("pool", [b_w7], [], lambda e: e.dma_start(w7[:, 0:320], w7up))
        p.dma("pool", [b_w7], [b_w7], lambda e: e.dma_start(w7[:, 320:640], w7dn))
        b_wp = Buf("wp")
        p.dma("pool", [b_wp], [], lambda e: e.dma_start(
            WS[:, 0:8192].rearrange("p (k m) -> p k m", k=8),
            wproj.rearrange("(k p) m -> p k m", p=128)))

        r3regs = [Buf(f"r3reg{i}") for i in range(4)]
        for qt in range(4):
            p.dma("sp", [b_q], [], lambda e, _qt=qt: e.dma_start(
                q_sb[:, 0:1024].rearrange("p (c t) -> p c t", c=2),
                qs_in[:, _qt * 512:(_qt + 1) * 512].rearrange("(c p) t -> p c t", p=128)))
            p.dma("sp", [b_q], [b_q], lambda e, _qt=qt: e.dma_start(
                q_sb[:, 1024:5120].rearrange("p (c t) -> p c t", c=8),
                ql_in[:, _qt * 512:(_qt + 1) * 512].rearrange("(c p) t -> p c t", p=128)))
            p.dma("pool", [b_msk], [], lambda e, _qt=qt: e.dma_start(
                msk.rearrange("p (kb t) -> p kb t", kb=18),
                masksL4[_qt * 18 * 128:(_qt + 1) * 18 * 128, :].rearrange(
                    "(kb p) t -> p kb t", p=128)))
            for pass_i, (nkb, win, kdram, vdram, scl, opair0, ktiles) in enumerate((
                    (NKB_S, 256, ks_pad, vs_pad, 1.0 / math.sqrt(32.0), 0, 2),
                    (NKB_L, 1024, kl_pad, vl_pad, 1.0 / math.sqrt(128.0), 4, 8))):
                mb0 = 0 if pass_i == 0 else NKB_S
                for un in ("u0", "u1"):
                    p.op("dve", [psb[un]], [], lambda e, _u=ps[un]: e.memset(_u[:, :], 1.0))
                for kb in range(nkb):
                    par = kb % 2
                    fst, lst = kb == 0, kb == nkb - 1
                    kc0 = (qt // 2) * 2048 + 1024 + (qt % 2) * 512 - win + kb * 128
                    p.dma("sp", [b_kst[par]], [], lambda e, _kc0=kc0, _par=par,
                          _kd=kdram, _kt=ktiles: e.dma_start(
                        kst[:, _par * 2048:_par * 2048 + _kt * 128].rearrange(
                            "p (c k) -> p c k", c=_kt),
                        _kd[:, _kc0:_kc0 + 128].rearrange("(c p) k -> p c k", p=128)))
                    p.dma("sp", [b_vst[par]], [], lambda e, _kc0=kc0, _par=par,
                          _vd=vdram: e.dma_start(
                        vst[:, _par * 2048:_par * 2048 + 512],
                        _vd[_kc0:_kc0 + 128, :]))
                    for h in range(8):
                        sp_, spb_ = (ps["s0"], psb["s0"]) if h % 2 == 0 else (ps["s1"], psb["s1"])
                        if pass_i == 0:
                            koff = par * 2048 + (h // 4) * 128
                            kbase = (h % 4) * 32
                            p.pe_group([spb_], [b_kst[par], b_q],
                                       lambda te, _s=sp_, _ko=koff, _kb2=kbase, _h=h: te.matmul(
                                           _s[:, :], kst[_kb2:_kb2 + 32, _ko:_ko + 128],
                                           q_sb[_kb2:_kb2 + 32,
                                                (_h // 4) * 512:(_h // 4 + 1) * 512],
                                           start=True, stop=True, tile_position=(_kb2, 0)))
                        else:
                            koff = par * 2048 + h * 128
                            p.pe_group([spb_], [b_kst[par], b_q],
                                       lambda te, _s=sp_, _ko=koff, _h=h: te.matmul(
                                           _s[:, :], kst[:, _ko:_ko + 128],
                                           q_sb[:, 1024 + _h * 512:1024 + (_h + 1) * 512],
                                           start=True, stop=True))
                        slot = h % 4
                        dst = e_sb[:, slot * 512:(slot + 1) * 512]
                        p.op("act", [b_e[slot]], [spb_, b_csh],
                             lambda e, _s=sp_, _d=dst, _scl=scl: e.activation(
                                 _d, _s[:, :], AF.Exp, scale=_scl, bias=csh[:, 0:1]))
                        p.op("dve", [b_e[slot]], [b_e[slot], b_msk],
                             lambda e, _d=dst, _m=mb0 + kb: e.tensor_tensor(
                                 _d, _d, msk[:, _m * 512:(_m + 1) * 512], ALU.mult))
                        pvn = "pv" + str(h // 2)
                        p.pe_group([psb[pvn]], [b_vst[par], b_e[slot]],
                                   lambda te, _h=h, _e=slot, _f=fst, _l=lst, _par=par,
                                   _pv=ps[pvn]: te.matmul(
                                       _pv[(_h % 2) * 64:(_h % 2) * 64 + 64, :],
                                       vst[:, _par * 2048 + _h * 64:_par * 2048 + _h * 64 + 64],
                                       e_sb[:, _e * 512:(_e + 1) * 512],
                                       start=_f, stop=_l, tile_position=(0, (_h % 2) * 64)))
                        ut, ub = (ps["u0"], psb["u0"]) if h < 4 else (ps["u1"], psb["u1"])
                        strip = (h % 4) * 32
                        p.pe_group([ub], [b_e[slot], b_ones],
                                   lambda te, _st=strip, _e=slot, _f=fst, _l=lst, _u=ut: te.matmul(
                                       _u[_st:_st + 1, :], ones_col[:, :],
                                       e_sb[:, _e * 512:(_e + 1) * 512],
                                       start=_f, stop=_l, tile_position=(0, _st)))
                for ui in range(2):
                    p.op("dve", [b_r], [psb["u0" if ui == 0 else "u1"]],
                         lambda e, _u=ps["u0" if ui == 0 else "u1"], _ui=ui: e.reciprocal(
                             r_sb[:, _ui * 512:(_ui + 1) * 512], _u[:, :]))
                p.op("dve", [b_rb], [b_r], lambda e: e.tensor_copy(rb_sb[:, :], r_sb[:, :]))
                for pi in range(4):
                    rt, rtb = ps["s0"], psb["s0"]
                    p.pe_group([rtb], [b_rb, b_sel], lambda te, _s=pi % 2, _u=pi // 2: te.matmul(
                        rt[:, :], sel_sb[:, _s * 128:(_s + 1) * 128],
                        rb_sb[:, _u * 512:(_u + 1) * 512], start=True, stop=True))
                    p.op("act", [b_oraw], [psb[f"pv{pi}"]],
                         lambda e, _p=ps[f"pv{pi}"]: e.activation(
                             o_raw[:, 0:512], _p[:, :], AF.Copy, scale=1.0))
                    p.op("act", [b_oraw], [rtb], lambda e: e.activation(
                        o_raw[:, 512:1024], rt[:, :], AF.Copy, scale=1.0))
                    p.op("dve", [b_o], [b_oraw], lambda e, _pg=opair0 + pi: e.tensor_tensor(
                        o_sb[:, _pg * 512:(_pg + 1) * 512],
                        o_raw[:, 0:512], o_raw[:, 512:1024], ALU.mult))
            p.dma("sp", [b_xst], [], lambda e, _q=qt: e.dma_start(
                x_st[:].rearrange("p (g t) -> p g t", g=8),
                r2[:, _q * 512:(_q + 1) * 512].rearrange("(g p) t -> p g t", p=128)))
            for fo in range(8):
                pt, pb_ = (ps["s1"], psb["s1"]) if fo % 2 == 0 else (ps["pv0"], psb["pv0"])

                def emitc(te, _fo=fo, _pt=pt):
                    last = None
                    for k in range(8):
                        last = te.matmul(_pt[:, :],
                                         WS[:, k * 1024 + _fo * 128:k * 1024 + _fo * 128 + 128],
                                         o_sb[:, k * 512:(k + 1) * 512],
                                         start=(k == 0), stop=(k == 7))
                    return last
                p.pe_group([pb_], [b_wp, b_o], emitc)
                p.op("dve", [b_t], [pb_, b_xst], lambda e, _fo=fo, _pt=pt: e.tensor_tensor(
                    t_ar[:, _fo * 512:(_fo + 1) * 512], _pt[:, :],
                    x_st[:, _fo * 512:(_fo + 1) * 512], ALU.add))
            emit_postln_chunk(
                p, t_ap=t_ar, t_buf=b_t, col=0, scr_bf=scr_bf, scr_bf_buf=b_scrbf,
                rows=rows, rows_buf=b_rows, consts=bcols, consts_buf=b_bcols,
                ones_f32=ones_f32, ones_f32_buf=b_ones32, ones_bf=ones_col,
                ones_bf_buf=b_ones, psS=ps["u0"], psSb=psb["u0"], psQ=ps["u1"],
                psQb=psb["u1"], psA=ps["pv2"], psAb=psb["pv2"], psB=ps["pv3"],
                psBb=psb["pv3"],
                out_f32_ap=lambda g: (u_ar[:, g * 512:(g + 1) * 512], b_u),
                stride=512)
            out_toks.append(p.dma("sp", [r3regs[qt]], [b_u], lambda e, _q=qt: e.dma_start(
                r3_dram[:, _q * 512:(_q + 1) * 512].rearrange("(g p) t -> p g t", p=128),
                u_ar[:].rearrange("p (g t) -> p g t", g=8))))


        p.op("dve", [b_hA, b_hB], [b_q, b_msk, b_kst[0], b_kst[1], b_vst[0], b_vst[1],
                                    b_o, b_e[0], b_e[1], b_e[2], b_e[3]],
             lambda e: e.memset(hA_t[:, 0:8], 0.0))

        b_m = Buf("mload")
        for which, (msrc, Edst, b_E) in enumerate(((mupT, ET, b_ET), (mdnT, EDT, b_EDT))):
            exdeps = []
            for t in range(32):
                p.dma("pool", [b_m] + exdeps, [], lambda e, _t=t, _m=msrc: e.dma_start(
                    mtmp[:, 0:128], _m[_t * 128:(_t + 1) * 128, :]))
                exdeps = []
                p.op("dve", [b_E], [b_m], lambda e, _t=t, _E=Edst: e.tensor_copy(
                    _E[:, _t * 128:(_t + 1) * 128], mtmp[:, 0:128]))
                for s in range(1, 7):
                    p.dma("pool", [b_m], [b_m], lambda e, _t=t, _s=s, _m=msrc: e.dma_start(
                        mtmp[:, 0:128], _m[(_s * 32 + _t) * 128:(_s * 32 + _t + 1) * 128, :]))
                    p.pe_group([psb["s0"]], [b_m, b_E], lambda te, _t=t, _E=Edst: te.matmul(
                        ps["s0"][:, 0:128], mtmp[:, 0:128], _E[:, _t * 128:(_t + 1) * 128],
                        start=True, stop=True))
                    p.op("act", [b_E], [psb["s0"]], lambda e, _t=t, _E=Edst: e.activation(
                        _E[:, _t * 128:(_t + 1) * 128], ps["s0"][:, 0:128], AF.Copy, scale=1.0))
                p.pe_group([psb["s1"]], [b_E, b_id], lambda te, _t=t, _E=Edst: te.matmul(
                    ps["s1"][:, 0:128], _E[:, _t * 128:(_t + 1) * 128], id_sb[:, :],
                    start=True, stop=True))
                p.op("act", [b_E], [psb["s1"]], lambda e, _t=t, _E=Edst: e.activation(
                    _E[:, _t * 128:(_t + 1) * 128], ps["s1"][:, 0:128], AF.Copy, scale=1.0))

        def bfly7(src_ap, src_buf, dst_ap, dst_buf, w_off, stage_i):
            st = 2 ** stage_i
            for t in range(32):
                pt_ = t ^ st
                wc = w_off + (stage_i * 32 + t) * 2
                qd = "dve"
                sl = t % 4
                p.op(qd, [b_ev[sl]], [src_buf, b_w7], lambda e, _t=t, _wc=wc, _sl=sl: e.tensor_scalar(
                    ev[:, _sl * 512:(_sl + 1) * 512],
                    src_ap[:, _t * 512:(_t + 1) * 512], w7[:, _wc:_wc + 1], None, ALU.mult))
                p.op(qd, [dst_buf], [src_buf, b_w7, b_ev[sl]],
                     lambda e, _t=t, _pt=pt_, _wc=wc, _sl=sl: e.scalar_tensor_tensor(
                         dst_ap[:, _t * 512:(_t + 1) * 512],
                         src_ap[:, _pt * 512:(_pt + 1) * 512], w7[:, _wc + 1:_wc + 2],
                         ev[:, _sl * 512:(_sl + 1) * 512], ALU.mult, ALU.add))

        b_wld = Buf("wld")
        b_wcn = Buf("wcn")
        for nt in range(4):
            p.dma("pool", [b_r3bf], [r3regs[nt]], lambda e, _nt=nt: e.dma_start(
                r3bf[:].rearrange("p (k t) -> p k t", k=8),
                r3_dram[:, _nt * 512:(_nt + 1) * 512].rearrange("(k p) t -> p k t", p=128)))
            for mg in range(4):
                deps = [b_wp] if (nt == 0 and mg == 0) else []
                p.dma("pool", [b_wld] + deps, [], lambda e, _mg=mg: e.dma_start(
                    WS[:, 0:8192].rearrange("p (k m) -> p k m", k=8),
                    wexp[:, _mg * 1024:(_mg + 1) * 1024].rearrange("(k p) m -> p k m", p=128)))
                for mm in range(8):
                    m = mg * 8 + mm
                    pt, pb_ = (ps["s0"], psb["s0"]) if mm % 2 == 0 else (ps["s1"], psb["s1"])

                    def emite(te, _mm=mm, _pt=pt):
                        last = None
                        for k in range(8):
                            last = te.matmul(_pt[:, :],
                                             WS[:, k * 1024 + _mm * 128:k * 1024 + _mm * 128 + 128],
                                             r3bf[:, k * 512:(k + 1) * 512],
                                             start=(k == 0), stop=(k == 7))
                        return last
                    p.pe_group([pb_], [b_wld, b_r3bf], emite)
                    p.op("dve", [b_hA], [pb_, b_bcols], lambda e, _m=m, _pt=pt: e.tensor_scalar(
                        hA_t[:, _m * 512:(_m + 1) * 512], _pt[:, :],
                        bcols[:, 20 + _m:21 + _m], None, ALU.add))
            for t in range(32):
                pt, pb_ = (ps["s0"], psb["s0"]) if t % 2 == 0 else (ps["s1"], psb["s1"])
                p.pe_group([pb_], [b_ET, b_hA], lambda te, _t=t, _pt=pt: te.matmul(
                    _pt[:, :], ET[:, _t * 128:(_t + 1) * 128],
                    hA_t[:, _t * 512:(_t + 1) * 512], start=True, stop=True))
                p.op("act", [b_hA], [pb_], lambda e, _t=t, _pt=pt: e.activation(
                    hA_t[:, _t * 512:(_t + 1) * 512], _pt[:, :], AF.Copy, scale=1.0))
            bufs = [(hA_t, b_hA), (hB_t, b_hB)]
            cur = 0
            for si in range(5):
                bfly7(bufs[cur][0], bufs[cur][1], bufs[1 - cur][0], bufs[1 - cur][1], 0, si)
                cur = 1 - cur
            src_ap, src_b = bufs[cur]
            dst_ap, dst_b = bufs[1 - cur]
            for t in range(32):
                p.op("act", [dst_b], [src_b, b_bcols], lambda e, _t=t: e.activation(
                    dst_ap[:, _t * 512:(_t + 1) * 512], src_ap[:, _t * 512:(_t + 1) * 512],
                    AF.Gelu, bias=bcols[:, 52 + _t:53 + _t]))
            cur = 1 - cur
            src_ap, src_b = bufs[cur]
            for t in range(32):
                pt, pb_ = (ps["s0"], psb["s0"]) if t % 2 == 0 else (ps["s1"], psb["s1"])
                p.pe_group([pb_], [b_EDT, src_b], lambda te, _t=t, _pt=pt: te.matmul(
                    _pt[:, :], EDT[:, _t * 128:(_t + 1) * 128],
                    src_ap[:, _t * 512:(_t + 1) * 512], start=True, stop=True))
                p.op("act", [src_b], [pb_], lambda e, _t=t, _pt=pt: e.activation(
                    src_ap[:, _t * 512:(_t + 1) * 512], _pt[:, :], AF.Copy, scale=1.0))
            for si in range(5):
                bfly7(bufs[cur][0], bufs[cur][1], bufs[1 - cur][0], bufs[1 - cur][1], 320, si)
                cur = 1 - cur
            h_fin, b_hfin = bufs[cur]
            bank_names = ["pv0", "pv1", "pv2", "pv3", "s0", "s1", "u0", "u1"]
            for kg in range(4):
                p.dma("pool", [b_wcn], [], lambda e, _kg=kg: e.dma_start(
                    WS[:, 8192:16384].rearrange("p (k m) -> p k m", k=8),
                    wcon[_kg * 1024:(_kg + 1) * 1024, :].rearrange("(k p) m -> p k m", p=128)))
                for fo in range(8):
                    def emitw(te, _kg=kg, _fo=fo):
                        last = None
                        for k in range(8):
                            last = te.matmul(ps[bank_names[_fo]][:, :],
                                             WS[:, 8192 + k * 1024 + _fo * 128:
                                                8192 + k * 1024 + _fo * 128 + 128],
                                             h_fin[:, (_kg * 8 + k) * 512:(_kg * 8 + k + 1) * 512],
                                             start=(_kg == 0 and k == 0),
                                             stop=(_kg == 3 and k == 7))
                        return last
                    p.pe_group([psb[bank_names[fo]]], [b_wcn, b_hfin], emitw)
            p.dma("sp", [b_xst], [r3regs[nt]], lambda e, _q=nt: e.dma_start(
                x_st[:].rearrange("p (g t) -> p g t", g=8),
                r3_dram[:, _q * 512:(_q + 1) * 512].rearrange("(g p) t -> p g t", p=128)))
            for fo in range(8):
                p.op("dve", [b_t], [psb[bank_names[fo]], b_bcols], lambda e, _fo=fo: e.tensor_scalar(
                    t_ar[:, _fo * 512:(_fo + 1) * 512], ps[bank_names[_fo]][:, :],
                    bcols[:, 84 + _fo:85 + _fo], bcols[:, 92 + _fo:93 + _fo],
                    ALU.add, ALU.add))
                p.op("dve", [b_u], [b_t, b_xst], lambda e, _fo=fo: e.tensor_tensor(
                    u_ar[:, _fo * 512:(_fo + 1) * 512],
                    t_ar[:, _fo * 512:(_fo + 1) * 512],
                    x_st[:, _fo * 512:(_fo + 1) * 512], ALU.add))
            emit_postln_chunk(
                p, t_ap=u_ar, t_buf=b_u, col=0, scr_bf=scr_bf, scr_bf_buf=b_scrbf,
                rows=rows, rows_buf=b_rows, consts=bcols, consts_buf=b_bcols,
                ones_f32=ones_f32, ones_f32_buf=b_ones32, ones_bf=ones_col,
                ones_bf_buf=b_ones, psS=ps["pv2"], psSb=psb["pv2"], psQ=ps["pv3"],
                psQb=psb["pv3"], psA=ps["s0"], psAb=psb["s0"], psB=ps["s1"],
                psBb=psb["s1"],
                out_f32_ap=lambda g: (t_ar[:, g * 512:(g + 1) * 512], b_t),
                stride=512)
            out_toks.append(p.dma("sp", [], [b_t], lambda e, _q=nt: e.dma_start(
                out_f[:, _q * 512:(_q + 1) * 512].rearrange("(g p) t -> p g t", p=128),
                t_ar[:].rearrange("p (g t) -> p g t", g=8))))
        p.finish(out_toks)
    return nc


def host_l4_masks(core):
    _, lo, hi = core_chunks(core)
    qts = [2 * lo, 2 * lo + 1, 2 * hi, 2 * hi + 1]
    i = np.arange(128)[:, None]
    j = np.arange(512)[None, :]
    out = np.zeros((4, 18, 128, 512), np.float32)
    for qi, qt in enumerate(qts):
        q0 = qt * 512
        for pi, (win, nkb, kb0i) in enumerate(((256, NKB_S, 0), (1024, NKB_L, NKB_S))):
            for kb in range(nkb):
                k0 = q0 - win + kb * 128
                qpos = q0 + j
                kpos = k0 + i
                valid = (kpos >= 0) & (qpos - kpos >= 0) & (qpos - kpos < win)
                out[qi, kb0i + kb] = valid.astype(np.float32)
    return out.reshape(4 * 18 * 128, 512)


def host_sel2():
    sel = np.zeros((128, 2 * 128), np.float32)
    sel[0, 0:64] = 1.0
    sel[32, 64:128] = 1.0
    sel[64, 128:192] = 1.0
    sel[96, 192:256] = 1.0
    return sel


def host_butterfly_mats(w):
    w = np.asarray(w, np.float32)
    mT = np.zeros((7, 32, 128, 128), np.float32)
    for s in range(7):
        sz = 2 ** s
        M = np.zeros((4096, 4096), np.float32)
        for t in range(32):
            Mt = np.zeros((128, 128), np.float32)
            for p in range(128):
                f = t * 128 + p
                kpair = (f // (2 * sz)) * sz + (f % sz)
                if (f // sz) % 2 == 0:
                    a, b2 = p, p + sz
                    Mt[p, a] = w[s, kpair, 0, 0]
                    Mt[p, b2] = w[s, kpair, 0, 1]
                else:
                    a, b2 = p - sz, p
                    Mt[p, a] = w[s, kpair, 1, 0]
                    Mt[p, b2] = w[s, kpair, 1, 1]
            mT[s, t] = Mt if s == 0 else Mt.T
    w7 = np.zeros((5, 32, 2, 128), np.float32)
    for si in range(5):
        s = si + 7
        sz = 2 ** s
        st = 2 ** si
        for t in range(32):
            for p in range(128):
                f = t * 128 + p
                kpair = (f // (2 * sz)) * sz + (f % sz)
                if (f // sz) % 2 == 0:
                    w7[si, t, 0, p] = w[s, kpair, 0, 0]
                    w7[si, t, 1, p] = w[s, kpair, 0, 1]
                else:
                    w7[si, t, 0, p] = w[s, kpair, 1, 1]
                    w7[si, t, 1, p] = w[s, kpair, 1, 0]
    return (np.ascontiguousarray(mT.reshape(7 * 32 * 128, 128)),
            np.ascontiguousarray(w7.transpose(3, 0, 1, 2).reshape(128, 5 * 32 * 2)))


def host_l4_inputs(inputs, core, res_3b):
    b = core // 4
    cores = list(range(b * 4, b * 4 + 4))
    ks = assemble_full_fm(res_3b, "ks_out", cores)
    kl = assemble_full_fm(res_3b, "kl_out", cores)
    vs = assemble_full_tm(res_3b, "vs_out", cores)
    vl = assemble_full_tm(res_3b, "vl_out", cores)
    _, lo, hi = core_chunks(core)

    def pad(a):
        out = np.zeros((a.shape[0], 2 * 2048), a.dtype)
        for c2, ch in enumerate((lo, hi)):
            g0 = ch * 1024 - 1024
            src0 = max(g0, 0)
            out[:, c2 * 2048 + (src0 - g0):c2 * 2048 + 2048] = a[:, src0:ch * 1024 + 1024]
        return out

    def padr(a):
        out = np.zeros((2 * 2048, a.shape[1]), a.dtype)
        for c2, ch in enumerate((lo, hi)):
            g0 = ch * 1024 - 1024
            src0 = max(g0, 0)
            out[c2 * 2048 + (src0 - g0):c2 * 2048 + 2048, :] = a[src0:ch * 1024 + 1024, :]
        return out
    mupT, w7up = host_butterfly_mats(inputs["L1_ffn_up_weights"])
    mdnT, w7dn = host_butterfly_mats(inputs["L1_ffn_down_weights"])
    bias = np.concatenate([
        col_tiles(np.asarray(inputs["L1_ffn_b_exp"])),
        col_tiles(np.asarray(inputs["L1_ffn_up_bias"])),
        col_tiles(np.asarray(inputs["L1_ffn_b_con"])),
        col_tiles(np.asarray(inputs["L1_ffn_down_bias"]))], 1)
    return {
        "qs_in": res_3b[core]["qs_out"], "ql_in": res_3b[core]["ql_out"],
        "ks_pad": pad(ks), "vs_pad": padr(vs), "kl_pad": pad(kl), "vl_pad": padr(vl),
        "masksL4": host_l4_masks(core),
        "r2": res_3b[core]["r2_out"],
        "wproj": np.ascontiguousarray(np.asarray(inputs["L1_attn_Wproj"], np.float32)),
        "sel2": host_sel2(),
        "wexp": np.ascontiguousarray(np.asarray(inputs["L1_ffn_W_exp"], np.float32)),
        "mupT": mupT, "mdnT": mdnT, "w7up": w7up, "w7dn": w7dn,
        "bias_in": bias,
        "wcon": np.ascontiguousarray(np.asarray(inputs["L1_ffn_W_con"], np.float32)),
        "idbf": np.eye(128, dtype=np.float32),
    }



_NC_CACHE = {}


def _get_nc(name, builder):
    if name not in _NC_CACHE:
        _NC_CACHE[name] = builder()
    return _NC_CACHE[name]


def kernel(**inputs):
    cores = list(range(NCORE))
    nc1 = _get_nc("l1", build_l0a_qkv)
    res1 = run_bass_kernel_spmd(
        nc1, [host_l0a_qkv_inputs(inputs, c) for c in cores], core_ids=cores).results

    nc_ap = _get_nc("ap", build_attn_partial)
    full = {}
    for b in range(2):
        cs = cores[b * 4:b * 4 + 4]
        full[b] = dict(q=assemble_full_fm(res1, "q_out", cs),
                       k=assemble_full_fm(res1, "k_out", cs),
                       v=assemble_full_tm(res1, "v_out", cs))
    res2a = run_bass_kernel_spmd(nc_ap, [host_attn_partial_inputs(
        c, full[c // 4]["q"], full[c // 4]["k"], full[c // 4]["v"]) for c in cores],
        core_ids=cores).results

    nc2b = _get_nc("l2b", build_l2b)
    res2b = run_bass_kernel_spmd(
        nc2b, [host_l2b_inputs(inputs, c, res2a) for c in cores], core_ids=cores).results

    mla = {}
    for b in range(2):
        cs = cores[b * 4:b * 4 + 4]
        qq = assemble_full_fm(res2b, "qq_out", cs)
        kg = assemble_full_fm(res2b, "k_out2", cs)
        krg = assemble_full_fm(res2b, "kr_out", cs)
        kk = np.zeros((1024, S), kg.dtype)
        for h in range(16):
            kk[64 * h:64 * h + 32] = kg[32 * h:32 * h + 32]
            kk[64 * h + 32:64 * h + 64] = krg[32 * h:32 * h + 32]
        mla[b] = dict(q=qq, k=kk, v=assemble_full_tm(res2b, "vm_out", cs))
    res3a = run_bass_kernel_spmd(nc_ap, [host_attn_partial_inputs(
        c, mla[c // 4]["q"], mla[c // 4]["k"], mla[c // 4]["v"]) for c in cores],
        core_ids=cores).results

    nc3b = _get_nc("l3b", build_l3b)
    res3b = run_bass_kernel_spmd(
        nc3b, [host_l3b_inputs(inputs, c, res3a, res2b) for c in cores],
        core_ids=cores).results

    nc4 = _get_nc("l4", build_l4)
    res4 = run_bass_kernel_spmd(
        nc4, [host_l4_inputs(inputs, c, res3b) for c in cores], core_ids=cores).results

    out = np.zeros((B, S, D), np.float32)
    for c in cores:
        b, lo, hi = core_chunks(c)
        of = np.asarray(res4[c]["out_f"])
        out[b, lo * CH:(lo + 1) * CH] = of[:, 0:CH].T
        out[b, hi * CH:(hi + 1) * CH] = of[:, CH:2 * CH].T
    return out
